# version 23
# baseline (speedup 1.0000x reference)
import numpy as np
import concourse.bass as bass
import concourse.mybir as mybir
from concourse.bass_utils import run_bass_kernel_spmd

F32 = mybir.dt.float32
BF16 = mybir.dt.bfloat16
OP = mybir.AluOpType
AF = mybir.ActivationFunctionType
AX = mybir.AxisListType.X

D = 1024
DT = 8
DH = 2048
HT = 16
NE = 8
TOK = 4096
HTOK = 4097
TC = 373
QW = 1024
BIGM = 30000.0
EPS = 1e-5

ENGS = ("pe", "act", "dve", "pool", "sp")
BLK_ATTR = {"pe": "tensor", "act": "scalar", "dve": "vector", "pool": "gpsimd", "sp": "sync"}
GRAN = 256
CROT = 4
SB_BASE = 16640
SB_LIMIT = 228352

R1 = SB_BASE + 26624
R2 = R1 + 65536
R3 = R2 + 65536
R3SZ = SB_LIMIT - R3
WO_OFF = R3 + 36864


def chunk_grid(total):
    out, t = [], 0
    while t < total:
        out.append((t, min(TC, total - t)))
        t += TC
    return out


class Rec:
    __slots__ = ("eng", "fn", "kind", "chan", "cidx", "ms", "deps", "waits", "gi", "snap", "tiny")

    def __init__(self, eng, fn, kind, chan, cidx, gi):
        self.eng, self.fn, self.kind, self.chan, self.cidx, self.gi = eng, fn, kind, chan, cidx, gi
        self.ms = 0
        self.deps = None
        self.waits = None


class ST:

    def __init__(self, t, off, nbytes):
        self.t = t
        self.off0 = off
        g0, g1 = off // GRAN, (off + nbytes + GRAN - 1) // GRAN
        self.keys = tuple(("s", g) for g in range(g0, g1))

    def __getitem__(self, idx):
        return self.t[idx]

    @property
    def ap(self):
        return self.t[:]


class RegionAlloc:
    def __init__(self, b, base, size, tag):
        self.b, self.base, self.end, self.off, self.tag = b, base, base + size, base, tag

    def t(self, name, shape, dtype):
        esz = 2 if dtype == BF16 else 4
        nb = int(np.prod(shape[1:])) * esz
        off = (self.off + 31) & ~31
        assert off + nb <= self.end, f"{self.tag}:{name} overflow {off}+{nb}>{self.end}"
        self.off = off + nb
        return self.b.sb(f"{self.tag}_{name}", shape, dtype, off=off)


class B:
    def __init__(self, nc):
        self.nc = nc
        self.order = []
        self.per_eng = {e: [] for e in ENGS}
        self.chans = {}
        self.writers = {}
        self.readers = {}
        self.n_coll = 0
        self._names = set()

    def sb(self, name, shape, dtype, off):
        assert name not in self._names, name
        self._names.add(name)
        esz = 2 if dtype == BF16 else 4
        nb = int(np.prod(shape[1:])) * esz
        assert off + nb <= SB_LIMIT, f"sbuf overflow {name}: {off}+{nb}"
        t = self.nc.alloc_sbuf_tensor_at(name, list(shape), dtype, offset=off)
        return ST(t, off, nb)

    def chan(self, name, eng):
        self.chans[name] = {"eng": eng, "n": 0}

    @staticmethod
    def _flat(keys):
        out = []
        for k in keys:
            if isinstance(k, tuple) and k and isinstance(k[0], tuple):
                out.extend(k)
            else:
                out.append(k)
        return out

    def emit(self, eng, fn, r=(), w=(), chan=None, coll=False, tiny=False):
        kind = "coll" if coll else ("dma" if chan else "eng")
        cidx = None
        if chan:
            cidx = self.chans[chan]["n"]
            self.chans[chan]["n"] += 1
        if coll:
            cidx = self.n_coll
            self.n_coll += 1
        rec = Rec(eng, fn, kind, chan, cidx, len(self.order))
        rec.tiny = tiny
        rec.snap = ({e: sum(1 for x in self.per_eng[e] if x.kind != "dma") for e in ENGS},
                    {c: m["n"] for c, m in self.chans.items()}, self.n_coll)
        rset, wset = self._flat(r), self._flat(w)
        deps = set()
        for k in rset:
            p = self.writers.get(k)
            if p is not None:
                deps.add(p)
        for k in wset:
            p = self.writers.get(k)
            if p is not None:
                deps.add(p)
            for q in self.readers.get(k, ()):
                deps.add(q)
        for k in rset:
            self.readers.setdefault(k, []).append(rec)
        for k in wset:
            self.writers[k] = rec
            self.readers[k] = []
        deps.discard(rec)
        rec.deps = deps
        self.order.append(rec)
        self.per_eng[eng].append(rec)
        return rec

    def finalize(self, serial=False):
        if serial:
            ms_no = {e: 0 for e in ENGS}
            eng_ms = {}
            for e in ENGS:
                for rec in self.per_eng[e]:
                    if rec.kind != "dma" and rec.kind != "coll":
                        ms_no[e] += 1
                        rec.ms = ms_no[e]
            cnt = {e: 0 for e in ENGS}
            for rec in self.order:
                waits = {}
                for e2 in ENGS:
                    if e2 != rec.eng and cnt[e2] > 0:
                        waits[("e", e2)] = cnt[e2]
                for c2, n2 in rec.snap[1].items():
                    for r_ in range(CROT):
                        nr = (n2 + CROT - 1 - r_) // CROT
                        if nr > 0:
                            waits[("c", c2, r_)] = 16 * nr
                if rec.snap[2] > 0:
                    waits[("x",)] = rec.snap[2]
                rec.waits = waits
                if rec.kind == "eng" and rec.ms > 0:
                    cnt[rec.eng] = rec.ms
            return ms_no
        for rec in self.order:
            for d in rec.deps:
                if d.kind == "eng" and (d.eng != rec.eng or d.tiny or rec.kind != "eng"):
                    d.ms = -1
        ms_no = {e: 0 for e in ENGS}
        for e in ENGS:
            for rec in self.per_eng[e]:
                if rec.kind == "eng" and rec.ms == -1:
                    ms_no[e] += 1
                    rec.ms = ms_no[e]
        for rec in self.order:
            waits = {}
            for d in rec.deps:
                if d.kind == "eng":
                    if d.eng == rec.eng and rec.kind == "eng" and not d.tiny:
                        continue
                    k = ("e", d.eng)
                    waits[k] = max(waits.get(k, 0), d.ms)
                elif d.kind == "dma":
                    k = ("c", d.chan, d.cidx % CROT)
                    waits[k] = max(waits.get(k, 0), 16 * (d.cidx // CROT + 1))
                else:
                    k = ("x",)
                    waits[k] = max(waits.get(k, 0), d.cidx + 1)
            rec.waits = waits
        return ms_no

    def build_blocks(self, serial=False):
        ms_no = self.finalize(serial=serial)
        nc = self.nc
        esem = {e: nc.alloc_semaphore(f"sem_{e}") for e in ENGS}
        csem = {c: [nc.alloc_semaphore(f"sem_{c}{r}") for r in range(CROT)]
                for c in self.chans}
        xsem = nc.alloc_semaphore("sem_coll")

        def semof(key):
            if key[0] == "e":
                return esem[key[1]]
            if key[0] == "c":
                return csem[key[1]][key[2]]
            return xsem

        with nc.Block() as blk:
            def make(engname, rlist):
                def body(eng):
                    cur = {}
                    for rec in rlist:
                        for key, val in sorted(rec.waits.items()):
                            if cur.get(key, 0) < val:
                                eng.wait_ge(semof(key), val)
                                cur[key] = val
                        inst = rec.fn(eng)
                        if rec.kind == "dma":
                            inst.then_inc(csem[rec.chan][rec.cidx % CROT], 16)
                        elif rec.kind == "coll":
                            inst.then_inc(xsem, 1)
                        elif rec.ms > 0:
                            inst.then_inc(esem[rec.eng], 1)
                    if engname == "sp":
                        for c, meta in self.chans.items():
                            for r in range(CROT):
                                nr = (meta["n"] + CROT - 1 - r) // CROT
                                if nr:
                                    eng.wait_ge(csem[c][r], 16 * nr)
                        if self.n_coll:
                            eng.wait_ge(xsem, self.n_coll)
                        for e2 in ENGS:
                            if e2 != "sp" and ms_no[e2]:
                                eng.wait_ge(esem[e2], ms_no[e2])
                return body

            for e in ENGS:
                getattr(blk, BLK_ATTR[e])(make(e, self.per_eng[e]))



DBG_HSH = [False]
SERIAL_PA1 = [False]
DBG_PD = [False]
FORCE_TINY = [False]


def build_program(debug=(), phases=None, serial=False):
    DBG_HSH[0] = "hsh" in debug
    DBG_PD[0] = "pd" in debug
    nc = bass.Bass("TRN2", target_bir_lowering=False)
    b = B(nc)
    b.chan("cin", "sp")
    b.chan("cout", "act")
    b.chan("cw", "pool")
    b.chan("crow", "sp")
    b.chan("csc", "act")
    b.chan("cscw", "act")
    b.chan("cxb", "sp")
    b.chan("cxl", "sp")
    b.chan("cpk", "pool")
    b.chan("cpv", "pool")
    b.chan("cpr", "pool")

    def din(name, shape, dtype=F32):
        return nc.dram_tensor(name, list(shape), dtype, kind="ExternalInput").ap()

    NCH = len(chunk_grid(HTOK))
    x_in = din("x", [NCH * 128, DT * TC])
    w_in_T = din("w_in_T", [D, 2 * DH])
    w1_T = din("w1_T", [D, 2 * DH])
    w_out_T = din("w_out_T", [DH, D])
    w2_T = din("w2_T", [NE * DH, D])
    wk_T = din("wk_T", [D, D])
    wv_T = din("wv_T", [D, D])
    wr_T = din("wr_T", [D, D])
    wo_T = din("wo_T", [D, D])
    wge_T = din("wge_T", [D, 10])
    gb_in = din("gbias", [10, 1])
    tpar_in = din("tparams", [128, 6 * DT])

    out_d = nc.dram_tensor("out", [D, TOK], F32, kind="ExternalOutput").ap()

    def idram(name, shape, dtype=F32):
        return nc.dram_tensor(name, list(shape), dtype).ap()

    xln_d = idram("xln_d", [NCH * 128, DT * TC], BF16)
    hrt_d = idram("hrt_d", [NCH * 128, HT * TC], BF16)
    pA_d = idram("pA_d", [NCH * 128, DT * TC])
    pB_d = idram("pB_d", [NCH * 128, DT * TC])
    xr2_d = idram("xr2_d", [NCH * 128, DT * TC])
    wse_d = idram("wse_d", [NE, HTOK])
    wsehl_d = idram("wsehl_d", [4 * NCH, 4 * TC], BF16)
    wsehl_v = wsehl_d.rearrange("(p c) (e t) -> p c e t", c=NCH, e=4)
    xr1_d = idram("xr1_d", [D, HTOK])
    k_d = idram("k_d", [D, TOK])
    v_d = idram("v_d", [D, TOK])
    rs_d = idram("rs_d", [D, TOK])
    as_d = idram("as_d", [D, TOK])
    bs_d = idram("bs_d", [D, TOK])
    hsh_d = idram("hsh_d", [DH, HTOK], BF16)
    dcb_d = idram("dcb_d", [D, QW])
    pt_d = idram("pt_d", [D, QW])
    ag_in_d = idram("ag_in", [128, 16])
    ag_out_d = idram("ag_out", [8 * 128, 16])

    def jv(ap):
        return ap.rearrange("(j p) t -> p j t", p=128)

    def blk(ap, c):
        return ap[c * 128:(c + 1) * 128, :]

    def blkm(ap, c, m, w):
        return ap[c * 128:(c + 1) * 128, m * TC:m * TC + w]

    xr1vw = jv(xr1_d)
    outvw = jv(out_d)
    kvw = jv(k_d)
    vvw = jv(v_d)
    rsvw = jv(rs_d)

    PS = [nc.alloc_psum_tensor(f"psb{i}", [128, 512], F32) for i in range(8)]

    def pkk(i):
        return ("p", i)

    CA = RegionAlloc(b, SB_BASE, R1 - SB_BASE, "c")
    idn = CA.t("idn", [128, 128], F32)
    ones_k = CA.t("ones_k", [128, 1], BF16)
    ones_r = CA.t("ones_r", [1, 128], BF16)
    tpar = CA.t("tpar", [128, 6 * DT], F32)
    tmix = b.sb("tmix", [128, 3 * DT], F32, off=tpar.off0)
    tdec = b.sb("tdec", [128, DT], F32, off=tpar.off0 + 96)
    tfir = b.sb("tfir", [128, DT], F32, off=tpar.off0 + 128)
    selv = b.sb("selv", [128, DT], F32, off=tpar.off0 + 160)
    etd = CA.t("etd", [128, DT], F32)
    netd = CA.t("netd", [128, DT], F32)
    decC = CA.t("dec", [128, DT], F32)
    wge_f = CA.t("wge_f", [128, DT, 10], F32)
    wge_hi = CA.t("wge_hi", [128, DT, 10], BF16)
    wge_lo = CA.t("wge_lo", [128, DT, 10], BF16)
    gbias = CA.t("gbias", [10, 1], F32)
    SAB1 = CA.t("SAB1", [128, 16], F32)
    SAB2 = CA.t("SAB2", [128, 16], F32)
    stage = CA.t("stage", [128, 16], F32)
    gsb = CA.t("gsb", [128, DT, 16], F32)
    carry_a = CA.t("carry_a", [128, DT], F32)
    carry_b = CA.t("carry_b", [128, DT], F32)
    pcol = CA.t("pcol", [128, DT], F32)
    rowA = CA.t("rowA", [1, 512], F32)
    rowB = CA.t("rowB", [1, 512], F32)
    rowC = CA.t("rowC", [1, 512], F32)
    rowD = CA.t("rowD", [1, 512], F32)
    rowE = CA.t("rowE", [1, 512], F32)
    rhiA = CA.t("rhiA", [1, 512], BF16)
    rloA = CA.t("rloA", [1, 512], BF16)
    rhiB = CA.t("rhiB", [1, 512], BF16)
    rloB = CA.t("rloB", [1, 512], BF16)
    glog = CA.t("glog", [10, 512], F32)
    wse8 = CA.t("wse8", [8, 512], F32)
    ones8 = CA.t("ones8", [128, 8], F32)
    wse8h = CA.t("wse8h", [8, 512], BF16)
    wse8l = CA.t("wse8l", [8, 512], BF16)
    assert CA.off <= R1, CA.off
    pbs0 = rowA.off0
    pb_pin = [b.sb(f"pbs_pin{i}", [128, TC], F32, off=pbs0 + i * 1536) for i in range(3)]
    pb_wb = [[b.sb(f"pbs_wb{p}{e}", [128, TC], F32, off=pbs0 + 4608 + (2 * p + e) * 1536)
              for e in range(2)] for p in range(2)]
    pb_rows4 = b.sb("pbs_rows4", [1, 4, TC], BF16, off=pbs0 + 10752)
    assert pbs0 + 10752 + 4 * TC * 2 <= rowA.off0 + 16384
    tc1 = b.sb("tc1", [128, 128], F32, off=R3)
    tc2 = b.sb("tc2", [128, 128], F32, off=R3 + 512)

    def wslot(call):
        return (R1, R2) if call == 0 else (R2, R1)

    w1_sb, w_in_sb, w_out_sb, w2_sb = {}, {}, {}, {}
    for call in (0, 1):
        sA, sB = wslot(call)
        w1_sb[call] = b.sb(f"w1s{call}", [128, DT, 2 * DH], BF16, off=sA)
        w_in_sb[call] = b.sb(f"wins{call}", [128, DT, 2 * DH], BF16, off=sB)
        w_out_sb[call] = b.sb(f"wouts{call}", [128, HT, D], BF16, off=sA)
        for q in range(4):
            off = sB if q % 2 == 0 else sA
            w2_sb[(call, q)] = b.sb(f"w2s{call}_{q}", [128, 2, HT, D], BF16, off=off)
    wk_sb = b.sb("wk_sb", [128, DT, D], BF16, off=R2)
    wv_sb = b.sb("wv_sb", [128, DT, D], BF16, off=R2 + 16384)
    wr_sb = b.sb("wr_sb", [128, DT, D], BF16, off=R2 + 32768)
    rwb = b.sb("rwb", [128, DT, TOK], BF16, off=R1)
    wo_sb = b.sb("wo_sb", [128, DT, D], BF16, off=WO_OFF)

    def dma(chan, dst, src, r=(), w=(), slow=False):
        eng = b.chans[chan]["eng"]

        def fn(e):
            if slow:
                return e.dma_start(dst, src, allow_slow_non_contiguous=True)
            return e.dma_start(dst, src)
        return b.emit(eng, fn, r=r, w=w, chan=chan)

    def op(eng, fn, r=(), w=(), tiny=False):
        return b.emit(eng, fn, r=r, w=w, tiny=tiny or FORCE_TINY[0])

    def mm(bank, out_ap, passes, reads, start=True, stop=True):
        def fn(eng):
            n = len(passes)
            inst = None
            for i, (l, rr) in enumerate(passes):
                inst = eng.matmul(out_ap, l, rr, start=(start and i == 0),
                                  stop=(stop and i == n - 1))
            return inst
        b.emit("pe", fn, r=reads, w=(pkk(bank),))

    def emit_ln(srcT, w, xbf, xc, tmpT, dstT,
                bf_out=None, lo=None, rows2=None):
        src3 = srcT.t[:, :, 0:w]
        src_keys = srcT.keys
        tmp3 = tmpT.t[:, :, 0:w]
        tmp_keys = tmpT.keys
        dst3 = dstT.t[:, :, 0:w]
        dst_keys = dstT.keys
        src2 = srcT.t[:, :, :].rearrange("p j t -> p (j t)")
        op("act", lambda e: e.copy(xbf.ap, src2), r=(src_keys,), w=(xbf.keys,))
        op("act", lambda e: e.activation(xc.ap, src2, AF.Square),
           r=(src_keys,), w=(xc.keys,))
        xbf3 = xbf.t[:, :].rearrange("p (j t) -> p j t", j=DT)
        xc3 = xc.t[:, :].rearrange("p (j t) -> p j t", j=DT)
        xbf3 = xbf3[:, :, 0:w]
        xc3 = xc3[:, :, 0:w]
        mm(4, PS[4][0:1, 0:w], [(ones_k.ap, xbf3[:, j, :]) for j in range(DT)], (xbf.keys, ones_k.keys))
        mm(5, PS[5][0:1, 0:w], [(ones_k.ap, xc3[:, j, :]) for j in range(DT)], (xc.keys, ones_k.keys))
        op("dve", lambda e: e.tensor_scalar(rowA[0:1, 0:w], PS[4][0:1, 0:w], 1.0 / D, None, OP.mult),
           r=(pkk(4),), w=(rowA.keys,), tiny=True)
        op("dve", lambda e: e.tensor_scalar(rowB[0:1, 0:w], PS[5][0:1, 0:w], 1.0 / D, EPS, OP.mult, OP.add),
           r=(pkk(5),), w=(rowB.keys,), tiny=True)
        op("dve", lambda e: e.tensor_tensor(rowC[0:1, 0:w], rowA[0:1, 0:w], rowA[0:1, 0:w], OP.mult),
           r=(rowA.keys,), w=(rowC.keys,), tiny=True)
        op("dve", lambda e: e.tensor_tensor(rowD[0:1, 0:w], rowB[0:1, 0:w], rowC[0:1, 0:w], OP.subtract),
           r=(rowB.keys, rowC.keys), w=(rowD.keys,), tiny=True)
        op("act", lambda e: e.activation(rowC[0:1, 0:w], rowD[0:1, 0:w], AF.Sqrt),
           r=(rowD.keys,), w=(rowC.keys,), tiny=True)
        op("dve", lambda e: e.reciprocal(rowE[0:1, 0:w], rowC[0:1, 0:w]),
           r=(rowC.keys,), w=(rowE.keys,), tiny=True)
        op("dve", lambda e: e.tensor_tensor(rowD[0:1, 0:w], rowA[0:1, 0:w], rowE[0:1, 0:w], OP.mult),
           r=(rowA.keys, rowE.keys), w=(rowD.keys,), tiny=True)
        op("act", lambda e: e.copy(rhiA[0:1, 0:w], rowE[0:1, 0:w]), r=(rowE.keys,), w=(rhiA.keys,), tiny=True)
        op("dve", lambda e: e.tensor_tensor(rloA[0:1, 0:w], rowE[0:1, 0:w], rhiA[0:1, 0:w], OP.subtract),
           r=(rowE.keys, rhiA.keys), w=(rloA.keys,), tiny=True)
        op("act", lambda e: e.copy(rhiB[0:1, 0:w], rowD[0:1, 0:w]), r=(rowD.keys,), w=(rhiB.keys,), tiny=True)
        op("dve", lambda e: e.tensor_tensor(rloB[0:1, 0:w], rowD[0:1, 0:w], rhiB[0:1, 0:w], OP.subtract),
           r=(rowD.keys, rhiB.keys), w=(rloB.keys,), tiny=True)
        mm(6, PS[6][:, 0:w], [(ones_r.ap, rhiA[0:1, 0:w]), (ones_r.ap, rloA[0:1, 0:w])],
           (rhiA.keys, rloA.keys, ones_r.keys))
        mm(7, PS[7][:, 0:w], [(ones_r.ap, rhiB[0:1, 0:w]), (ones_r.ap, rloB[0:1, 0:w])],
           (rhiB.keys, rloB.keys, ones_r.keys))
        if rows2 is not None:
            rstd_t, mr_t = rows2
            op("act", lambda e: e.copy(rstd_t.t[:, 0:w], PS[6][:, 0:w]),
               r=(pkk(6),), w=(rstd_t.keys,))
            op("act", lambda e: e.copy(mr_t.t[:, 0:w], PS[7][:, 0:w]),
               r=(pkk(7),), w=(mr_t.keys,))
            for j in range(DT):
                op("pool", lambda e, j=j: e.tensor_tensor(tmp3[:, j, :], src3[:, j, :], rstd_t.t[:, 0:w], OP.mult),
                   r=(src_keys, rstd_t.keys), w=(tmp_keys,))
            for j in range(DT):
                op("dve", lambda e, j=j: e.tensor_tensor(dst3[:, j, :], tmp3[:, j, :], mr_t.t[:, 0:w], OP.subtract),
                   r=(tmp_keys, mr_t.keys), w=(dst_keys,))
        else:
            for j in range(DT):
                op("dve", lambda e, j=j: e.tensor_tensor(tmp3[:, j, :], src3[:, j, :], PS[6][:, 0:w], OP.mult),
                   r=(src_keys, pkk(6)), w=(tmp_keys,))
            for j in range(DT):
                op("dve", lambda e, j=j: e.tensor_tensor(dst3[:, j, :], tmp3[:, j, :], PS[7][:, 0:w], OP.subtract),
                   r=(tmp_keys, pkk(7)), w=(dst_keys,))
        if bf_out is not None:
            dst2 = dstT.t[:, :, :].rearrange("p j t -> p (j t)")
            op("act", lambda e: e.copy(bf_out.ap, dst2), r=(dst_keys,), w=(bf_out.keys,))
        if lo is not None:
            bf3 = bf_out.t[:, :].rearrange("p (j t) -> p j t", j=DT)
            lo3 = lo.t[:, :].rearrange("p (j t) -> p j t", j=DT)
            for j in range(DT):
                op("dve", lambda e, j=j: e.tensor_tensor(lo3[:, j, 0:w], dst3[:, j, :], bf3[:, j, 0:w], OP.subtract),
                   r=(dst_keys, bf_out.keys), w=(lo.keys,))

    def emit_gating(t0, w, xbf3, xbf_keys, xlo3, xlo_keys, gsc):
        wgehi3 = wge_hi.t[:, :, :]
        wgelo3 = wge_lo.t[:, :, :]
        passes = [(wgehi3[:, j, :], xbf3[:, j, :]) for j in range(DT)]
        passes += [(wgehi3[:, j, :], xlo3[:, j, :]) for j in range(DT)]
        passes += [(wgelo3[:, j, :], xbf3[:, j, :]) for j in range(DT)]
        mm(4, PS[4][0:10, 0:w], passes, (wge_hi.keys, wge_lo.keys, xbf_keys, xlo_keys))
        op("dve", lambda e: e.tensor_scalar(glog[0:10, 0:w], PS[4][0:10, 0:w], gbias[0:10, 0:1], None, OP.add),
           r=(pkk(4), gbias.keys), w=(glog.keys,), tiny=True)
        nsub = (w + 127) // 128
        subs = []
        for s in range(nsub):
            tw = min(128, w - 128 * s)
            subs.append((s, tw, slice(128 * s, 128 * s + tw)) + tuple(gsc[s]))

        def step(emitter):
            for args in subs:
                emitter(*args)

        for (s, tw, sl, gt, gco, gm, gex_, gpx_, gz_, wtm) in subs:
            op("pe", lambda e, sl=sl, tw=tw: e.transpose(PS[5][0:tw, 0:10], glog[0:10, sl], idn[0:10, 0:10]),
               r=(glog.keys, idn.keys), w=(pkk(5),))
            op("dve", lambda e, tw=tw, gt=gt: e.tensor_copy(gt[0:tw, 0:10], PS[5][0:tw, 0:10]),
               r=(pkk(5),), w=(gt.keys,), tiny=True)
        step(lambda s, tw, sl, gt, gco, gm, gex_, gpx_, gz_, wtm:
             op("dve", lambda e, tw=tw, gt=gt, gco=gco: e.tensor_tensor(gco[0:tw, 0:1], gt[0:tw, 0:1], gt[0:tw, 1:2], OP.subtract),
                r=(gt.keys,), w=(gco.keys,), tiny=True))
        step(lambda s, tw, sl, gt, gco, gm, gex_, gpx_, gz_, wtm:
             op("dve", lambda e, tw=tw, gco=gco: e.scalar_tensor_tensor(gco[0:tw, 1:2], gco[0:tw, 0:1], -1.0, gco[0:tw, 0:1], OP.mult, OP.max),
                r=(gco.keys,), w=(gco.keys,), tiny=True))
        step(lambda s, tw, sl, gt, gco, gm, gex_, gpx_, gz_, wtm:
             op("act", lambda e, tw=tw, gco=gco: e.activation(gco[0:tw, 2:3], gco[0:tw, 1:2], AF.Sigmoid),
                r=(gco.keys,), w=(gco.keys,), tiny=True))
        step(lambda s, tw, sl, gt, gco, gm, gex_, gpx_, gz_, wtm:
             op("dve", lambda e, tw=tw, gco=gco: e.tensor_scalar(gco[0:tw, 4:5], gco[0:tw, 0:1], 0.0, None, OP.is_lt),
                r=(gco.keys,), w=(gco.keys,), tiny=True))
        step(lambda s, tw, sl, gt, gco, gm, gex_, gpx_, gz_, wtm:
             op("dve", lambda e, tw=tw, gco=gco, gm=gm: e.tensor_scalar(gm[0:tw, 0:4], ones8[0:tw, 0:4], gco[0:tw, 4:5], -1.0, OP.mult, OP.mult),
                r=(ones8.keys, gco.keys), w=(gm.keys,), tiny=True))
        step(lambda s, tw, sl, gt, gco, gm, gex_, gpx_, gz_, wtm:
             op("dve", lambda e, tw=tw, gco=gco, gm=gm: e.tensor_scalar(gm[0:tw, 4:8], ones8[0:tw, 0:4], gco[0:tw, 4:5], -1.0, OP.subtract, OP.mult),
                r=(ones8.keys, gco.keys), w=(gm.keys,), tiny=True))
        step(lambda s, tw, sl, gt, gco, gm, gex_, gpx_, gz_, wtm:
             op("dve", lambda e, tw=tw, gt=gt, gm=gm, gz_=gz_: e.scalar_tensor_tensor(gz_[0:tw, 0:8], gm[0:tw, 0:8], BIGM, gt[0:tw, 2:10], OP.mult, OP.add),
                r=(gm.keys, gt.keys), w=(gz_.keys,), tiny=True))
        step(lambda s, tw, sl, gt, gco, gm, gex_, gpx_, gz_, wtm:
             op("dve", lambda e, tw=tw, gco=gco, gz_=gz_: e.reduce_max(gco[0:tw, 5:6], gz_[0:tw, 0:8], AX, negate=True),
                r=(gz_.keys,), w=(gco.keys,), tiny=True))
        step(lambda s, tw, sl, gt, gco, gm, gex_, gpx_, gz_, wtm:
             op("act", lambda e, tw=tw, gco=gco, gex_=gex_, gz_=gz_: e.activation(gex_[0:tw, 0:8], gz_[0:tw, 0:8], AF.Exp, bias=gco[0:tw, 5:6]),
                r=(gz_.keys, gco.keys), w=(gex_.keys,), tiny=True))
        step(lambda s, tw, sl, gt, gco, gm, gex_, gpx_, gz_, wtm:
             op("dve", lambda e, tw=tw, gco=gco, gex_=gex_: e.reduce_sum(gco[0:tw, 6:7], gex_[0:tw, 0:8], AX),
                r=(gex_.keys,), w=(gco.keys,), tiny=True))
        step(lambda s, tw, sl, gt, gco, gm, gex_, gpx_, gz_, wtm:
             op("dve", lambda e, tw=tw, gco=gco: e.reciprocal(gco[0:tw, 7:8], gco[0:tw, 6:7]),
                r=(gco.keys,), w=(gco.keys,), tiny=True))
        step(lambda s, tw, sl, gt, gco, gm, gex_, gpx_, gz_, wtm:
             op("dve", lambda e, tw=tw, gco=gco: e.tensor_tensor(gco[0:tw, 4:5], gco[0:tw, 2:3], gco[0:tw, 7:8], OP.mult),
                r=(gco.keys,), w=(gco.keys,), tiny=True))
        step(lambda s, tw, sl, gt, gco, gm, gex_, gpx_, gz_, wtm:
             op("dve", lambda e, tw=tw, gco=gco, gex_=gex_, gpx_=gpx_: e.tensor_scalar(gpx_[0:tw, 0:8], gex_[0:tw, 0:8], gco[0:tw, 4:5], None, OP.mult),
                r=(gex_.keys, gco.keys), w=(gpx_.keys,), tiny=True))
        step(lambda s, tw, sl, gt, gco, gm, gex_, gpx_, gz_, wtm:
             op("dve", lambda e, tw=tw, gco=gco, gpx_=gpx_: e.reduce_max(gco[0:tw, 0:1], gpx_[0:tw, 0:8], AX),
                r=(gpx_.keys,), w=(gco.keys,), tiny=True))
        step(lambda s, tw, sl, gt, gco, gm, gex_, gpx_, gz_, wtm:
             op("dve", lambda e, tw=tw, gco=gco, gex_=gex_, gpx_=gpx_: e.tensor_scalar(gex_[0:tw, 0:8], gpx_[0:tw, 0:8], gco[0:tw, 0:1], None, OP.is_equal),
                r=(gpx_.keys, gco.keys), w=(gex_.keys,), tiny=True))
        step(lambda s, tw, sl, gt, gco, gm, gex_, gpx_, gz_, wtm:
             op("dve", lambda e, tw=tw, gex_=gex_, gpx_=gpx_, gz_=gz_: e.scalar_tensor_tensor(gz_[0:tw, 0:8], gex_[0:tw, 0:8], -BIGM, gpx_[0:tw, 0:8], OP.mult, OP.add),
                r=(gex_.keys, gpx_.keys), w=(gz_.keys,), tiny=True))
        step(lambda s, tw, sl, gt, gco, gm, gex_, gpx_, gz_, wtm:
             op("dve", lambda e, tw=tw, gco=gco, gz_=gz_: e.reduce_max(gco[0:tw, 1:2], gz_[0:tw, 0:8], AX),
                r=(gz_.keys,), w=(gco.keys,), tiny=True))
        step(lambda s, tw, sl, gt, gco, gm, gex_, gpx_, gz_, wtm:
             op("dve", lambda e, tw=tw, gco=gco, gpx_=gpx_, wtm=wtm: e.scalar_tensor_tensor(wtm[0:tw, 0:8], gpx_[0:tw, 0:8], gco[0:tw, 1:2], gpx_[0:tw, 0:8], OP.is_ge, OP.mult),
                r=(gpx_.keys, gco.keys), w=(wtm.keys,), tiny=True))
        for (s, tw, sl, gt, gco, gm, gex_, gpx_, gz_, wtm) in subs:
            op("pe", lambda e, tw=tw, wtm=wtm: e.transpose(PS[6][0:8, 0:tw], wtm[0:tw, 0:8], idn[0:tw, 0:tw]),
               r=(wtm.keys, idn.keys), w=(pkk(6),))
            op("dve", lambda e, s=s, tw=tw: e.tensor_copy(wse8[0:8, 128 * s:128 * s + tw], PS[6][0:8, 0:tw]),
               r=(pkk(6),), w=(wse8.keys,), tiny=True)
        dma("crow", wse_d[0:NE, t0:t0 + w], wse8[0:8, 0:w], r=(wse8.keys,), w=(("d", "wse", t0),))
        op("act", lambda e, w=w: e.copy(wse8h[0:8, 0:w], wse8[0:8, 0:w]),
           r=(wse8.keys,), w=(wse8h.keys,), tiny=True)
        op("dve", lambda e, w=w: e.tensor_tensor(wse8l[0:8, 0:w], wse8[0:8, 0:w], wse8h[0:8, 0:w], OP.subtract),
           r=(wse8.keys, wse8h.keys), w=(wse8l.keys,), tiny=True)
        ci_g = t0 // TC
        for p_ in range(4):
            dma("crow", wsehl_v[p_, ci_g, 0:2, 0:w], wse8h[2 * p_:2 * p_ + 2, 0:w],
                r=(wse8h.keys,), w=(("d", "wsehl", t0, p_, 0),))
            dma("crow", wsehl_v[p_, ci_g, 2:4, 0:w], wse8l[2 * p_:2 * p_ + 2, 0:w],
                r=(wse8l.keys,), w=(("d", "wsehl", t0, p_, 1),))
    def emit_swiglu(wsb, xbf3, xbf_keys, hbuf3, h_keys, w, scr):
        for i in range(HT):
            ba = (2 * i) % 4
            bb = ba + 1
            mm(ba, PS[ba][:, 0:w],
               [(wsb.t[:, kt, 128 * i:128 * (i + 1)], xbf3[:, kt, :]) for kt in range(DT)],
               (wsb.keys, xbf_keys))
            mm(bb, PS[bb][:, 0:w],
               [(wsb.t[:, kt, 128 * (HT + i):128 * (HT + i + 1)], xbf3[:, kt, :]) for kt in range(DT)],
               (wsb.keys, xbf_keys))
            op("act", lambda e, ba=ba: e.activation(scr.t[:, 0:w], PS[ba][:, 0:w], AF.Silu),
               r=(pkk(ba),), w=(scr.keys,))
            op("dve", lambda e, bb=bb, i=i: e.tensor_tensor(hbuf3[:, i, :], scr.t[:, 0:w], PS[bb][:, 0:w], OP.mult),
               r=(scr.keys, pkk(bb)), w=(h_keys,))

    grid_h = chunk_grid(HTOK)
    grid_t = chunk_grid(TOK)
    PCW = 420
    pc_grid = [(t, min(PCW, HTOK - t)) for t in range(0, HTOK, PCW)]

    dma("crow", tpar.ap, tpar_in, w=(tpar.keys,))
    dma("crow", wge_f.t[:, :, :], wge_T.rearrange("(j p) n -> p j n", p=128), w=(wge_f.keys,))
    dma("crow", gbias.ap, gb_in, w=(gbias.keys,))
    op("pool", lambda e: e.iota(tc1.ap, [[1, 128]], channel_multiplier=0,
                                allow_small_or_imprecise_dtypes=True), w=(tc1.keys,))
    op("pool", lambda e: e.iota(tc2.ap, [[0, 128]], channel_multiplier=1,
                                allow_small_or_imprecise_dtypes=True), w=(tc2.keys,))
    op("dve", lambda e: e.tensor_tensor(idn.ap, tc1.ap, tc2.ap, OP.is_equal),
       r=(tc1.keys, tc2.keys), w=(idn.keys,))
    op("dve", lambda e: e.memset(ones_k.ap, 1.0), w=(ones_k.keys,), tiny=True)
    op("dve", lambda e: e.memset(ones_r.ap, 1.0), w=(ones_r.keys,), tiny=True)
    op("dve", lambda e: e.memset(ones8.ap, 1.0), w=(ones8.keys,), tiny=True)
    op("dve", lambda e: e.memset(pcol.ap, 0.0), w=(pcol.keys,), tiny=True)
    op("dve", lambda e: e.memset(carry_a.ap, 0.0), w=(carry_a.keys,), tiny=True)
    op("dve", lambda e: e.memset(carry_b.ap, 0.0), w=(carry_b.keys,), tiny=True)
    op("act", lambda e: e.activation(etd.ap, tdec.ap, AF.Exp), r=(tdec.keys,), w=(etd.keys,), tiny=True)
    op("dve", lambda e: e.tensor_scalar(netd.ap, etd.ap, -1.0, None, OP.mult),
       r=(etd.keys,), w=(netd.keys,), tiny=True)
    op("act", lambda e: e.activation(decC.ap, etd.ap, AF.Exp, scale=-1.0),
       r=(etd.keys,), w=(decC.keys,), tiny=True)
    op("act", lambda e: e.copy(wge_hi.t[:, :, :], wge_f.t[:, :, :]), r=(wge_f.keys,), w=(wge_hi.keys,), tiny=True)
    for j in range(DT):
        op("dve", lambda e, j=j: e.tensor_tensor(wge_lo.t[:, j, :], wge_f.t[:, j, :], wge_hi.t[:, j, :], OP.subtract),
           r=(wge_f.keys, wge_hi.keys), w=(wge_lo.keys,), tiny=True)

    def wload3(sbT, src, kt, m, tag):
        dma("cw", sbT.t[:, :, :], src.rearrange("(kt p) m -> p kt m", p=128),
            w=(sbT.keys,))

    wload3(w1_sb[0], w1_T, DT, 2 * DH, "w1_0")

    def emit_PA2(call):
        srcB = x_in if call == 0 else xr2_d
        stag = "x" if call == 0 else "xr2"
        grid = grid_h if call == 0 else grid_t
        A = RegionAlloc(b, R3, R3SZ, f"a2{call}")
        xb = A.t("xb", [128, DT, TC], F32)
        tmp = A.t("tmp", [128, DT, TC], F32)
        xbf = A.t("xbf", [128, DT * TC], BF16)
        xc = A.t("xc", [128, DT * TC], BF16)
        scr = A.t("scr", [128, TC], F32)
        hb = b.sb(f"a2{call}_hb", [128, HT, TC], BF16, off=tmp.off0)
        A.off = (A.off + 255) & ~255
        gsc = [tuple(A.t(f"g{nm}{s}", [128, 64], F32)
                     for nm in ("t", "c", "m", "e", "p", "z", "w"))
               for s in range(3)]
        lnr = (A.t("rstd", [128, TC], F32), A.t("mrow", [128, TC], F32))
        for ci, (t0, w) in enumerate(grid):
            dma("cin", xb.t[:, :, :].rearrange("p j t -> p (j t)"), blk(srcB, ci),
                r=(("d", stag, t0),), w=(xb.keys,))
            emit_ln(xb, w, xbf, xc, tmp, xb, bf_out=xbf, lo=xc)
            xbf3 = xbf.t[:, :].rearrange("p (j t) -> p j t", j=DT)[:, :, 0:w]
            xc3 = xc.t[:, :].rearrange("p (j t) -> p j t", j=DT)[:, :, 0:w]
            dma("cxl", blk(xln_d, ci), xbf.ap, r=(xbf.keys,), w=(("d", "xln", t0),))
            emit_gating(t0, w, xbf3, xbf.keys, xc3, xc.keys, gsc)
            emit_swiglu(w1_sb[call], xbf3, xbf.keys, hb.t[:, :, 0:w], hb.keys, w, scr)
            dma("cin", blk(hrt_d, ci), hb.t[:, :, :].rearrange("p j t -> p (j t)"),
                r=(hb.keys,), w=(("d", "hrt", t0),))
        wload3(w_in_sb[call], w_in_T, DT, 2 * DH, f"win{call}")

    def emit_PA1(call):
        grid = grid_h if call == 0 else grid_t
        A = RegionAlloc(b, R3, R3SZ, f"a1{call}")
        xlnb = A.t("xlnb", [128, DT, TC], BF16)
        xlnb2 = A.t("xlnb2", [128, DT, TC], BF16)
        hb = A.t("hb", [128, HT, TC], BF16)
        scr = A.t("scr", [128, TC], F32)
        pob = A.t("pob", [128, DT, TC], F32)
        wload3(w_out_sb[call], w_out_T, HT, D, f"wo{call}")
        bufs = (xlnb, xlnb2)
        prev_key = None
        for ci, (t0, w) in enumerate(grid):
            xl = bufs[ci % 2]
            rk = (("d", "xln", t0),) if (prev_key is None or not SERIAL_PA1[0]) else (("d", "xln", t0), prev_key)
            dma("cxl", xl.t[:, :, :].rearrange("p j t -> p (j t)"), blk(xln_d, ci), r=rk, w=(xl.keys,))
            prev_key = ("d", "pA", t0)
            xbf3 = xl.t[:, :, 0:w]
            emit_swiglu(w_in_sb[call], xbf3, xl.keys, hb.t[:, :, 0:w], hb.keys, w, scr)
            if call == 0 and DBG_HSH[0]:
                dma("cout", hsh_d.rearrange("(j p) t -> p j t", p=128)[:, :, t0:t0 + w],
                    hb.t[:, :, 0:w], r=(hb.keys,), w=(("d", "hsh", t0),))
            for m in range(DT):
                bank = m % 4
                mm(bank, PS[bank][:, 0:w],
                   [(w_out_sb[call].t[:, kt, 128 * m:128 * (m + 1)], hb.t[:, kt, 0:w]) for kt in range(HT)],
                   (w_out_sb[call].keys, hb.keys))
                op("dve", lambda e, bank=bank, m=m, w=w: e.tensor_copy(pob.t[:, m, 0:w], PS[bank][:, 0:w]),
                   r=(pkk(bank),), w=(pob.keys,))
            dma("cin", blk(pA_d, ci), pob.t[:, :, :].rearrange("p j t -> p (j t)"),
                r=(pob.keys,), w=(("d", "pA", t0),))

    HRB = HT * TC * 2
    pb_hrtb = [b.sb(f"pb_hrt{i}", [128, HT, TC], BF16, off=R3 + i * HRB) for i in range(2)]
    he0b = b.sb("pb_he0", [128, HT, TC], BF16, off=R3 + 2 * HRB)
    he1b = b.sb("pb_he1", [128, HT, TC], BF16, off=R3 + 3 * HRB)
    TCB = 1536
    pb_xq = [b.sb(f"pb_xq{i}", [128, TC], F32, off=R3 + 4 * HRB + i * TCB) for i in range(2)]
    pb_po = [b.sb(f"pb_po{i}", [128, TC], F32, off=R3 + 4 * HRB + (2 + i) * TCB) for i in range(2)]
    assert R3 + 4 * HRB + 4 * TCB <= SB_LIMIT

    def emit_PB_call(call):
        grid = grid_h if call == 0 else grid_t
        stag = "x" if call == 0 else "xr2"
        xsrcB = x_in if call == 0 else xr2_d
        n = len(grid)
        steps = [(q, ci) for q in range(4) for ci in range(n)]
        N = len(steps)

        def qinfo(q):
            pin_b = (pA_d, pB_d, pA_d, pB_d)[q]
            pint = ("pA", "pB", "pA", "pB")[q]
            if q < 3:
                pout_b = (pB_d, pA_d, pB_d)[q]
                poutt = ("pB", "pA", "pB")[q]
            else:
                pout_b = xr1vw if call == 0 else outvw
                poutt = "xr1" if call == 0 else "out"
            return pin_b, pint, pout_b, poutt

        def load_hrt(s):
            q, ci = steps[s]
            t0, w = grid[ci]
            dma("cin", pb_hrtb[s % 2].t[:, :, :].rearrange("p j t -> p (j t)"), blk(hrt_d, ci),
                r=(("d", "hrt", t0),), w=(pb_hrtb[s % 2].keys,))

        def rows_and_bcast(s):
            q, ci = steps[s]
            t0, w = grid[ci]
            p = s % 2
            row = q * NCH + ci
            dma("crow", pb_rows4.t[0:1, :, :].rearrange("p e t -> p (e t)"),
                wsehl_d[row:row + 1, :],
                r=(("d", "wsehl", t0, q, 0), ("d", "wsehl", t0, q, 1)),
                w=(pb_rows4.keys,))
            for ei in range(2):
                bank = 6 + ei
                mm(bank, PS[bank][:, 0:w],
                   [(ones_r.ap, pb_rows4.t[0:1, ei, 0:w]), (ones_r.ap, pb_rows4.t[0:1, 2 + ei, 0:w])],
                   (pb_rows4.keys, ones_r.keys))
                op("act", lambda e, bank=bank, p=p, ei=ei, w=w:
                   e.copy(pb_wb[p][ei].t[:, 0:w], PS[bank][:, 0:w]),
                   r=(pkk(bank),), w=(pb_wb[p][ei].keys,))

        def emit_he(s, ei):
            q, ci = steps[s]
            t0, w = grid[ci]
            p = s % 2
            heX = (he0b, he1b)[ei]
            hr = pb_hrtb[p]
            wbX = pb_wb[p][ei]
            eng = "dve" if ei == 0 else "pool"
            for ht in range(HT):
                op(eng, lambda e, ht=ht, hr=hr, wbX=wbX, heX=heX, w=w:
                   e.tensor_tensor(heX.t[:, ht, 0:w], hr.t[:, ht, 0:w], wbX.t[:, 0:w], OP.mult),
                   r=(hr.keys, wbX.keys), w=(heX.keys,))

        load_hrt(0)
        rows_and_bcast(0)
        emit_he(0, 0)
        emit_he(0, 1)

        for s in range(N):
            q, ci = steps[s]
            t0, w = grid[ci]
            pin_b, pint, pout_b, poutt = qinfo(q)
            wq = w2_sb[(call, q)]
            if ci == 0:
                if q < 3:
                    nq = w2_sb[(call, q + 1)]
                    dma("cw", nq.t[:, :, :, :],
                        w2_T[(2 * q + 2) * DH:(2 * q + 4) * DH, :].rearrange(
                            "(e kt p) m -> p e kt m", p=128, kt=HT),
                        w=(nq.keys,))
                elif call == 0:
                    dma("cw", wk_sb.t[:, :, :], wk_T.rearrange("(kt p) m -> p kt m", p=128), w=(wk_sb.keys,))
                    dma("cw", wv_sb.t[:, :, :], wv_T.rearrange("(kt p) m -> p kt m", p=128), w=(wv_sb.keys,))
                    dma("cw", wr_sb.t[:, :, :], wr_T.rearrange("(kt p) m -> p kt m", p=128), w=(wr_sb.keys,))
            if s + 1 < N:
                load_hrt(s + 1)
                rows_and_bcast(s + 1)
            for m in range(DT):
                if m < 3:
                    dma("cout", pb_pin[m].t[:, 0:w], blkm(pin_b, ci, m, w),
                        r=(("d", pint, t0),), w=(pb_pin[m].keys,))
                mm(m, PS[m][:, 0:w],
                   [(wq.t[:, 0, kt, 128 * m:128 * (m + 1)], he0b.t[:, kt, 0:w]) for kt in range(HT)],
                   (wq.keys, he0b.keys), start=True, stop=False)
            if s + 1 < N:
                emit_he(s + 1, 0)
            for m in range(DT):
                if q == 3:
                    xqX = pb_xq[m % 2]
                    dma("cout", xqX.t[:, 0:w], blkm(xsrcB, ci, m, w),
                        r=(("d", stag, t0),), w=(xqX.keys,))
                mm(m, PS[m][:, 0:w],
                   [(wq.t[:, 1, kt, 128 * m:128 * (m + 1)], he1b.t[:, kt, 0:w]) for kt in range(HT)],
                   (wq.keys, he1b.keys), start=False, stop=True)
                pinX = pb_pin[m % 3]
                poX = pb_po[m % 2]
                if q < 3:
                    op("dve", lambda e, m=m, pinX=pinX, poX=poX, w=w:
                       e.scalar_tensor_tensor(poX.t[:, 0:w], PS[m][:, 0:w], 1.0, pinX.t[:, 0:w], OP.mult, OP.add),
                       r=(pkk(m), pinX.keys), w=(poX.keys,))
                else:
                    xqX = pb_xq[m % 2]
                    op("dve", lambda e, m=m, xqX=xqX, poX=poX, w=w:
                       e.scalar_tensor_tensor(poX.t[:, 0:w], PS[m][:, 0:w], 0.5, xqX.t[:, 0:w], OP.mult, OP.add),
                       r=(pkk(m), xqX.keys), w=(poX.keys,))
                    op("dve", lambda e, pinX=pinX, poX=poX, xqX=xqX, w=w:
                       e.scalar_tensor_tensor(xqX.t[:, 0:w], pinX.t[:, 0:w], 0.5, poX.t[:, 0:w], OP.mult, OP.add),
                       r=(pinX.keys, poX.keys), w=(xqX.keys,))
                    poX = xqX
                if q < 3:
                    dst = blkm(pout_b, ci, m, w)
                else:
                    dst = pout_b[:, m, t0:t0 + w]
                dma("cout", dst, poX.t[:, 0:w], r=(poX.keys,), w=(("d", poutt, t0),))
                if m + 3 < DT:
                    dma("cout", pb_pin[m % 3].t[:, 0:w], blkm(pin_b, ci, m + 3, w),
                        r=(("d", pint, t0),), w=(pb_pin[m % 3].keys,))
            if s + 1 < N:
                emit_he(s + 1, 1)

    def emit_q0_load(call):
        wq = w2_sb[(call, 0)]
        dma("cw", wq.t[:, :, :, :],
            w2_T[0:2 * DH, :].rearrange("(e kt p) m -> p e kt m", p=128, kt=HT),
            w=(wq.keys,))

    def emit_PC():
        A = RegionAlloc(b, R3, R3SZ, "pc")
        xb2 = [A.t(f"xb{i}", [128, DT, PCW], F32) for i in range(2)]
        tmp = A.t("tmp", [128, DT, PCW], F32)
        xbf = A.t("xbf", [128, DT * PCW], BF16)
        xc = A.t("xc", [128, DT * PCW], BF16)
        xprev = b.sb("pc_xprev", [128, DT, PCW], F32, off=R1)
        mixA = b.sb("pc_mixA", [128, DT, PCW], BF16, off=R1 + 14336)
        mixB = b.sb("pc_mixB", [128, DT, PCW], BF16, off=R1 + 21504)
        evacs = [b.sb(f"pc_evac{i}", [128, DT, PCW], F32, off=R1 + 28672 + i * 14336)
                 for i in range(2)]
        lnr = (b.sb("pc_rstd", [128, PCW], F32, off=R1 + 57344),
               b.sb("pc_mrow", [128, PCW], F32, off=R1 + 59392))
        wsbs = (wk_sb, wv_sb, wr_sb)
        dsts = (kvw, vvw, rsvw)
        dtags = ("k", "v", "rs")
        xr1_all = tuple(("d", "xr1", t0) for (t0, _w) in grid_h)
        for ci, (t0, w) in enumerate(pc_grid):
            xb = xb2[ci % 2]
            rk = tuple(k for k, (kt0, kw) in zip(xr1_all, grid_h)
                       if kt0 < t0 + w and kt0 + kw > t0)
            dma("cin", xb[:, :, 0:w], xr1vw[:, :, t0:t0 + w], r=rk, w=(xb.keys,))
            emit_ln(xb, w, xbf, xc, tmp, xb)
            for j in range(DT):
                op("pool", lambda e, j=j, w=w, xb=xb: e.tensor_copy(xprev.t[:, j, 1:w], xb.t[:, j, 0:w - 1]),
                   r=(xb.keys,), w=(xprev.keys,))
            pcol3 = pcol.t[:, :].rearrange("p (j t) -> p j t", t=1)
            op("dve", lambda e, pcol3=pcol3: e.tensor_copy(xprev.t[:, :, 0:1], pcol3),
               r=(pcol.keys,), w=(xprev.keys,), tiny=True)
            op("dve", lambda e, w=w, pcol3=pcol3, xb=xb: e.tensor_copy(pcol3, xb.t[:, :, w - 1:w]),
               r=(xb.keys,), w=(pcol.keys,), tiny=True)
            for j in range(DT):
                op("dve", lambda e, j=j, w=w, xb=xb: e.tensor_tensor(tmp.t[:, j, 0:w], xb.t[:, j, 0:w], xprev.t[:, j, 0:w], OP.subtract),
                   r=(xb.keys, xprev.keys), w=(tmp.keys,))
            if t0 == 0:
                s0, c0, cw = 1, 0, w - 1
            else:
                s0, c0, cw = 0, t0 - 1, w
            for i in range(3):
                mx = (mixA, mixB)[i % 2]
                evac = evacs[i % 2]
                for j in range(DT):
                    op("dve", lambda e, i=i, j=j, mx=mx, w=w: e.scalar_tensor_tensor(
                        mx.t[:, j, 0:w], tmp.t[:, j, 0:w], tmix[:, 8 * i + j:8 * i + j + 1],
                        xprev.t[:, j, 0:w], OP.mult, OP.add),
                       r=(tmp.keys, xprev.keys, tmix.keys), w=(mx.keys,))
                for m in range(DT):
                    bank = m % 4
                    mm(bank, PS[bank][:, 0:w],
                       [(wsbs[i].t[:, kt, 128 * m:128 * (m + 1)], mx.t[:, kt, 0:w]) for kt in range(DT)],
                       (wsbs[i].keys, mx.keys))
                    if i == 0:
                        op("act", lambda e, bank=bank, m=m, w=w, evac=evac: e.copy(evac.t[:, m, 0:w], PS[bank][:, 0:w]),
                           r=(pkk(bank),), w=(evac.keys,))
                    elif i == 1:
                        op("dve", lambda e, bank=bank, m=m, w=w, evac=evac: e.tensor_copy(evac.t[:, m, 0:w], PS[bank][:, 0:w]),
                           r=(pkk(bank),), w=(evac.keys,))
                    else:
                        op("act", lambda e, bank=bank, m=m, w=w, evac=evac: e.activation(evac.t[:, m, 0:w], PS[bank][:, 0:w], AF.Sigmoid),
                           r=(pkk(bank),), w=(evac.keys,))
                if i < 2:
                    dma(("cpk", "cpv")[i], dsts[i][:, :, c0:c0 + cw], evac.t[:, :, s0:s0 + cw],
                        r=(evac.keys,), w=(("d", dtags[i], t0),))
                else:
                    dma("cpr", dsts[i][:, 0:4, c0:c0 + cw], evac.t[:, 0:4, s0:s0 + cw],
                        r=(evac.keys,), w=(("d", "rs", t0, 0),))
                    dma("cpr", dsts[i][:, 4:8, c0:c0 + cw], evac.t[:, 4:8, s0:s0 + cw],
                        r=(evac.keys,), w=(("d", "rs", t0, 1),))
        return

    def pc_cov(q0, tag, rs=False):
        out = []
        for (t0, wq_) in pc_grid:
            c0 = 0 if t0 == 0 else t0 - 1
            cw_ = wq_ - 1 if t0 == 0 else wq_
            if c0 < q0 + QW and c0 + cw_ > q0:
                out.append(t0)
        if rs:
            return tuple(("d", tag, t0, h) for t0 in out for h in (0, 1))
        return tuple(("d", tag, t0) for t0 in out)

    def emit_PD():
        A = RegionAlloc(b, R3, WO_OFF - R3, "pd")
        kt_ = A.t("kt", [128, QW], F32)
        vt_ = A.t("vt", [128, QW], F32)
        ekt = A.t("ekt", [128, QW], F32)
        pt_ = A.t("pt", [128, QW], F32)
        ones1k = A.t("ones1k", [128, QW], F32)
        decb = A.t("decb", [128, QW], F32)
        abuf = A.t("abuf", [128, QW], F32)
        bbuf = A.t("bbuf", [128, QW], F32)
        dma("cw", wo_sb.t[:, :, :], wo_T.rearrange("(kt p) m -> p kt m", p=128), w=(wo_sb.keys,))
        op("dve", lambda e: e.memset(ones1k.ap, 1.0), w=(ones1k.keys,), tiny=True)

        for qi in range(TOK // QW):
            q0 = qi * QW
            kkeys = pc_cov(q0, "k")
            vkeys = pc_cov(q0, "v")
            for j in range(DT):
                op("dve", lambda e, j=j: e.tensor_scalar(decb.ap, ones1k.ap, decC[:, j:j + 1], None, OP.mult),
                   r=(ones1k.keys, decC.keys), w=(decb.keys,))
                if DBG_PD[0] and qi == 0:
                    dma("cout", dcb_d[128 * j:128 * (j + 1), :], decb.ap, r=(decb.keys,), w=(("d", "dcb", j),))
                dma("csc", kt_.ap, k_d[128 * j:128 * (j + 1), q0:q0 + QW], r=kkeys, w=(kt_.keys,))
                dma("csc", vt_.ap, v_d[128 * j:128 * (j + 1), q0:q0 + QW], r=vkeys, w=(vt_.keys,))
                op("act", lambda e: e.activation(ekt.ap, kt_.ap, AF.Exp), r=(kt_.keys,), w=(ekt.keys,))
                op("pool", lambda e: e.tensor_tensor(pt_.ap, ekt.ap, vt_.ap, OP.mult),
                   r=(ekt.keys, vt_.keys), w=(pt_.keys,))
                if DBG_PD[0] and qi == 0:
                    dma("cout", pt_d[128 * j:128 * (j + 1), :], pt_.ap, r=(pt_.keys,), w=(("d", "ptd", j),))
                op("dve", lambda e, j=j: e.tensor_tensor_scan(abuf.ap, decb.ap, pt_.ap, carry_a[:, j:j + 1], OP.mult, OP.add),
                   r=(decb.keys, pt_.keys, carry_a.keys), w=(abuf.keys,))
                op("dve", lambda e, j=j: e.tensor_tensor_scan(bbuf.ap, decb.ap, ekt.ap, carry_b[:, j:j + 1], OP.mult, OP.add),
                   r=(decb.keys, ekt.keys, carry_b.keys), w=(bbuf.keys,))
                op("dve", lambda e, j=j: e.tensor_copy(carry_a[:, j:j + 1], abuf[:, QW - 1:QW]),
                   r=(abuf.keys,), w=(carry_a.keys,), tiny=True)
                op("dve", lambda e, j=j: e.tensor_copy(carry_b[:, j:j + 1], bbuf[:, QW - 1:QW]),
                   r=(bbuf.keys,), w=(carry_b.keys,), tiny=True)
                dma("cscw", as_d[128 * j:128 * (j + 1), q0:q0 + QW], abuf.ap,
                    r=(abuf.keys,), w=(("d", "as", j, qi),))
                dma("cscw", bs_d[128 * j:128 * (j + 1), q0:q0 + QW], bbuf.ap,
                    r=(bbuf.keys,), w=(("d", "bs", j, qi),))
        op("dve", lambda e: e.tensor_copy(stage[:, 0:8], carry_a.ap), r=(carry_a.keys,), w=(stage.keys,), tiny=True)
        op("dve", lambda e: e.tensor_copy(stage[:, 8:16], carry_b.ap), r=(carry_b.keys,), w=(stage.keys,), tiny=True)
        dma("crow", ag_in_d, stage.ap, r=(stage.keys,), w=(("d", "agin"),))
        b.emit("pool", lambda e: e.collective_compute(
            "AllGather", OP.bypass, replica_groups=[list(range(8))],
            ins=[ag_in_d.opt()], outs=[ag_out_d.opt()]),
            r=(("d", "agin"),), w=(("d", "agout"),), coll=True)
        dma("crow", gsb.t[:, :, :], ag_out_d.rearrange("(r p) c -> p r c", p=128),
            r=(("d", "agout"),), w=(gsb.keys,))
        op("dve", lambda e: e.tensor_scalar(SAB2.ap, gsb.t[:, 0, :], selv[:, 0:1], None, OP.mult),
           r=(gsb.keys, selv.keys), w=(SAB2.keys,), tiny=True)
        for rr in range(1, 8):
            src_, dst_ = (SAB2, SAB1) if rr % 2 == 1 else (SAB1, SAB2)
            op("dve", lambda e, rr=rr, src_=src_, dst_=dst_: e.scalar_tensor_tensor(
                dst_.ap, gsb.t[:, rr, :], selv[:, rr:rr + 1], src_.ap, OP.mult, OP.add),
               r=(gsb.keys, selv.keys, src_.keys), w=(dst_.keys,), tiny=True)

    def emit_PE():
        names = ("kt", "vt", "rt", "ap", "bp", "dpw", "eku", "tq")
        PEB = [{nm: b.sb(f"pe_{nm}{p}", [128, QW], F32, off=R2 + i * 4096 + p * 32768)
                for i, nm in enumerate(names)} for p in (0, 1)]
        iqs = [b.sb(f"pe_iq{p}", [128, QW], F32, off=R3 + p * 4096) for p in (0, 1)]
        for qi in range(TOK // QW):
            q0 = qi * QW
            kkey = pc_cov(q0, "k")
            vkey = pc_cov(q0, "v")
            rkey = pc_cov(q0, "rs", rs=True)
            iq = iqs[qi % 2]
            op("pool", lambda e, q0=q0, iq=iq: e.iota(iq.ap, [[1, QW]], base=q0, channel_multiplier=0,
                                                      allow_small_or_imprecise_dtypes=True), w=(iq.keys,))
            for j in range(DT):
                P = PEB[(qi * DT + j) % 2]
                kt_, vt_, rt_ = P["kt"], P["vt"], P["rt"]
                ap_, bp_, dpw, eku, tq = P["ap"], P["bp"], P["dpw"], P["eku"], P["tq"]
                akey = (("d", "as", j, qi), ("d", "as", j, max(0, qi - 1)))
                bkey = (("d", "bs", j, qi), ("d", "bs", j, max(0, qi - 1)))
                dma("csc", kt_.ap, k_d[128 * j:128 * (j + 1), q0:q0 + QW], r=kkey, w=(kt_.keys,))
                dma("csc", vt_.ap, v_d[128 * j:128 * (j + 1), q0:q0 + QW], r=vkey, w=(vt_.keys,))
                dma("csc", rt_.ap, rs_d[128 * j:128 * (j + 1), q0:q0 + QW], r=rkey, w=(rt_.keys,))
                if q0 == 0:
                    op("dve", lambda e, ap_=ap_: e.memset(ap_[:, 0:1], 0.0), w=(ap_.keys,), tiny=True)
                    op("dve", lambda e, bp_=bp_: e.memset(bp_[:, 0:1], 0.0), w=(bp_.keys,), tiny=True)
                    dma("csc", ap_[:, 1:QW], as_d[128 * j:128 * (j + 1), 0:QW - 1], r=akey, w=(ap_.keys,))
                    dma("csc", bp_[:, 1:QW], bs_d[128 * j:128 * (j + 1), 0:QW - 1], r=bkey, w=(bp_.keys,))
                else:
                    dma("csc", ap_.ap, as_d[128 * j:128 * (j + 1), q0 - 1:q0 + QW - 1], r=akey, w=(ap_.keys,))
                    dma("csc", bp_.ap, bs_d[128 * j:128 * (j + 1), q0 - 1:q0 + QW - 1], r=bkey, w=(bp_.keys,))
                op("dve", lambda e, j=j, tq=tq, iq=iq: e.tensor_scalar(tq.ap, iq.ap, netd[:, j:j + 1], None, OP.mult),
                   r=(iq.keys, netd.keys), w=(tq.keys,))
                op("act", lambda e, dpw=dpw, tq=tq: e.activation(dpw.ap, tq.ap, AF.Exp), r=(tq.keys,), w=(dpw.keys,))
                op("act", lambda e, j=j, eku=eku, kt_=kt_: e.activation(eku.ap, kt_.ap, AF.Exp, bias=tfir[:, j:j + 1]),
                   r=(kt_.keys, tfir.keys), w=(eku.keys,))
                op("pool", lambda e, tq=tq, eku=eku, vt_=vt_: e.tensor_tensor(tq.ap, eku.ap, vt_.ap, OP.mult),
                   r=(eku.keys, vt_.keys), w=(tq.keys,))
                op("dve", lambda e, j=j, kt_=kt_, dpw=dpw, ap_=ap_: e.scalar_tensor_tensor(kt_.ap, dpw.ap, SAB1[:, j:j + 1], ap_.ap, OP.mult, OP.add),
                   r=(dpw.keys, SAB1.keys, ap_.keys), w=(kt_.keys,))
                op("dve", lambda e, j=j, vt_=vt_, dpw=dpw, bp_=bp_: e.scalar_tensor_tensor(vt_.ap, dpw.ap, SAB1[:, 8 + j:8 + j + 1], bp_.ap, OP.mult, OP.add),
                   r=(dpw.keys, SAB1.keys, bp_.keys), w=(vt_.keys,))
                op("pool", lambda e, ap_=ap_, kt_=kt_, tq=tq: e.tensor_tensor(ap_.ap, kt_.ap, tq.ap, OP.add),
                   r=(kt_.keys, tq.keys), w=(ap_.keys,))
                op("pool", lambda e, bp_=bp_, vt_=vt_, eku=eku: e.tensor_tensor(bp_.ap, vt_.ap, eku.ap, OP.add),
                   r=(vt_.keys, eku.keys), w=(bp_.keys,))
                op("dve", lambda e, dpw=dpw, bp_=bp_: e.tensor_scalar(dpw.ap, bp_.ap, 1e-6, None, OP.max),
                   r=(bp_.keys,), w=(dpw.keys,))
                op("dve", lambda e, tq=tq, dpw=dpw: e.reciprocal(tq.ap, dpw.ap), r=(dpw.keys,), w=(tq.keys,))
                op("pool", lambda e, kt_=kt_, ap_=ap_, tq=tq: e.tensor_tensor(kt_.ap, ap_.ap, tq.ap, OP.mult),
                   r=(ap_.keys, tq.keys), w=(kt_.keys,))
                op("dve", lambda e, j=j, q0=q0, kt_=kt_, rt_=rt_: e.tensor_tensor(rwb.t[:, j, q0:q0 + QW], kt_.ap, rt_.ap, OP.mult),
                   r=(kt_.keys, rt_.keys), w=(rwb.keys,))


    def emit_PF():
        A = RegionAlloc(b, R3, WO_OFF - R3, "pf")
        xbs = [A.t(f"xb{i}", [128, DT, TC], F32) for i in range(2)]
        ob = A.t("ob", [128, DT, TC], F32)
        xr1keys = tuple(("d", "xr1", t0) for (t0, _w) in grid_h)
        for ci, (t0, w) in enumerate(grid_t):
            xb = xbs[ci % 2]
            dma("cxb", xb[:, 0:4, 0:w], xr1vw[:, 0:4, t0 + 1:t0 + 1 + w],
                r=xr1keys, w=(xb.keys,))
            dma("cxb", xb[:, 4:8, 0:w], xr1vw[:, 4:8, t0 + 1:t0 + 1 + w],
                r=xr1keys, w=(xb.keys,))
            for m in range(DT):
                bank = m % 4
                mm(bank, PS[bank][:, 0:w],
                   [(wo_sb.t[:, kt, 128 * m:128 * (m + 1)], rwb.t[:, kt, t0:t0 + w]) for kt in range(DT)],
                   (wo_sb.keys, rwb.keys))
                op("dve", lambda e, bank=bank, m=m, w=w, xb=xb: e.tensor_tensor(ob.t[:, m, 0:w], PS[bank][:, 0:w], xb.t[:, m, 0:w], OP.add),
                   r=(pkk(bank), xb.keys), w=(ob.keys,))
            dma("cin", blk(xr2_d, ci), ob.t[:, :, :].rearrange("p j t -> p (j t)"),
                r=(ob.keys,), w=(("d", "xr2", t0),))
        wload3(w1_sb[1], w1_T, DT, 2 * DH, "w1_1")

    allph = ["A2", "A1", "B", "C", "D", "E", "F", "A2b", "A1b", "Bb"]
    ph = allph if phases is None else phases
    if "A2" in ph:
        emit_PA2(0)
    if "A1" in ph:
        emit_PA1(0)
    if "B" in ph:
        emit_q0_load(0)
        emit_PB_call(0)
    if "C" in ph:
        emit_PC()
    if "D" in ph:
        emit_PD()
    if "E" in ph:
        emit_PE()
    if "F" in ph:
        emit_PF()
    if "A2b" in ph:
        emit_PA2(1)
    if "A1b" in ph:
        emit_PA1(1)
    if "Bb" in ph:
        emit_q0_load(1)
        emit_PB_call(1)

    dbg_src = {"xln1": (xln_d, "xln", grid_h), "wse": (wse_d, "wse", grid_h),
               "hsh": (hsh_d, "hsh", grid_h), "pB": (pB_d, "pB", grid_h),
               "as": (as_d, "as", None), "bs": (bs_d, "bs", None),
               "dcb": (dcb_d, "dcb", None), "ptd": (pt_d, "ptd", None),
               "pA": (pA_d, "pA", grid_h), "xr1": (xr1_d, "xr1", grid_h),
               "hrt": (hrt_d, "hrt", grid_h),
               "k": (k_d, "k", pc_grid), "v": (v_d, "v", pc_grid), "rs": (rs_d, "rs", pc_grid),
               "xr2": (xr2_d, "xr2", grid_t)}
    if "rwb" in debug:
        dd = nc.dram_tensor("dbg_rwb", [128, DT * TOK], BF16, kind="ExternalOutput").ap()
        dma("cout", dd, rwb.t[:, :, :].rearrange("p j t -> p (j t)"), r=(rwb.keys,), w=())
    for nm in debug:
        if nm not in dbg_src:
            continue
        src, tag, grid = dbg_src[nm]
        shp = list(src.tensor.shape) if hasattr(src, "tensor") else None
        dt = BF16 if nm in ("xln1", "hrt", "hsh") else F32
        dd = nc.dram_tensor("dbg_" + nm, [src.shape[0], src.shape[1]], dt, kind="ExternalOutput").ap()
        if nm in ("dcb", "ptd"):
            keys = tuple(("d", tag, j) for j in range(DT))
        elif nm in ("as", "bs"):
            keys = tuple(("d", tag, j, qi) for j in range(DT) for qi in range(TOK // QW))
        elif nm == "rs":
            keys = tuple(("d", "rs", t0, h) for (t0, _w) in grid for h in (0, 1))
        else:
            keys = tuple(("d", tag, t0) for (t0, _w) in grid)
        dma("cout", dd, src, r=keys, w=())

    b.build_blocks(serial=serial)
    nc._b = b
    return nc



def _prep(inputs):
    f = np.float32
    xs = np.asarray(inputs["L0_attn_x"], f)
    w_in = np.asarray(inputs["L0_ffn_w_in"], f)
    w_out = np.asarray(inputs["L0_ffn_w_out"], f)
    w1 = np.asarray(inputs["L0_ffn_w1"], f)
    w2 = np.asarray(inputs["L0_ffn_w2"], f)
    wg = np.asarray(inputs["L0_ffn_wg"], f)
    we = np.asarray(inputs["L0_ffn_we"], f)
    gb = np.asarray(inputs["L0_ffn_gb"], f)
    eb = np.asarray(inputs["L0_ffn_eb"], f)
    wk = np.asarray(inputs["L0_attn_Wk"], f)
    wv = np.asarray(inputs["L0_attn_Wv"], f)
    wr = np.asarray(inputs["L0_attn_Wr"], f)
    wo = np.asarray(inputs["L0_attn_Wo"], f)
    td = np.asarray(inputs["L0_attn_time_decay"], f)
    tf = np.asarray(inputs["L0_attn_time_first"], f)
    tmk = np.asarray(inputs["L0_attn_time_mix_k"], f)
    tmv = np.asarray(inputs["L0_attn_time_mix_v"], f)
    tmr = np.asarray(inputs["L0_attn_time_mix_r"], f)

    def colmaj(vec):
        return np.ascontiguousarray(vec.reshape(DT, 128).T)

    tmix_a = np.concatenate([colmaj(tmk), colmaj(tmv), colmaj(tmr)], axis=1)
    tpar_base = np.concatenate([tmix_a, colmaj(td), colmaj(tf)], axis=1)
    shared = {
        "w_in_T": np.ascontiguousarray(w_in.T),
        "w1_T": np.ascontiguousarray(w1.T),
        "w_out_T": np.ascontiguousarray(w_out.T),
        "w2_T": np.ascontiguousarray(w2.transpose(0, 2, 1).reshape(NE * DH, D)),
        "wk_T": np.ascontiguousarray(wk.T),
        "wv_T": np.ascontiguousarray(wv.T),
        "wr_T": np.ascontiguousarray(wr.T),
        "wo_T": np.ascontiguousarray(wo.T),
        "wge_T": np.ascontiguousarray(np.concatenate([wg, we], 0).T),
        "gbias": np.ascontiguousarray(np.concatenate([gb, eb])[:, None]),
    }
    grid = []
    t = 0
    while t < HTOK:
        grid.append((t, min(TC, HTOK - t)))
        t += TC
    nch = len(grid)
    in_maps = []
    for c in range(8):
        bb, h = divmod(c, 2)
        seg = xs[bb, h * TOK:(h + 1) * TOK]
        halo = xs[bb, h * TOK - 1][None] if h == 1 else np.zeros((1, D), f)
        shard = np.concatenate([halo, seg], 0).T
        sh3 = shard.reshape(DT, 128, HTOK).transpose(1, 0, 2)
        xblk = np.zeros((nch, 128, DT, TC), f)
        for ci, (t0, w) in enumerate(grid):
            xblk[ci, :, :, 0:w] = sh3[:, :, t0:t0 + w]
        sel = np.zeros((128, DT), f)
        if h == 1:
            sel[:, c - 1] = 1.0
        m = dict(shared)
        m["x"] = np.ascontiguousarray(xblk.reshape(nch * 128, DT * TC))
        m["tparams"] = np.ascontiguousarray(np.concatenate([tpar_base, sel], axis=1))
        in_maps.append(m)
    return in_maps


_NC_CACHE = {}


def kernel(**inputs):
    in_maps = _prep(inputs)
    key = "main"
    if key not in _NC_CACHE:
        _NC_CACHE[key] = build_program()
    nc = _NC_CACHE[key]
    res = run_bass_kernel_spmd(nc, in_maps, core_ids=list(range(8)))
    out = np.empty((4, 8192, D), np.float32)
    for c in range(8):
        bb, h = divmod(c, 2)
        out[bb, h * TOK:(h + 1) * TOK] = np.asarray(res.results[c]["out"]).T
    return out


# version 24
# speedup vs baseline: 1.1674x; 1.1674x over previous
import numpy as np
import concourse.bass as bass
import concourse.mybir as mybir
from concourse.bass_utils import run_bass_kernel_spmd

F32 = mybir.dt.float32
BF16 = mybir.dt.bfloat16
OP = mybir.AluOpType
AF = mybir.ActivationFunctionType
AX = mybir.AxisListType.X

D = 1024
DT = 8
DH = 2048
HT = 16
NE = 8
TOK = 4096
HTOK = 4097
TC = 373
QW = 1024
BIGM = 30000.0
EPS = 1e-5

ENGS = ("pe", "act", "dve", "pool", "sp")
BLK_ATTR = {"pe": "tensor", "act": "scalar", "dve": "vector", "pool": "gpsimd", "sp": "sync"}
GRAN = 256
CROT = 4
SB_BASE = 16640
SB_LIMIT = 228352

R1 = SB_BASE + 26624
R2 = R1 + 65536
R3 = R2 + 65536
R3SZ = SB_LIMIT - R3
WO_OFF = R3 + 36864


def chunk_grid(total):
    out, t = [], 0
    while t < total:
        out.append((t, min(TC, total - t)))
        t += TC
    return out


class Rec:
    __slots__ = ("eng", "fn", "kind", "chan", "cidx", "ms", "deps", "waits", "gi", "snap", "tiny")

    def __init__(self, eng, fn, kind, chan, cidx, gi):
        self.eng, self.fn, self.kind, self.chan, self.cidx, self.gi = eng, fn, kind, chan, cidx, gi
        self.ms = 0
        self.deps = None
        self.waits = None


class ST:

    def __init__(self, t, off, nbytes):
        self.t = t
        self.off0 = off
        g0, g1 = off // GRAN, (off + nbytes + GRAN - 1) // GRAN
        self.keys = tuple(("s", g) for g in range(g0, g1))

    def __getitem__(self, idx):
        return self.t[idx]

    @property
    def ap(self):
        return self.t[:]


class RegionAlloc:
    def __init__(self, b, base, size, tag):
        self.b, self.base, self.end, self.off, self.tag = b, base, base + size, base, tag

    def t(self, name, shape, dtype):
        esz = 2 if dtype == BF16 else 4
        nb = int(np.prod(shape[1:])) * esz
        off = (self.off + 31) & ~31
        assert off + nb <= self.end, f"{self.tag}:{name} overflow {off}+{nb}>{self.end}"
        self.off = off + nb
        return self.b.sb(f"{self.tag}_{name}", shape, dtype, off=off)


class B:
    def __init__(self, nc):
        self.nc = nc
        self.order = []
        self.per_eng = {e: [] for e in ENGS}
        self.chans = {}
        self.writers = {}
        self.readers = {}
        self.n_coll = 0
        self._names = set()

    def sb(self, name, shape, dtype, off):
        assert name not in self._names, name
        self._names.add(name)
        esz = 2 if dtype == BF16 else 4
        nb = int(np.prod(shape[1:])) * esz
        assert off + nb <= SB_LIMIT, f"sbuf overflow {name}: {off}+{nb}"
        t = self.nc.alloc_sbuf_tensor_at(name, list(shape), dtype, offset=off)
        return ST(t, off, nb)

    def chan(self, name, eng):
        self.chans[name] = {"eng": eng, "n": 0}

    @staticmethod
    def _flat(keys):
        out = []
        for k in keys:
            if isinstance(k, tuple) and k and isinstance(k[0], tuple):
                out.extend(k)
            else:
                out.append(k)
        return out

    def emit(self, eng, fn, r=(), w=(), chan=None, coll=False, tiny=False):
        kind = "coll" if coll else ("dma" if chan else "eng")
        cidx = None
        if chan:
            cidx = self.chans[chan]["n"]
            self.chans[chan]["n"] += 1
        if coll:
            cidx = self.n_coll
            self.n_coll += 1
        rec = Rec(eng, fn, kind, chan, cidx, len(self.order))
        rec.tiny = tiny
        rec.snap = ({e: sum(1 for x in self.per_eng[e] if x.kind != "dma") for e in ENGS},
                    {c: m["n"] for c, m in self.chans.items()}, self.n_coll)
        rset, wset = self._flat(r), self._flat(w)
        deps = set()
        for k in rset:
            p = self.writers.get(k)
            if p is not None:
                deps.add(p)
        for k in wset:
            p = self.writers.get(k)
            if p is not None:
                deps.add(p)
            for q in self.readers.get(k, ()):
                deps.add(q)
        for k in rset:
            self.readers.setdefault(k, []).append(rec)
        for k in wset:
            self.writers[k] = rec
            self.readers[k] = []
        deps.discard(rec)
        rec.deps = deps
        self.order.append(rec)
        self.per_eng[eng].append(rec)
        return rec

    def finalize(self, serial=False):
        if serial:
            ms_no = {e: 0 for e in ENGS}
            eng_ms = {}
            for e in ENGS:
                for rec in self.per_eng[e]:
                    if rec.kind != "dma" and rec.kind != "coll":
                        ms_no[e] += 1
                        rec.ms = ms_no[e]
            cnt = {e: 0 for e in ENGS}
            for rec in self.order:
                waits = {}
                for e2 in ENGS:
                    if e2 != rec.eng and cnt[e2] > 0:
                        waits[("e", e2)] = cnt[e2]
                for c2, n2 in rec.snap[1].items():
                    for r_ in range(CROT):
                        nr = (n2 + CROT - 1 - r_) // CROT
                        if nr > 0:
                            waits[("c", c2, r_)] = 16 * nr
                if rec.snap[2] > 0:
                    waits[("x",)] = rec.snap[2]
                rec.waits = waits
                if rec.kind == "eng" and rec.ms > 0:
                    cnt[rec.eng] = rec.ms
            return ms_no
        for rec in self.order:
            for d in rec.deps:
                if d.kind == "eng" and (d.eng != rec.eng or d.tiny or rec.kind != "eng"):
                    d.ms = -1
        ms_no = {e: 0 for e in ENGS}
        for e in ENGS:
            for rec in self.per_eng[e]:
                if rec.kind == "eng" and rec.ms == -1:
                    ms_no[e] += 1
                    rec.ms = ms_no[e]
        for rec in self.order:
            waits = {}
            for d in rec.deps:
                if d.kind == "eng":
                    if d.eng == rec.eng and rec.kind == "eng" and not d.tiny:
                        continue
                    k = ("e", d.eng)
                    waits[k] = max(waits.get(k, 0), d.ms)
                elif d.kind == "dma":
                    k = ("c", d.chan, d.cidx % CROT)
                    waits[k] = max(waits.get(k, 0), 16 * (d.cidx // CROT + 1))
                else:
                    k = ("x",)
                    waits[k] = max(waits.get(k, 0), d.cidx + 1)
            rec.waits = waits
        return ms_no

    def build_blocks(self, serial=False):
        ms_no = self.finalize(serial=serial)
        nc = self.nc
        esem = {e: nc.alloc_semaphore(f"sem_{e}") for e in ENGS}
        csem = {c: [nc.alloc_semaphore(f"sem_{c}{r}") for r in range(CROT)]
                for c in self.chans}
        xsem = nc.alloc_semaphore("sem_coll")

        def semof(key):
            if key[0] == "e":
                return esem[key[1]]
            if key[0] == "c":
                return csem[key[1]][key[2]]
            return xsem

        with nc.Block() as blk:
            def make(engname, rlist):
                def body(eng):
                    cur = {}
                    for rec in rlist:
                        for key, val in sorted(rec.waits.items()):
                            if cur.get(key, 0) < val:
                                eng.wait_ge(semof(key), val)
                                cur[key] = val
                        inst = rec.fn(eng)
                        if rec.kind == "dma":
                            inst.then_inc(csem[rec.chan][rec.cidx % CROT], 16)
                        elif rec.kind == "coll":
                            inst.then_inc(xsem, 1)
                        elif rec.ms > 0:
                            inst.then_inc(esem[rec.eng], 1)
                    if engname == "sp":
                        for c, meta in self.chans.items():
                            for r in range(CROT):
                                nr = (meta["n"] + CROT - 1 - r) // CROT
                                if nr:
                                    eng.wait_ge(csem[c][r], 16 * nr)
                        if self.n_coll:
                            eng.wait_ge(xsem, self.n_coll)
                        for e2 in ENGS:
                            if e2 != "sp" and ms_no[e2]:
                                eng.wait_ge(esem[e2], ms_no[e2])
                return body

            for e in ENGS:
                getattr(blk, BLK_ATTR[e])(make(e, self.per_eng[e]))



DBG_HSH = [False]
SERIAL_PA1 = [False]
DBG_PD = [False]
FORCE_TINY = [False]


def build_program(debug=(), phases=None, serial=False):
    DBG_HSH[0] = "hsh" in debug
    DBG_PD[0] = "pd" in debug
    nc = bass.Bass("TRN2", target_bir_lowering=False)
    b = B(nc)
    b.chan("cin", "sp")
    b.chan("cout", "act")
    b.chan("cw", "pool")
    b.chan("crow", "sp")
    b.chan("csc", "act")
    b.chan("cscw", "act")
    b.chan("cxb", "sp")
    b.chan("cxl", "sp")
    b.chan("cpk", "pool")
    b.chan("cpv", "pool")
    b.chan("cpr", "pool")

    def din(name, shape, dtype=F32):
        return nc.dram_tensor(name, list(shape), dtype, kind="ExternalInput").ap()

    NCH = len(chunk_grid(HTOK))
    x_in = din("x", [NCH * 128, DT * TC])
    w_in_T = din("w_in_T", [D, 2 * DH])
    w1_T = din("w1_T", [D, 2 * DH])
    w_out_T = din("w_out_T", [DH, D])
    w2_T = din("w2_T", [NE * DH, D])
    wk_T = din("wk_T", [D, D])
    wv_T = din("wv_T", [D, D])
    wr_T = din("wr_T", [D, D])
    wo_T = din("wo_T", [D, D])
    wge_T = din("wge_T", [D, 10])
    gb_in = din("gbias", [10, 1])
    tpar_in = din("tparams", [128, 6 * DT])

    out_d = nc.dram_tensor("out", [D, TOK], F32, kind="ExternalOutput").ap()

    def idram(name, shape, dtype=F32):
        return nc.dram_tensor(name, list(shape), dtype).ap()

    xln_d = idram("xln_d", [NCH * 128, DT * TC], BF16)
    hrt_d = idram("hrt_d", [NCH * 128, HT * TC], BF16)
    pA_d = idram("pA_d", [NCH * 128, DT * TC])
    pB_d = idram("pB_d", [NCH * 128, DT * TC])
    xr2_d = idram("xr2_d", [NCH * 128, DT * TC])
    wse_d = idram("wse_d", [NE, HTOK])
    wsehl_d = idram("wsehl_d", [4 * NCH, 4 * TC], BF16)
    wsehl_v = wsehl_d.rearrange("(p c) (e t) -> p c e t", c=NCH, e=4)
    xr1_d = idram("xr1_d", [D, HTOK])
    k_d = idram("k_d", [D, TOK])
    v_d = idram("v_d", [D, TOK])
    rs_d = idram("rs_d", [D, TOK])
    as_d = idram("as_d", [D, TOK])
    bs_d = idram("bs_d", [D, TOK])
    hsh_d = idram("hsh_d", [DH, HTOK], BF16)
    dcb_d = idram("dcb_d", [D, QW])
    pt_d = idram("pt_d", [D, QW])
    ag_in_d = idram("ag_in", [128, 16])
    ag_out_d = idram("ag_out", [8 * 128, 16])

    def jv(ap):
        return ap.rearrange("(j p) t -> p j t", p=128)

    def blk(ap, c):
        return ap[c * 128:(c + 1) * 128, :]

    def blkm(ap, c, m, w):
        return ap[c * 128:(c + 1) * 128, m * TC:m * TC + w]

    xr1vw = jv(xr1_d)
    outvw = jv(out_d)
    kvw = jv(k_d)
    vvw = jv(v_d)
    rsvw = jv(rs_d)

    PS = [nc.alloc_psum_tensor(f"psb{i}", [128, 512], F32) for i in range(8)]

    def pkk(i):
        return ("p", i)

    CA = RegionAlloc(b, SB_BASE, R1 - SB_BASE, "c")
    idn = CA.t("idn", [128, 128], F32)
    ones_k = CA.t("ones_k", [128, 1], BF16)
    ones_r = CA.t("ones_r", [1, 128], BF16)
    tpar = CA.t("tpar", [128, 6 * DT], F32)
    tmix = b.sb("tmix", [128, 3 * DT], F32, off=tpar.off0)
    tdec = b.sb("tdec", [128, DT], F32, off=tpar.off0 + 96)
    tfir = b.sb("tfir", [128, DT], F32, off=tpar.off0 + 128)
    selv = b.sb("selv", [128, DT], F32, off=tpar.off0 + 160)
    etd = CA.t("etd", [128, DT], F32)
    netd = CA.t("netd", [128, DT], F32)
    decC = CA.t("dec", [128, DT], F32)
    wge_f = CA.t("wge_f", [128, DT, 10], F32)
    wge_hi = CA.t("wge_hi", [128, DT, 10], BF16)
    wge_lo = CA.t("wge_lo", [128, DT, 10], BF16)
    gbias = CA.t("gbias", [10, 1], F32)
    SAB1 = CA.t("SAB1", [128, 16], F32)
    SAB2 = CA.t("SAB2", [128, 16], F32)
    stage = CA.t("stage", [128, 16], F32)
    gsb = CA.t("gsb", [128, DT, 16], F32)
    carry_a = CA.t("carry_a", [128, DT], F32)
    carry_b = CA.t("carry_b", [128, DT], F32)
    pcol = CA.t("pcol", [128, DT], F32)
    rowA = CA.t("rowA", [1, 512], F32)
    rowB = CA.t("rowB", [1, 512], F32)
    rowC = CA.t("rowC", [1, 512], F32)
    rowD = CA.t("rowD", [1, 512], F32)
    rowE = CA.t("rowE", [1, 512], F32)
    rhiA = CA.t("rhiA", [1, 512], BF16)
    rloA = CA.t("rloA", [1, 512], BF16)
    rhiB = CA.t("rhiB", [1, 512], BF16)
    rloB = CA.t("rloB", [1, 512], BF16)
    glog = CA.t("glog", [10, 512], F32)
    wse8 = CA.t("wse8", [8, 512], F32)
    ones8 = CA.t("ones8", [128, 8], F32)
    wse8h = CA.t("wse8h", [8, 512], BF16)
    wse8l = CA.t("wse8l", [8, 512], BF16)
    assert CA.off <= R1, CA.off
    pbs0 = rowA.off0
    pb_pin = [b.sb(f"pbs_pin{i}", [128, TC], F32, off=pbs0 + i * 1536) for i in range(2)]
    pb_wb = [[b.sb(f"pbs_wb{p}{e}", [128, TC], F32, off=pbs0 + 3072 + (2 * p + e) * 1536)
              for e in range(2)] for p in range(2)]
    pb_rows4 = b.sb("pbs_rows4", [1, 4, TC], BF16, off=pbs0 + 9216)
    pb_po = [b.sb(f"pbs_po{i}", [128, TC], F32, off=pbs0 + 12288 + i * 1536) for i in range(2)]
    assert pbs0 + 12288 + 2 * 1536 <= rowA.off0 + 16384
    tc1 = b.sb("tc1", [128, 128], F32, off=R3)
    tc2 = b.sb("tc2", [128, 128], F32, off=R3 + 512)

    def wslot(call):
        return (R1, R2) if call == 0 else (R2, R1)

    w1_sb, w_in_sb, w_out_sb, w2_sb = {}, {}, {}, {}
    for call in (0, 1):
        sA, sB = wslot(call)
        w1_sb[call] = b.sb(f"w1s{call}", [128, DT, 2 * DH], BF16, off=sA)
        w_in_sb[call] = b.sb(f"wins{call}", [128, DT, 2 * DH], BF16, off=sB)
        w_out_sb[call] = b.sb(f"wouts{call}", [128, HT, D], BF16, off=sA)
        for q in range(4):
            off = sB if q % 2 == 0 else sA
            w2_sb[(call, q)] = b.sb(f"w2s{call}_{q}", [128, 2, HT, D], BF16, off=off)
    wk_sb = b.sb("wk_sb", [128, DT, D], BF16, off=R2)
    wv_sb = b.sb("wv_sb", [128, DT, D], BF16, off=R2 + 16384)
    wr_sb = b.sb("wr_sb", [128, DT, D], BF16, off=R2 + 32768)
    rwb = b.sb("rwb", [128, DT, TOK], BF16, off=R1)
    wo_sb = b.sb("wo_sb", [128, DT, D], BF16, off=WO_OFF)

    def dma(chan, dst, src, r=(), w=(), slow=False):
        eng = b.chans[chan]["eng"]

        def fn(e):
            if slow:
                return e.dma_start(dst, src, allow_slow_non_contiguous=True)
            return e.dma_start(dst, src)
        return b.emit(eng, fn, r=r, w=w, chan=chan)

    def op(eng, fn, r=(), w=(), tiny=False):
        return b.emit(eng, fn, r=r, w=w, tiny=tiny or FORCE_TINY[0])

    def mm(bank, out_ap, passes, reads, start=True, stop=True):
        def fn(eng):
            n = len(passes)
            inst = None
            for i, (l, rr) in enumerate(passes):
                inst = eng.matmul(out_ap, l, rr, start=(start and i == 0),
                                  stop=(stop and i == n - 1))
            return inst
        b.emit("pe", fn, r=reads, w=(pkk(bank),))

    def emit_ln(srcT, w, xbf, xc, tmpT, dstT,
                bf_out=None, lo=None, rows2=None):
        src3 = srcT.t[:, :, 0:w]
        src_keys = srcT.keys
        tmp3 = tmpT.t[:, :, 0:w]
        tmp_keys = tmpT.keys
        dst3 = dstT.t[:, :, 0:w]
        dst_keys = dstT.keys
        src2 = srcT.t[:, :, :].rearrange("p j t -> p (j t)")
        op("act", lambda e: e.copy(xbf.ap, src2), r=(src_keys,), w=(xbf.keys,))
        op("act", lambda e: e.activation(xc.ap, src2, AF.Square),
           r=(src_keys,), w=(xc.keys,))
        xbf3 = xbf.t[:, :].rearrange("p (j t) -> p j t", j=DT)
        xc3 = xc.t[:, :].rearrange("p (j t) -> p j t", j=DT)
        xbf3 = xbf3[:, :, 0:w]
        xc3 = xc3[:, :, 0:w]
        mm(4, PS[4][0:1, 0:w], [(ones_k.ap, xbf3[:, j, :]) for j in range(DT)], (xbf.keys, ones_k.keys))
        mm(5, PS[5][0:1, 0:w], [(ones_k.ap, xc3[:, j, :]) for j in range(DT)], (xc.keys, ones_k.keys))
        op("dve", lambda e: e.tensor_scalar(rowA[0:1, 0:w], PS[4][0:1, 0:w], 1.0 / D, None, OP.mult),
           r=(pkk(4),), w=(rowA.keys,), tiny=True)
        op("dve", lambda e: e.tensor_scalar(rowB[0:1, 0:w], PS[5][0:1, 0:w], 1.0 / D, EPS, OP.mult, OP.add),
           r=(pkk(5),), w=(rowB.keys,), tiny=True)
        op("dve", lambda e: e.tensor_tensor(rowC[0:1, 0:w], rowA[0:1, 0:w], rowA[0:1, 0:w], OP.mult),
           r=(rowA.keys,), w=(rowC.keys,), tiny=True)
        op("dve", lambda e: e.tensor_tensor(rowD[0:1, 0:w], rowB[0:1, 0:w], rowC[0:1, 0:w], OP.subtract),
           r=(rowB.keys, rowC.keys), w=(rowD.keys,), tiny=True)
        op("act", lambda e: e.activation(rowC[0:1, 0:w], rowD[0:1, 0:w], AF.Sqrt),
           r=(rowD.keys,), w=(rowC.keys,), tiny=True)
        op("dve", lambda e: e.reciprocal(rowE[0:1, 0:w], rowC[0:1, 0:w]),
           r=(rowC.keys,), w=(rowE.keys,), tiny=True)
        op("dve", lambda e: e.tensor_tensor(rowD[0:1, 0:w], rowA[0:1, 0:w], rowE[0:1, 0:w], OP.mult),
           r=(rowA.keys, rowE.keys), w=(rowD.keys,), tiny=True)
        op("act", lambda e: e.copy(rhiA[0:1, 0:w], rowE[0:1, 0:w]), r=(rowE.keys,), w=(rhiA.keys,), tiny=True)
        op("dve", lambda e: e.tensor_tensor(rloA[0:1, 0:w], rowE[0:1, 0:w], rhiA[0:1, 0:w], OP.subtract),
           r=(rowE.keys, rhiA.keys), w=(rloA.keys,), tiny=True)
        op("act", lambda e: e.copy(rhiB[0:1, 0:w], rowD[0:1, 0:w]), r=(rowD.keys,), w=(rhiB.keys,), tiny=True)
        op("dve", lambda e: e.tensor_tensor(rloB[0:1, 0:w], rowD[0:1, 0:w], rhiB[0:1, 0:w], OP.subtract),
           r=(rowD.keys, rhiB.keys), w=(rloB.keys,), tiny=True)
        mm(6, PS[6][:, 0:w], [(ones_r.ap, rhiA[0:1, 0:w]), (ones_r.ap, rloA[0:1, 0:w])],
           (rhiA.keys, rloA.keys, ones_r.keys))
        mm(7, PS[7][:, 0:w], [(ones_r.ap, rhiB[0:1, 0:w]), (ones_r.ap, rloB[0:1, 0:w])],
           (rhiB.keys, rloB.keys, ones_r.keys))
        if rows2 is not None:
            rstd_t, mr_t = rows2
            op("act", lambda e: e.copy(rstd_t.t[:, 0:w], PS[6][:, 0:w]),
               r=(pkk(6),), w=(rstd_t.keys,))
            op("act", lambda e: e.copy(mr_t.t[:, 0:w], PS[7][:, 0:w]),
               r=(pkk(7),), w=(mr_t.keys,))
            for j in range(DT):
                op("pool", lambda e, j=j: e.tensor_tensor(tmp3[:, j, :], src3[:, j, :], rstd_t.t[:, 0:w], OP.mult),
                   r=(src_keys, rstd_t.keys), w=(tmp_keys,))
            for j in range(DT):
                op("dve", lambda e, j=j: e.tensor_tensor(dst3[:, j, :], tmp3[:, j, :], mr_t.t[:, 0:w], OP.subtract),
                   r=(tmp_keys, mr_t.keys), w=(dst_keys,))
        else:
            for j in range(DT):
                op("dve", lambda e, j=j: e.tensor_tensor(tmp3[:, j, :], src3[:, j, :], PS[6][:, 0:w], OP.mult),
                   r=(src_keys, pkk(6)), w=(tmp_keys,))
            for j in range(DT):
                op("dve", lambda e, j=j: e.tensor_tensor(dst3[:, j, :], tmp3[:, j, :], PS[7][:, 0:w], OP.subtract),
                   r=(tmp_keys, pkk(7)), w=(dst_keys,))
        if bf_out is not None:
            dst2 = dstT.t[:, :, :].rearrange("p j t -> p (j t)")
            op("act", lambda e: e.copy(bf_out.ap, dst2), r=(dst_keys,), w=(bf_out.keys,))
        if lo is not None:
            bf3 = bf_out.t[:, :].rearrange("p (j t) -> p j t", j=DT)
            lo3 = lo.t[:, :].rearrange("p (j t) -> p j t", j=DT)
            for j in range(DT):
                op("dve", lambda e, j=j: e.tensor_tensor(lo3[:, j, 0:w], dst3[:, j, :], bf3[:, j, 0:w], OP.subtract),
                   r=(dst_keys, bf_out.keys), w=(lo.keys,))

    def emit_gating(t0, w, xbf3, xbf_keys, xlo3, xlo_keys, gsc):
        wgehi3 = wge_hi.t[:, :, :]
        wgelo3 = wge_lo.t[:, :, :]
        passes = [(wgehi3[:, j, :], xbf3[:, j, :]) for j in range(DT)]
        passes += [(wgehi3[:, j, :], xlo3[:, j, :]) for j in range(DT)]
        passes += [(wgelo3[:, j, :], xbf3[:, j, :]) for j in range(DT)]
        mm(4, PS[4][0:10, 0:w], passes, (wge_hi.keys, wge_lo.keys, xbf_keys, xlo_keys))
        op("dve", lambda e: e.tensor_scalar(glog[0:10, 0:w], PS[4][0:10, 0:w], gbias[0:10, 0:1], None, OP.add),
           r=(pkk(4), gbias.keys), w=(glog.keys,), tiny=True)
        nsub = (w + 127) // 128
        subs = []
        for s in range(nsub):
            tw = min(128, w - 128 * s)
            subs.append((s, tw, slice(128 * s, 128 * s + tw)) + tuple(gsc[s]))

        def step(emitter):
            for args in subs:
                emitter(*args)

        for (s, tw, sl, gt, gco, gm, gex_, gpx_, gz_, wtm) in subs:
            op("pe", lambda e, sl=sl, tw=tw: e.transpose(PS[5][0:tw, 0:10], glog[0:10, sl], idn[0:10, 0:10]),
               r=(glog.keys, idn.keys), w=(pkk(5),))
            op("dve", lambda e, tw=tw, gt=gt: e.tensor_copy(gt[0:tw, 0:10], PS[5][0:tw, 0:10]),
               r=(pkk(5),), w=(gt.keys,), tiny=True)
        step(lambda s, tw, sl, gt, gco, gm, gex_, gpx_, gz_, wtm:
             op("dve", lambda e, tw=tw, gt=gt, gco=gco: e.tensor_tensor(gco[0:tw, 0:1], gt[0:tw, 0:1], gt[0:tw, 1:2], OP.subtract),
                r=(gt.keys,), w=(gco.keys,), tiny=True))
        step(lambda s, tw, sl, gt, gco, gm, gex_, gpx_, gz_, wtm:
             op("dve", lambda e, tw=tw, gco=gco: e.scalar_tensor_tensor(gco[0:tw, 1:2], gco[0:tw, 0:1], -1.0, gco[0:tw, 0:1], OP.mult, OP.max),
                r=(gco.keys,), w=(gco.keys,), tiny=True))
        step(lambda s, tw, sl, gt, gco, gm, gex_, gpx_, gz_, wtm:
             op("act", lambda e, tw=tw, gco=gco: e.activation(gco[0:tw, 2:3], gco[0:tw, 1:2], AF.Sigmoid),
                r=(gco.keys,), w=(gco.keys,), tiny=True))
        step(lambda s, tw, sl, gt, gco, gm, gex_, gpx_, gz_, wtm:
             op("dve", lambda e, tw=tw, gco=gco: e.tensor_scalar(gco[0:tw, 4:5], gco[0:tw, 0:1], 0.0, None, OP.is_lt),
                r=(gco.keys,), w=(gco.keys,), tiny=True))
        step(lambda s, tw, sl, gt, gco, gm, gex_, gpx_, gz_, wtm:
             op("dve", lambda e, tw=tw, gco=gco, gm=gm: e.tensor_scalar(gm[0:tw, 0:4], ones8[0:tw, 0:4], gco[0:tw, 4:5], -1.0, OP.mult, OP.mult),
                r=(ones8.keys, gco.keys), w=(gm.keys,), tiny=True))
        step(lambda s, tw, sl, gt, gco, gm, gex_, gpx_, gz_, wtm:
             op("dve", lambda e, tw=tw, gco=gco, gm=gm: e.tensor_scalar(gm[0:tw, 4:8], ones8[0:tw, 0:4], gco[0:tw, 4:5], -1.0, OP.subtract, OP.mult),
                r=(ones8.keys, gco.keys), w=(gm.keys,), tiny=True))
        step(lambda s, tw, sl, gt, gco, gm, gex_, gpx_, gz_, wtm:
             op("dve", lambda e, tw=tw, gt=gt, gm=gm, gz_=gz_: e.scalar_tensor_tensor(gz_[0:tw, 0:8], gm[0:tw, 0:8], BIGM, gt[0:tw, 2:10], OP.mult, OP.add),
                r=(gm.keys, gt.keys), w=(gz_.keys,), tiny=True))
        step(lambda s, tw, sl, gt, gco, gm, gex_, gpx_, gz_, wtm:
             op("dve", lambda e, tw=tw, gco=gco, gz_=gz_: e.reduce_max(gco[0:tw, 5:6], gz_[0:tw, 0:8], AX, negate=True),
                r=(gz_.keys,), w=(gco.keys,), tiny=True))
        step(lambda s, tw, sl, gt, gco, gm, gex_, gpx_, gz_, wtm:
             op("act", lambda e, tw=tw, gco=gco, gex_=gex_, gz_=gz_: e.activation(gex_[0:tw, 0:8], gz_[0:tw, 0:8], AF.Exp, bias=gco[0:tw, 5:6]),
                r=(gz_.keys, gco.keys), w=(gex_.keys,), tiny=True))
        step(lambda s, tw, sl, gt, gco, gm, gex_, gpx_, gz_, wtm:
             op("dve", lambda e, tw=tw, gco=gco, gex_=gex_: e.reduce_sum(gco[0:tw, 6:7], gex_[0:tw, 0:8], AX),
                r=(gex_.keys,), w=(gco.keys,), tiny=True))
        step(lambda s, tw, sl, gt, gco, gm, gex_, gpx_, gz_, wtm:
             op("dve", lambda e, tw=tw, gco=gco: e.reciprocal(gco[0:tw, 7:8], gco[0:tw, 6:7]),
                r=(gco.keys,), w=(gco.keys,), tiny=True))
        step(lambda s, tw, sl, gt, gco, gm, gex_, gpx_, gz_, wtm:
             op("dve", lambda e, tw=tw, gco=gco: e.tensor_tensor(gco[0:tw, 4:5], gco[0:tw, 2:3], gco[0:tw, 7:8], OP.mult),
                r=(gco.keys,), w=(gco.keys,), tiny=True))
        step(lambda s, tw, sl, gt, gco, gm, gex_, gpx_, gz_, wtm:
             op("dve", lambda e, tw=tw, gco=gco, gex_=gex_, gpx_=gpx_: e.tensor_scalar(gpx_[0:tw, 0:8], gex_[0:tw, 0:8], gco[0:tw, 4:5], None, OP.mult),
                r=(gex_.keys, gco.keys), w=(gpx_.keys,), tiny=True))
        step(lambda s, tw, sl, gt, gco, gm, gex_, gpx_, gz_, wtm:
             op("dve", lambda e, tw=tw, gco=gco, gpx_=gpx_: e.reduce_max(gco[0:tw, 0:1], gpx_[0:tw, 0:8], AX),
                r=(gpx_.keys,), w=(gco.keys,), tiny=True))
        step(lambda s, tw, sl, gt, gco, gm, gex_, gpx_, gz_, wtm:
             op("dve", lambda e, tw=tw, gco=gco, gex_=gex_, gpx_=gpx_: e.tensor_scalar(gex_[0:tw, 0:8], gpx_[0:tw, 0:8], gco[0:tw, 0:1], None, OP.is_equal),
                r=(gpx_.keys, gco.keys), w=(gex_.keys,), tiny=True))
        step(lambda s, tw, sl, gt, gco, gm, gex_, gpx_, gz_, wtm:
             op("dve", lambda e, tw=tw, gex_=gex_, gpx_=gpx_, gz_=gz_: e.scalar_tensor_tensor(gz_[0:tw, 0:8], gex_[0:tw, 0:8], -BIGM, gpx_[0:tw, 0:8], OP.mult, OP.add),
                r=(gex_.keys, gpx_.keys), w=(gz_.keys,), tiny=True))
        step(lambda s, tw, sl, gt, gco, gm, gex_, gpx_, gz_, wtm:
             op("dve", lambda e, tw=tw, gco=gco, gz_=gz_: e.reduce_max(gco[0:tw, 1:2], gz_[0:tw, 0:8], AX),
                r=(gz_.keys,), w=(gco.keys,), tiny=True))
        step(lambda s, tw, sl, gt, gco, gm, gex_, gpx_, gz_, wtm:
             op("dve", lambda e, tw=tw, gco=gco, gpx_=gpx_, wtm=wtm: e.scalar_tensor_tensor(wtm[0:tw, 0:8], gpx_[0:tw, 0:8], gco[0:tw, 1:2], gpx_[0:tw, 0:8], OP.is_ge, OP.mult),
                r=(gpx_.keys, gco.keys), w=(wtm.keys,), tiny=True))
        for (s, tw, sl, gt, gco, gm, gex_, gpx_, gz_, wtm) in subs:
            op("pe", lambda e, tw=tw, wtm=wtm: e.transpose(PS[6][0:8, 0:tw], wtm[0:tw, 0:8], idn[0:tw, 0:tw]),
               r=(wtm.keys, idn.keys), w=(pkk(6),))
            op("dve", lambda e, s=s, tw=tw: e.tensor_copy(wse8[0:8, 128 * s:128 * s + tw], PS[6][0:8, 0:tw]),
               r=(pkk(6),), w=(wse8.keys,), tiny=True)
        dma("crow", wse_d[0:NE, t0:t0 + w], wse8[0:8, 0:w], r=(wse8.keys,), w=(("d", "wse", t0),))
        op("act", lambda e, w=w: e.copy(wse8h[0:8, 0:w], wse8[0:8, 0:w]),
           r=(wse8.keys,), w=(wse8h.keys,), tiny=True)
        op("dve", lambda e, w=w: e.tensor_tensor(wse8l[0:8, 0:w], wse8[0:8, 0:w], wse8h[0:8, 0:w], OP.subtract),
           r=(wse8.keys, wse8h.keys), w=(wse8l.keys,), tiny=True)
        ci_g = t0 // TC
        for p_ in range(4):
            dma("crow", wsehl_v[p_, ci_g, 0:2, 0:w], wse8h[2 * p_:2 * p_ + 2, 0:w],
                r=(wse8h.keys,), w=(("d", "wsehl", t0, p_, 0),))
            dma("crow", wsehl_v[p_, ci_g, 2:4, 0:w], wse8l[2 * p_:2 * p_ + 2, 0:w],
                r=(wse8l.keys,), w=(("d", "wsehl", t0, p_, 1),))
    def emit_swiglu(wsb, xbf3, xbf_keys, hbuf3, h_keys, w, scr):
        for i in range(HT):
            ba = (2 * i) % 4
            bb = ba + 1
            mm(ba, PS[ba][:, 0:w],
               [(wsb.t[:, kt, 128 * i:128 * (i + 1)], xbf3[:, kt, :]) for kt in range(DT)],
               (wsb.keys, xbf_keys))
            mm(bb, PS[bb][:, 0:w],
               [(wsb.t[:, kt, 128 * (HT + i):128 * (HT + i + 1)], xbf3[:, kt, :]) for kt in range(DT)],
               (wsb.keys, xbf_keys))
            op("act", lambda e, ba=ba: e.activation(scr.t[:, 0:w], PS[ba][:, 0:w], AF.Silu),
               r=(pkk(ba),), w=(scr.keys,))
            op("dve", lambda e, bb=bb, i=i: e.tensor_tensor(hbuf3[:, i, :], scr.t[:, 0:w], PS[bb][:, 0:w], OP.mult),
               r=(scr.keys, pkk(bb)), w=(h_keys,))

    grid_h = chunk_grid(HTOK)
    grid_t = chunk_grid(TOK)
    PCW = 420
    pc_grid = [(t, min(PCW, HTOK - t)) for t in range(0, HTOK, PCW)]

    dma("crow", tpar.ap, tpar_in, w=(tpar.keys,))
    dma("crow", wge_f.t[:, :, :], wge_T.rearrange("(j p) n -> p j n", p=128), w=(wge_f.keys,))
    dma("crow", gbias.ap, gb_in, w=(gbias.keys,))
    op("pool", lambda e: e.iota(tc1.ap, [[1, 128]], channel_multiplier=0,
                                allow_small_or_imprecise_dtypes=True), w=(tc1.keys,))
    op("pool", lambda e: e.iota(tc2.ap, [[0, 128]], channel_multiplier=1,
                                allow_small_or_imprecise_dtypes=True), w=(tc2.keys,))
    op("dve", lambda e: e.tensor_tensor(idn.ap, tc1.ap, tc2.ap, OP.is_equal),
       r=(tc1.keys, tc2.keys), w=(idn.keys,))
    op("dve", lambda e: e.memset(ones_k.ap, 1.0), w=(ones_k.keys,), tiny=True)
    op("dve", lambda e: e.memset(ones_r.ap, 1.0), w=(ones_r.keys,), tiny=True)
    op("dve", lambda e: e.memset(ones8.ap, 1.0), w=(ones8.keys,), tiny=True)
    op("dve", lambda e: e.memset(pcol.ap, 0.0), w=(pcol.keys,), tiny=True)
    op("dve", lambda e: e.memset(carry_a.ap, 0.0), w=(carry_a.keys,), tiny=True)
    op("dve", lambda e: e.memset(carry_b.ap, 0.0), w=(carry_b.keys,), tiny=True)
    op("act", lambda e: e.activation(etd.ap, tdec.ap, AF.Exp), r=(tdec.keys,), w=(etd.keys,), tiny=True)
    op("dve", lambda e: e.tensor_scalar(netd.ap, etd.ap, -1.0, None, OP.mult),
       r=(etd.keys,), w=(netd.keys,), tiny=True)
    op("act", lambda e: e.activation(decC.ap, etd.ap, AF.Exp, scale=-1.0),
       r=(etd.keys,), w=(decC.keys,), tiny=True)
    op("act", lambda e: e.copy(wge_hi.t[:, :, :], wge_f.t[:, :, :]), r=(wge_f.keys,), w=(wge_hi.keys,), tiny=True)
    for j in range(DT):
        op("dve", lambda e, j=j: e.tensor_tensor(wge_lo.t[:, j, :], wge_f.t[:, j, :], wge_hi.t[:, j, :], OP.subtract),
           r=(wge_f.keys, wge_hi.keys), w=(wge_lo.keys,), tiny=True)

    def wload3(sbT, src, kt, m, tag):
        dma("cw", sbT.t[:, :, :], src.rearrange("(kt p) m -> p kt m", p=128),
            w=(sbT.keys,))

    wload3(w1_sb[0], w1_T, DT, 2 * DH, "w1_0")

    def emit_PA2(call):
        srcB = x_in if call == 0 else xr2_d
        stag = "x" if call == 0 else "xr2"
        grid = grid_h if call == 0 else grid_t
        A = RegionAlloc(b, R3, R3SZ, f"a2{call}")
        xb = A.t("xb", [128, DT, TC], F32)
        tmp = A.t("tmp", [128, DT, TC], F32)
        xbf = A.t("xbf", [128, DT * TC], BF16)
        xc = A.t("xc", [128, DT * TC], BF16)
        scr = A.t("scr", [128, TC], F32)
        hb = b.sb(f"a2{call}_hb", [128, HT, TC], BF16, off=tmp.off0)
        A.off = (A.off + 255) & ~255
        gsc = [tuple(A.t(f"g{nm}{s}", [128, 64], F32)
                     for nm in ("t", "c", "m", "e", "p", "z", "w"))
               for s in range(3)]
        lnr = (A.t("rstd", [128, TC], F32), A.t("mrow", [128, TC], F32))
        for ci, (t0, w) in enumerate(grid):
            dma("cin", xb.t[:, :, :].rearrange("p j t -> p (j t)"), blk(srcB, ci),
                r=(("d", stag, t0),), w=(xb.keys,))
            emit_ln(xb, w, xbf, xc, tmp, xb, bf_out=xbf, lo=xc)
            xbf3 = xbf.t[:, :].rearrange("p (j t) -> p j t", j=DT)[:, :, 0:w]
            xc3 = xc.t[:, :].rearrange("p (j t) -> p j t", j=DT)[:, :, 0:w]
            dma("cxl", blk(xln_d, ci), xbf.ap, r=(xbf.keys,), w=(("d", "xln", t0),))
            emit_gating(t0, w, xbf3, xbf.keys, xc3, xc.keys, gsc)
            emit_swiglu(w1_sb[call], xbf3, xbf.keys, hb.t[:, :, 0:w], hb.keys, w, scr)
            dma("cin", blk(hrt_d, ci), hb.t[:, :, :].rearrange("p j t -> p (j t)"),
                r=(hb.keys,), w=(("d", "hrt", t0),))
        wload3(w_in_sb[call], w_in_T, DT, 2 * DH, f"win{call}")

    def emit_PA1(call):
        grid = grid_h if call == 0 else grid_t
        A = RegionAlloc(b, R3, R3SZ, f"a1{call}")
        xlnb = A.t("xlnb", [128, DT, TC], BF16)
        xlnb2 = A.t("xlnb2", [128, DT, TC], BF16)
        hb = A.t("hb", [128, HT, TC], BF16)
        scr = A.t("scr", [128, TC], F32)
        pob = A.t("pob", [128, DT, TC], F32)
        wload3(w_out_sb[call], w_out_T, HT, D, f"wo{call}")
        bufs = (xlnb, xlnb2)
        prev_key = None
        for ci, (t0, w) in enumerate(grid):
            xl = bufs[ci % 2]
            rk = (("d", "xln", t0),) if (prev_key is None or not SERIAL_PA1[0]) else (("d", "xln", t0), prev_key)
            dma("cxl", xl.t[:, :, :].rearrange("p j t -> p (j t)"), blk(xln_d, ci), r=rk, w=(xl.keys,))
            prev_key = ("d", "pA", t0)
            xbf3 = xl.t[:, :, 0:w]
            emit_swiglu(w_in_sb[call], xbf3, xl.keys, hb.t[:, :, 0:w], hb.keys, w, scr)
            if call == 0 and DBG_HSH[0]:
                dma("cout", hsh_d.rearrange("(j p) t -> p j t", p=128)[:, :, t0:t0 + w],
                    hb.t[:, :, 0:w], r=(hb.keys,), w=(("d", "hsh", t0),))
            for m in range(DT):
                bank = m % 4
                mm(bank, PS[bank][:, 0:w],
                   [(w_out_sb[call].t[:, kt, 128 * m:128 * (m + 1)], hb.t[:, kt, 0:w]) for kt in range(HT)],
                   (w_out_sb[call].keys, hb.keys))
                op("dve", lambda e, bank=bank, m=m, w=w: e.tensor_copy(pob.t[:, m, 0:w], PS[bank][:, 0:w]),
                   r=(pkk(bank),), w=(pob.keys,))
            dma("cin", blk(pA_d, ci), pob.t[:, :, :].rearrange("p j t -> p (j t)"),
                r=(pob.keys,), w=(("d", "pA", t0),))

    HRBP = (HT * TC * 2 + 255) & ~255
    pb_hrtb = [b.sb(f"pb_hrt{i}", [128, HT, TC], BF16, off=R3 + i * HRBP) for i in range(2)]
    he0b = b.sb("pb_he0", [128, HT, TC], BF16, off=R3 + 2 * HRBP)
    he1b = b.sb("pb_he1", [128, HT, TC], BF16, off=R3 + 3 * HRBP)
    pb_xq = [b.sb(f"pb_xq{i}", [128, TC], F32, off=R3 + 4 * HRBP + i * 1536) for i in range(2)]
    assert R3 + 4 * HRBP + 2 * 1536 <= SB_LIMIT

    def emit_PB_call(call):
        grid = grid_h if call == 0 else grid_t
        stag = "x" if call == 0 else "xr2"
        xsrcB = x_in if call == 0 else xr2_d
        n = len(grid)
        steps = [(q, ci) for q in range(4) for ci in range(n)]
        N = len(steps)

        def qinfo(q):
            pin_b = (pA_d, pB_d, pA_d, pB_d)[q]
            pint = ("pA", "pB", "pA", "pB")[q]
            if q < 3:
                pout_b = (pB_d, pA_d, pB_d)[q]
                poutt = ("pB", "pA", "pB")[q]
            else:
                pout_b = xr1vw if call == 0 else outvw
                poutt = "xr1" if call == 0 else "out"
            return pin_b, pint, pout_b, poutt

        def load_hrt(s):
            q, ci = steps[s]
            t0, w = grid[ci]
            dma("cin", pb_hrtb[s % 2].t[:, :, :].rearrange("p j t -> p (j t)"), blk(hrt_d, ci),
                r=(("d", "hrt", t0),), w=(pb_hrtb[s % 2].keys,))

        def rows_and_bcast(s):
            q, ci = steps[s]
            t0, w = grid[ci]
            p = s % 2
            row = q * NCH + ci
            dma("crow", pb_rows4.t[0:1, :, :].rearrange("p e t -> p (e t)"),
                wsehl_d[row:row + 1, :],
                r=(("d", "wsehl", t0, q, 0), ("d", "wsehl", t0, q, 1)),
                w=(pb_rows4.keys,))
            for ei in range(2):
                bank = 6 + ei
                mm(bank, PS[bank][:, 0:w],
                   [(ones_r.ap, pb_rows4.t[0:1, ei, 0:w]), (ones_r.ap, pb_rows4.t[0:1, 2 + ei, 0:w])],
                   (pb_rows4.keys, ones_r.keys))
                op("act", lambda e, bank=bank, p=p, ei=ei, w=w:
                   e.copy(pb_wb[p][ei].t[:, 0:w], PS[bank][:, 0:w]),
                   r=(pkk(bank),), w=(pb_wb[p][ei].keys,))

        def emit_he(s, ei):
            q, ci = steps[s]
            t0, w = grid[ci]
            p = s % 2
            heX = (he0b, he1b)[ei]
            hr = pb_hrtb[p]
            wbX = pb_wb[p][ei]
            eng = "dve" if ei == 0 else "pool"
            for ht in range(HT):
                op(eng, lambda e, ht=ht, hr=hr, wbX=wbX, heX=heX, w=w:
                   e.tensor_tensor(heX.t[:, ht, 0:w], hr.t[:, ht, 0:w], wbX.t[:, 0:w], OP.mult),
                   r=(hr.keys, wbX.keys), w=(heX.keys,))

        load_hrt(0)
        rows_and_bcast(0)
        emit_he(0, 0)
        emit_he(0, 1)

        for s in range(N):
            q, ci = steps[s]
            t0, w = grid[ci]
            pin_b, pint, pout_b, poutt = qinfo(q)
            wq = w2_sb[(call, q)]
            if ci == 0:
                if q < 3:
                    nq = w2_sb[(call, q + 1)]
                    dma("cw", nq.t[:, :, :, :],
                        w2_T[(2 * q + 2) * DH:(2 * q + 4) * DH, :].rearrange(
                            "(e kt p) m -> p e kt m", p=128, kt=HT),
                        w=(nq.keys,))
                elif call == 0:
                    dma("cw", wk_sb.t[:, :, :], wk_T.rearrange("(kt p) m -> p kt m", p=128), w=(wk_sb.keys,))
                    dma("cw", wv_sb.t[:, :, :], wv_T.rearrange("(kt p) m -> p kt m", p=128), w=(wv_sb.keys,))
                    dma("cw", wr_sb.t[:, :, :], wr_T.rearrange("(kt p) m -> p kt m", p=128), w=(wr_sb.keys,))
            if s + 1 < N:
                load_hrt(s + 1)
                rows_and_bcast(s + 1)
            for m in range(DT):
                if m < 2:
                    dma("cout", pb_pin[m].t[:, 0:w], blkm(pin_b, ci, m, w),
                        r=(("d", pint, t0),), w=(pb_pin[m].keys,))
                mm(m, PS[m][:, 0:w],
                   [(wq.t[:, 0, kt, 128 * m:128 * (m + 1)], he0b.t[:, kt, 0:w]) for kt in range(HT)],
                   (wq.keys, he0b.keys), start=True, stop=False)
            if s + 1 < N:
                emit_he(s + 1, 0)
            for m in range(DT):
                if q == 3:
                    xqX = pb_xq[m % 2]
                    dma("cout", xqX.t[:, 0:w], blkm(xsrcB, ci, m, w),
                        r=(("d", stag, t0),), w=(xqX.keys,))
                mm(m, PS[m][:, 0:w],
                   [(wq.t[:, 1, kt, 128 * m:128 * (m + 1)], he1b.t[:, kt, 0:w]) for kt in range(HT)],
                   (wq.keys, he1b.keys), start=False, stop=True)
                pinX = pb_pin[m % 2]
                poX = pb_po[m % 2]
                if q < 3:
                    op("dve", lambda e, m=m, pinX=pinX, poX=poX, w=w:
                       e.scalar_tensor_tensor(poX.t[:, 0:w], PS[m][:, 0:w], 1.0, pinX.t[:, 0:w], OP.mult, OP.add),
                       r=(pkk(m), pinX.keys), w=(poX.keys,))
                else:
                    xqX = pb_xq[m % 2]
                    op("dve", lambda e, m=m, xqX=xqX, poX=poX, w=w:
                       e.scalar_tensor_tensor(poX.t[:, 0:w], PS[m][:, 0:w], 0.5, xqX.t[:, 0:w], OP.mult, OP.add),
                       r=(pkk(m), xqX.keys), w=(poX.keys,))
                    op("dve", lambda e, pinX=pinX, poX=poX, xqX=xqX, w=w:
                       e.scalar_tensor_tensor(xqX.t[:, 0:w], pinX.t[:, 0:w], 0.5, poX.t[:, 0:w], OP.mult, OP.add),
                       r=(pinX.keys, poX.keys), w=(xqX.keys,))
                    poX = xqX
                if q < 3:
                    dst = blkm(pout_b, ci, m, w)
                else:
                    dst = pout_b[:, m, t0:t0 + w]
                dma("cout", dst, poX.t[:, 0:w], r=(poX.keys,), w=(("d", poutt, t0),))
                if m + 2 < DT:
                    dma("cout", pb_pin[m % 2].t[:, 0:w], blkm(pin_b, ci, m + 2, w),
                        r=(("d", pint, t0),), w=(pb_pin[m % 2].keys,))
            if s + 1 < N:
                emit_he(s + 1, 1)

    def emit_q0_load(call):
        wq = w2_sb[(call, 0)]
        dma("cw", wq.t[:, :, :, :],
            w2_T[0:2 * DH, :].rearrange("(e kt p) m -> p e kt m", p=128, kt=HT),
            w=(wq.keys,))

    def emit_PC():
        A = RegionAlloc(b, R3, R3SZ, "pc")
        xb2 = [A.t(f"xb{i}", [128, DT, PCW], F32) for i in range(2)]
        tmp = A.t("tmp", [128, DT, PCW], F32)
        xbf = A.t("xbf", [128, DT * PCW], BF16)
        xc = A.t("xc", [128, DT * PCW], BF16)
        xprev = b.sb("pc_xprev", [128, DT, PCW], F32, off=R1)
        mixA = b.sb("pc_mixA", [128, DT, PCW], BF16, off=R1 + 14336)
        mixB = b.sb("pc_mixB", [128, DT, PCW], BF16, off=R1 + 21504)
        evacs = [b.sb(f"pc_evac{i}", [128, DT, PCW], F32, off=R1 + 28672 + i * 14336)
                 for i in range(2)]
        lnr = (b.sb("pc_rstd", [128, PCW], F32, off=R1 + 57344),
               b.sb("pc_mrow", [128, PCW], F32, off=R1 + 59392))
        wsbs = (wk_sb, wv_sb, wr_sb)
        dsts = (kvw, vvw, rsvw)
        dtags = ("k", "v", "rs")
        xr1_all = tuple(("d", "xr1", t0) for (t0, _w) in grid_h)
        for ci, (t0, w) in enumerate(pc_grid):
            xb = xb2[ci % 2]
            rk = tuple(k for k, (kt0, kw) in zip(xr1_all, grid_h)
                       if kt0 < t0 + w and kt0 + kw > t0)
            dma("cin", xb[:, :, 0:w], xr1vw[:, :, t0:t0 + w], r=rk, w=(xb.keys,))
            emit_ln(xb, w, xbf, xc, tmp, xb)
            for j in range(DT):
                op("pool", lambda e, j=j, w=w, xb=xb: e.tensor_copy(xprev.t[:, j, 1:w], xb.t[:, j, 0:w - 1]),
                   r=(xb.keys,), w=(xprev.keys,))
            pcol3 = pcol.t[:, :].rearrange("p (j t) -> p j t", t=1)
            op("dve", lambda e, pcol3=pcol3: e.tensor_copy(xprev.t[:, :, 0:1], pcol3),
               r=(pcol.keys,), w=(xprev.keys,), tiny=True)
            op("dve", lambda e, w=w, pcol3=pcol3, xb=xb: e.tensor_copy(pcol3, xb.t[:, :, w - 1:w]),
               r=(xb.keys,), w=(pcol.keys,), tiny=True)
            for j in range(DT):
                op("dve", lambda e, j=j, w=w, xb=xb: e.tensor_tensor(tmp.t[:, j, 0:w], xb.t[:, j, 0:w], xprev.t[:, j, 0:w], OP.subtract),
                   r=(xb.keys, xprev.keys), w=(tmp.keys,))
            if t0 == 0:
                s0, c0, cw = 1, 0, w - 1
            else:
                s0, c0, cw = 0, t0 - 1, w
            for i in range(3):
                mx = (mixA, mixB)[i % 2]
                evac = evacs[i % 2]
                for j in range(DT):
                    op("dve", lambda e, i=i, j=j, mx=mx, w=w: e.scalar_tensor_tensor(
                        mx.t[:, j, 0:w], tmp.t[:, j, 0:w], tmix[:, 8 * i + j:8 * i + j + 1],
                        xprev.t[:, j, 0:w], OP.mult, OP.add),
                       r=(tmp.keys, xprev.keys, tmix.keys), w=(mx.keys,))
                for m in range(DT):
                    bank = m % 4
                    mm(bank, PS[bank][:, 0:w],
                       [(wsbs[i].t[:, kt, 128 * m:128 * (m + 1)], mx.t[:, kt, 0:w]) for kt in range(DT)],
                       (wsbs[i].keys, mx.keys))
                    if i == 0:
                        op("act", lambda e, bank=bank, m=m, w=w, evac=evac: e.copy(evac.t[:, m, 0:w], PS[bank][:, 0:w]),
                           r=(pkk(bank),), w=(evac.keys,))
                    elif i == 1:
                        op("dve", lambda e, bank=bank, m=m, w=w, evac=evac: e.tensor_copy(evac.t[:, m, 0:w], PS[bank][:, 0:w]),
                           r=(pkk(bank),), w=(evac.keys,))
                    else:
                        op("act", lambda e, bank=bank, m=m, w=w, evac=evac: e.activation(evac.t[:, m, 0:w], PS[bank][:, 0:w], AF.Sigmoid),
                           r=(pkk(bank),), w=(evac.keys,))
                if i < 2:
                    dma(("cpk", "cpv")[i], dsts[i][:, :, c0:c0 + cw], evac.t[:, :, s0:s0 + cw],
                        r=(evac.keys,), w=(("d", dtags[i], t0),))
                else:
                    dma("cpr", dsts[i][:, 0:4, c0:c0 + cw], evac.t[:, 0:4, s0:s0 + cw],
                        r=(evac.keys,), w=(("d", "rs", t0, 0),))
                    dma("cpr", dsts[i][:, 4:8, c0:c0 + cw], evac.t[:, 4:8, s0:s0 + cw],
                        r=(evac.keys,), w=(("d", "rs", t0, 1),))
        return

    def pc_cov(q0, tag, rs=False):
        out = []
        for (t0, wq_) in pc_grid:
            c0 = 0 if t0 == 0 else t0 - 1
            cw_ = wq_ - 1 if t0 == 0 else wq_
            if c0 < q0 + QW and c0 + cw_ > q0:
                out.append(t0)
        if rs:
            return tuple(("d", tag, t0, h) for t0 in out for h in (0, 1))
        return tuple(("d", tag, t0) for t0 in out)

    def emit_PD():
        A = RegionAlloc(b, R3, WO_OFF - R3, "pd")
        kt_ = A.t("kt", [128, QW], F32)
        vt_ = A.t("vt", [128, QW], F32)
        ekt = A.t("ekt", [128, QW], F32)
        pt_ = A.t("pt", [128, QW], F32)
        ones1k = A.t("ones1k", [128, QW], F32)
        decb = A.t("decb", [128, QW], F32)
        abuf = A.t("abuf", [128, QW], F32)
        bbuf = A.t("bbuf", [128, QW], F32)
        dma("cw", wo_sb.t[:, :, :], wo_T.rearrange("(kt p) m -> p kt m", p=128), w=(wo_sb.keys,))
        op("dve", lambda e: e.memset(ones1k.ap, 1.0), w=(ones1k.keys,), tiny=True)

        for qi in range(TOK // QW):
            q0 = qi * QW
            kkeys = pc_cov(q0, "k")
            vkeys = pc_cov(q0, "v")
            for j in range(DT):
                op("dve", lambda e, j=j: e.tensor_scalar(decb.ap, ones1k.ap, decC[:, j:j + 1], None, OP.mult),
                   r=(ones1k.keys, decC.keys), w=(decb.keys,))
                if DBG_PD[0] and qi == 0:
                    dma("cout", dcb_d[128 * j:128 * (j + 1), :], decb.ap, r=(decb.keys,), w=(("d", "dcb", j),))
                dma("csc", kt_.ap, k_d[128 * j:128 * (j + 1), q0:q0 + QW], r=kkeys, w=(kt_.keys,))
                dma("csc", vt_.ap, v_d[128 * j:128 * (j + 1), q0:q0 + QW], r=vkeys, w=(vt_.keys,))
                op("act", lambda e: e.activation(ekt.ap, kt_.ap, AF.Exp), r=(kt_.keys,), w=(ekt.keys,))
                op("pool", lambda e: e.tensor_tensor(pt_.ap, ekt.ap, vt_.ap, OP.mult),
                   r=(ekt.keys, vt_.keys), w=(pt_.keys,))
                if DBG_PD[0] and qi == 0:
                    dma("cout", pt_d[128 * j:128 * (j + 1), :], pt_.ap, r=(pt_.keys,), w=(("d", "ptd", j),))
                op("dve", lambda e, j=j: e.tensor_tensor_scan(abuf.ap, decb.ap, pt_.ap, carry_a[:, j:j + 1], OP.mult, OP.add),
                   r=(decb.keys, pt_.keys, carry_a.keys), w=(abuf.keys,))
                op("dve", lambda e, j=j: e.tensor_tensor_scan(bbuf.ap, decb.ap, ekt.ap, carry_b[:, j:j + 1], OP.mult, OP.add),
                   r=(decb.keys, ekt.keys, carry_b.keys), w=(bbuf.keys,))
                op("dve", lambda e, j=j: e.tensor_copy(carry_a[:, j:j + 1], abuf[:, QW - 1:QW]),
                   r=(abuf.keys,), w=(carry_a.keys,), tiny=True)
                op("dve", lambda e, j=j: e.tensor_copy(carry_b[:, j:j + 1], bbuf[:, QW - 1:QW]),
                   r=(bbuf.keys,), w=(carry_b.keys,), tiny=True)
                dma("cscw", as_d[128 * j:128 * (j + 1), q0:q0 + QW], abuf.ap,
                    r=(abuf.keys,), w=(("d", "as", j, qi),))
                dma("cscw", bs_d[128 * j:128 * (j + 1), q0:q0 + QW], bbuf.ap,
                    r=(bbuf.keys,), w=(("d", "bs", j, qi),))
        op("dve", lambda e: e.tensor_copy(stage[:, 0:8], carry_a.ap), r=(carry_a.keys,), w=(stage.keys,), tiny=True)
        op("dve", lambda e: e.tensor_copy(stage[:, 8:16], carry_b.ap), r=(carry_b.keys,), w=(stage.keys,), tiny=True)
        dma("crow", ag_in_d, stage.ap, r=(stage.keys,), w=(("d", "agin"),))
        b.emit("pool", lambda e: e.collective_compute(
            "AllGather", OP.bypass, replica_groups=[list(range(8))],
            ins=[ag_in_d.opt()], outs=[ag_out_d.opt()]),
            r=(("d", "agin"),), w=(("d", "agout"),), coll=True)
        dma("crow", gsb.t[:, :, :], ag_out_d.rearrange("(r p) c -> p r c", p=128),
            r=(("d", "agout"),), w=(gsb.keys,))
        op("dve", lambda e: e.tensor_scalar(SAB2.ap, gsb.t[:, 0, :], selv[:, 0:1], None, OP.mult),
           r=(gsb.keys, selv.keys), w=(SAB2.keys,), tiny=True)
        for rr in range(1, 8):
            src_, dst_ = (SAB2, SAB1) if rr % 2 == 1 else (SAB1, SAB2)
            op("dve", lambda e, rr=rr, src_=src_, dst_=dst_: e.scalar_tensor_tensor(
                dst_.ap, gsb.t[:, rr, :], selv[:, rr:rr + 1], src_.ap, OP.mult, OP.add),
               r=(gsb.keys, selv.keys, src_.keys), w=(dst_.keys,), tiny=True)

    def emit_PE():
        names = ("kt", "vt", "rt", "ap", "bp", "dpw", "eku", "tq")
        PEB = [{nm: b.sb(f"pe_{nm}{p}", [128, QW], F32, off=R2 + i * 4096 + p * 32768)
                for i, nm in enumerate(names)} for p in (0, 1)]
        iqs = [b.sb(f"pe_iq{p}", [128, QW], F32, off=R3 + p * 4096) for p in (0, 1)]
        for qi in range(TOK // QW):
            q0 = qi * QW
            kkey = pc_cov(q0, "k")
            vkey = pc_cov(q0, "v")
            rkey = pc_cov(q0, "rs", rs=True)
            iq = iqs[qi % 2]
            op("pool", lambda e, q0=q0, iq=iq: e.iota(iq.ap, [[1, QW]], base=q0, channel_multiplier=0,
                                                      allow_small_or_imprecise_dtypes=True), w=(iq.keys,))
            for j in range(DT):
                P = PEB[(qi * DT + j) % 2]
                kt_, vt_, rt_ = P["kt"], P["vt"], P["rt"]
                ap_, bp_, dpw, eku, tq = P["ap"], P["bp"], P["dpw"], P["eku"], P["tq"]
                akey = (("d", "as", j, qi), ("d", "as", j, max(0, qi - 1)))
                bkey = (("d", "bs", j, qi), ("d", "bs", j, max(0, qi - 1)))
                dma("csc", kt_.ap, k_d[128 * j:128 * (j + 1), q0:q0 + QW], r=kkey, w=(kt_.keys,))
                dma("csc", vt_.ap, v_d[128 * j:128 * (j + 1), q0:q0 + QW], r=vkey, w=(vt_.keys,))
                dma("csc", rt_.ap, rs_d[128 * j:128 * (j + 1), q0:q0 + QW], r=rkey, w=(rt_.keys,))
                if q0 == 0:
                    op("dve", lambda e, ap_=ap_: e.memset(ap_[:, 0:1], 0.0), w=(ap_.keys,), tiny=True)
                    op("dve", lambda e, bp_=bp_: e.memset(bp_[:, 0:1], 0.0), w=(bp_.keys,), tiny=True)
                    dma("csc", ap_[:, 1:QW], as_d[128 * j:128 * (j + 1), 0:QW - 1], r=akey, w=(ap_.keys,))
                    dma("csc", bp_[:, 1:QW], bs_d[128 * j:128 * (j + 1), 0:QW - 1], r=bkey, w=(bp_.keys,))
                else:
                    dma("csc", ap_.ap, as_d[128 * j:128 * (j + 1), q0 - 1:q0 + QW - 1], r=akey, w=(ap_.keys,))
                    dma("csc", bp_.ap, bs_d[128 * j:128 * (j + 1), q0 - 1:q0 + QW - 1], r=bkey, w=(bp_.keys,))
                op("dve", lambda e, j=j, tq=tq, iq=iq: e.tensor_scalar(tq.ap, iq.ap, netd[:, j:j + 1], None, OP.mult),
                   r=(iq.keys, netd.keys), w=(tq.keys,))
                op("act", lambda e, dpw=dpw, tq=tq: e.activation(dpw.ap, tq.ap, AF.Exp), r=(tq.keys,), w=(dpw.keys,))
                op("act", lambda e, j=j, eku=eku, kt_=kt_: e.activation(eku.ap, kt_.ap, AF.Exp, bias=tfir[:, j:j + 1]),
                   r=(kt_.keys, tfir.keys), w=(eku.keys,))
                op("pool", lambda e, tq=tq, eku=eku, vt_=vt_: e.tensor_tensor(tq.ap, eku.ap, vt_.ap, OP.mult),
                   r=(eku.keys, vt_.keys), w=(tq.keys,))
                op("dve", lambda e, j=j, kt_=kt_, dpw=dpw, ap_=ap_: e.scalar_tensor_tensor(kt_.ap, dpw.ap, SAB1[:, j:j + 1], ap_.ap, OP.mult, OP.add),
                   r=(dpw.keys, SAB1.keys, ap_.keys), w=(kt_.keys,))
                op("dve", lambda e, j=j, vt_=vt_, dpw=dpw, bp_=bp_: e.scalar_tensor_tensor(vt_.ap, dpw.ap, SAB1[:, 8 + j:8 + j + 1], bp_.ap, OP.mult, OP.add),
                   r=(dpw.keys, SAB1.keys, bp_.keys), w=(vt_.keys,))
                op("pool", lambda e, ap_=ap_, kt_=kt_, tq=tq: e.tensor_tensor(ap_.ap, kt_.ap, tq.ap, OP.add),
                   r=(kt_.keys, tq.keys), w=(ap_.keys,))
                op("pool", lambda e, bp_=bp_, vt_=vt_, eku=eku: e.tensor_tensor(bp_.ap, vt_.ap, eku.ap, OP.add),
                   r=(vt_.keys, eku.keys), w=(bp_.keys,))
                op("dve", lambda e, dpw=dpw, bp_=bp_: e.tensor_scalar(dpw.ap, bp_.ap, 1e-6, None, OP.max),
                   r=(bp_.keys,), w=(dpw.keys,))
                op("dve", lambda e, tq=tq, dpw=dpw: e.reciprocal(tq.ap, dpw.ap), r=(dpw.keys,), w=(tq.keys,))
                op("pool", lambda e, kt_=kt_, ap_=ap_, tq=tq: e.tensor_tensor(kt_.ap, ap_.ap, tq.ap, OP.mult),
                   r=(ap_.keys, tq.keys), w=(kt_.keys,))
                op("dve", lambda e, j=j, q0=q0, kt_=kt_, rt_=rt_: e.tensor_tensor(rwb.t[:, j, q0:q0 + QW], kt_.ap, rt_.ap, OP.mult),
                   r=(kt_.keys, rt_.keys), w=(rwb.keys,))


    def emit_PF():
        A = RegionAlloc(b, R3, WO_OFF - R3, "pf")
        xbs = [A.t(f"xb{i}", [128, DT, TC], F32) for i in range(2)]
        ob = A.t("ob", [128, DT, TC], F32)
        xr1keys = tuple(("d", "xr1", t0) for (t0, _w) in grid_h)
        for ci, (t0, w) in enumerate(grid_t):
            xb = xbs[ci % 2]
            dma("cxb", xb[:, 0:4, 0:w], xr1vw[:, 0:4, t0 + 1:t0 + 1 + w],
                r=xr1keys, w=(xb.keys,))
            dma("cxb", xb[:, 4:8, 0:w], xr1vw[:, 4:8, t0 + 1:t0 + 1 + w],
                r=xr1keys, w=(xb.keys,))
            for m in range(DT):
                bank = m % 4
                mm(bank, PS[bank][:, 0:w],
                   [(wo_sb.t[:, kt, 128 * m:128 * (m + 1)], rwb.t[:, kt, t0:t0 + w]) for kt in range(DT)],
                   (wo_sb.keys, rwb.keys))
                op("dve", lambda e, bank=bank, m=m, w=w, xb=xb: e.tensor_tensor(ob.t[:, m, 0:w], PS[bank][:, 0:w], xb.t[:, m, 0:w], OP.add),
                   r=(pkk(bank), xb.keys), w=(ob.keys,))
            dma("cin", blk(xr2_d, ci), ob.t[:, :, :].rearrange("p j t -> p (j t)"),
                r=(ob.keys,), w=(("d", "xr2", t0),))
        wload3(w1_sb[1], w1_T, DT, 2 * DH, "w1_1")

    allph = ["A2", "A1", "B", "C", "D", "E", "F", "A2b", "A1b", "Bb"]
    ph = allph if phases is None else phases
    if "A2" in ph:
        emit_PA2(0)
    if "A1" in ph:
        emit_PA1(0)
    if "B" in ph:
        emit_q0_load(0)
        emit_PB_call(0)
    if "C" in ph:
        emit_PC()
    if "D" in ph:
        emit_PD()
    if "E" in ph:
        emit_PE()
    if "F" in ph:
        emit_PF()
    if "A2b" in ph:
        emit_PA2(1)
    if "A1b" in ph:
        emit_PA1(1)
    if "Bb" in ph:
        emit_q0_load(1)
        emit_PB_call(1)

    dbg_src = {"xln1": (xln_d, "xln", grid_h), "wse": (wse_d, "wse", grid_h),
               "hsh": (hsh_d, "hsh", grid_h), "pB": (pB_d, "pB", grid_h),
               "as": (as_d, "as", None), "bs": (bs_d, "bs", None),
               "dcb": (dcb_d, "dcb", None), "ptd": (pt_d, "ptd", None),
               "pA": (pA_d, "pA", grid_h), "xr1": (xr1_d, "xr1", grid_h),
               "hrt": (hrt_d, "hrt", grid_h),
               "k": (k_d, "k", pc_grid), "v": (v_d, "v", pc_grid), "rs": (rs_d, "rs", pc_grid),
               "xr2": (xr2_d, "xr2", grid_t)}
    if "rwb" in debug:
        dd = nc.dram_tensor("dbg_rwb", [128, DT * TOK], BF16, kind="ExternalOutput").ap()
        dma("cout", dd, rwb.t[:, :, :].rearrange("p j t -> p (j t)"), r=(rwb.keys,), w=())
    for nm in debug:
        if nm not in dbg_src:
            continue
        src, tag, grid = dbg_src[nm]
        shp = list(src.tensor.shape) if hasattr(src, "tensor") else None
        dt = BF16 if nm in ("xln1", "hrt", "hsh") else F32
        dd = nc.dram_tensor("dbg_" + nm, [src.shape[0], src.shape[1]], dt, kind="ExternalOutput").ap()
        if nm in ("dcb", "ptd"):
            keys = tuple(("d", tag, j) for j in range(DT))
        elif nm in ("as", "bs"):
            keys = tuple(("d", tag, j, qi) for j in range(DT) for qi in range(TOK // QW))
        elif nm == "rs":
            keys = tuple(("d", "rs", t0, h) for (t0, _w) in grid for h in (0, 1))
        else:
            keys = tuple(("d", tag, t0) for (t0, _w) in grid)
        dma("cout", dd, src, r=keys, w=())

    b.build_blocks(serial=serial)
    nc._b = b
    return nc



def _prep(inputs):
    f = np.float32
    xs = np.asarray(inputs["L0_attn_x"], f)
    w_in = np.asarray(inputs["L0_ffn_w_in"], f)
    w_out = np.asarray(inputs["L0_ffn_w_out"], f)
    w1 = np.asarray(inputs["L0_ffn_w1"], f)
    w2 = np.asarray(inputs["L0_ffn_w2"], f)
    wg = np.asarray(inputs["L0_ffn_wg"], f)
    we = np.asarray(inputs["L0_ffn_we"], f)
    gb = np.asarray(inputs["L0_ffn_gb"], f)
    eb = np.asarray(inputs["L0_ffn_eb"], f)
    wk = np.asarray(inputs["L0_attn_Wk"], f)
    wv = np.asarray(inputs["L0_attn_Wv"], f)
    wr = np.asarray(inputs["L0_attn_Wr"], f)
    wo = np.asarray(inputs["L0_attn_Wo"], f)
    td = np.asarray(inputs["L0_attn_time_decay"], f)
    tf = np.asarray(inputs["L0_attn_time_first"], f)
    tmk = np.asarray(inputs["L0_attn_time_mix_k"], f)
    tmv = np.asarray(inputs["L0_attn_time_mix_v"], f)
    tmr = np.asarray(inputs["L0_attn_time_mix_r"], f)

    def colmaj(vec):
        return np.ascontiguousarray(vec.reshape(DT, 128).T)

    tmix_a = np.concatenate([colmaj(tmk), colmaj(tmv), colmaj(tmr)], axis=1)
    tpar_base = np.concatenate([tmix_a, colmaj(td), colmaj(tf)], axis=1)
    shared = {
        "w_in_T": np.ascontiguousarray(w_in.T),
        "w1_T": np.ascontiguousarray(w1.T),
        "w_out_T": np.ascontiguousarray(w_out.T),
        "w2_T": np.ascontiguousarray(w2.transpose(0, 2, 1).reshape(NE * DH, D)),
        "wk_T": np.ascontiguousarray(wk.T),
        "wv_T": np.ascontiguousarray(wv.T),
        "wr_T": np.ascontiguousarray(wr.T),
        "wo_T": np.ascontiguousarray(wo.T),
        "wge_T": np.ascontiguousarray(np.concatenate([wg, we], 0).T),
        "gbias": np.ascontiguousarray(np.concatenate([gb, eb])[:, None]),
    }
    grid = []
    t = 0
    while t < HTOK:
        grid.append((t, min(TC, HTOK - t)))
        t += TC
    nch = len(grid)
    in_maps = []
    for c in range(8):
        bb, h = divmod(c, 2)
        seg = xs[bb, h * TOK:(h + 1) * TOK]
        halo = xs[bb, h * TOK - 1][None] if h == 1 else np.zeros((1, D), f)
        shard = np.concatenate([halo, seg], 0).T
        sh3 = shard.reshape(DT, 128, HTOK).transpose(1, 0, 2)
        xblk = np.zeros((nch, 128, DT, TC), f)
        for ci, (t0, w) in enumerate(grid):
            xblk[ci, :, :, 0:w] = sh3[:, :, t0:t0 + w]
        sel = np.zeros((128, DT), f)
        if h == 1:
            sel[:, c - 1] = 1.0
        m = dict(shared)
        m["x"] = np.ascontiguousarray(xblk.reshape(nch * 128, DT * TC))
        m["tparams"] = np.ascontiguousarray(np.concatenate([tpar_base, sel], axis=1))
        in_maps.append(m)
    return in_maps


_NC_CACHE = {}


def kernel(**inputs):
    in_maps = _prep(inputs)
    key = "main"
    if key not in _NC_CACHE:
        _NC_CACHE[key] = build_program()
    nc = _NC_CACHE[key]
    res = run_bass_kernel_spmd(nc, in_maps, core_ids=list(range(8)))
    out = np.empty((4, 8192, D), np.float32)
    for c in range(8):
        bb, h = divmod(c, 2)
        out[bb, h * TOK:(h + 1) * TOK] = np.asarray(res.results[c]["out"]).T
    return out


# version 25
# speedup vs baseline: 1.2184x; 1.0436x over previous
import numpy as np
import concourse.bass as bass
import concourse.mybir as mybir
from concourse.bass_utils import run_bass_kernel_spmd

F32 = mybir.dt.float32
BF16 = mybir.dt.bfloat16
OP = mybir.AluOpType
AF = mybir.ActivationFunctionType
AX = mybir.AxisListType.X

D = 1024
DT = 8
DH = 2048
HT = 16
NE = 8
TOK = 4096
HTOK = 4097
TC = 368
QW = 1024
BIGM = 30000.0
EPS = 1e-5

ENGS = ("pe", "act", "dve", "pool", "sp")
BLK_ATTR = {"pe": "tensor", "act": "scalar", "dve": "vector", "pool": "gpsimd", "sp": "sync"}
GRAN = 256
CROT = 4
SB_BASE = 16640
SB_LIMIT = 228352

R1 = SB_BASE + 26624
R2 = R1 + 65536
R3 = R2 + 65536
R3SZ = SB_LIMIT - R3
WO_OFF = R3 + 36864


def chunk_grid(total):
    out, t = [], 0
    while t < total:
        out.append((t, min(TC, total - t)))
        t += TC
    return out


class Rec:
    __slots__ = ("eng", "fn", "kind", "chan", "cidx", "ms", "deps", "waits", "gi", "snap", "tiny")

    def __init__(self, eng, fn, kind, chan, cidx, gi):
        self.eng, self.fn, self.kind, self.chan, self.cidx, self.gi = eng, fn, kind, chan, cidx, gi
        self.ms = 0
        self.deps = None
        self.waits = None


class ST:

    def __init__(self, t, off, nbytes):
        self.t = t
        self.off0 = off
        g0, g1 = off // GRAN, (off + nbytes + GRAN - 1) // GRAN
        self.keys = tuple(("s", g) for g in range(g0, g1))

    def __getitem__(self, idx):
        return self.t[idx]

    @property
    def ap(self):
        return self.t[:]


class RegionAlloc:
    def __init__(self, b, base, size, tag):
        self.b, self.base, self.end, self.off, self.tag = b, base, base + size, base, tag

    def t(self, name, shape, dtype):
        esz = 2 if dtype == BF16 else 4
        nb = int(np.prod(shape[1:])) * esz
        off = (self.off + 31) & ~31
        assert off + nb <= self.end, f"{self.tag}:{name} overflow {off}+{nb}>{self.end}"
        self.off = off + nb
        return self.b.sb(f"{self.tag}_{name}", shape, dtype, off=off)


class B:
    def __init__(self, nc):
        self.nc = nc
        self.order = []
        self.per_eng = {e: [] for e in ENGS}
        self.chans = {}
        self.writers = {}
        self.readers = {}
        self.n_coll = 0
        self._names = set()

    def sb(self, name, shape, dtype, off):
        assert name not in self._names, name
        self._names.add(name)
        esz = 2 if dtype == BF16 else 4
        nb = int(np.prod(shape[1:])) * esz
        assert off + nb <= SB_LIMIT, f"sbuf overflow {name}: {off}+{nb}"
        t = self.nc.alloc_sbuf_tensor_at(name, list(shape), dtype, offset=off)
        return ST(t, off, nb)

    def chan(self, name, eng):
        self.chans[name] = {"eng": eng, "n": 0}

    @staticmethod
    def _flat(keys):
        out = []
        for k in keys:
            if isinstance(k, tuple) and k and isinstance(k[0], tuple):
                out.extend(k)
            else:
                out.append(k)
        return out

    def emit(self, eng, fn, r=(), w=(), chan=None, coll=False, tiny=False):
        kind = "coll" if coll else ("dma" if chan else "eng")
        cidx = None
        if chan:
            cidx = self.chans[chan]["n"]
            self.chans[chan]["n"] += 1
        if coll:
            cidx = self.n_coll
            self.n_coll += 1
        rec = Rec(eng, fn, kind, chan, cidx, len(self.order))
        rec.tiny = tiny
        rec.snap = ({e: sum(1 for x in self.per_eng[e] if x.kind != "dma") for e in ENGS},
                    {c: m["n"] for c, m in self.chans.items()}, self.n_coll)
        rset, wset = self._flat(r), self._flat(w)
        deps = set()
        for k in rset:
            p = self.writers.get(k)
            if p is not None:
                deps.add(p)
        for k in wset:
            p = self.writers.get(k)
            if p is not None:
                deps.add(p)
            for q in self.readers.get(k, ()):
                deps.add(q)
        for k in rset:
            self.readers.setdefault(k, []).append(rec)
        for k in wset:
            self.writers[k] = rec
            self.readers[k] = []
        deps.discard(rec)
        rec.deps = deps
        self.order.append(rec)
        self.per_eng[eng].append(rec)
        return rec

    def finalize(self, serial=False):
        if serial:
            ms_no = {e: 0 for e in ENGS}
            eng_ms = {}
            for e in ENGS:
                for rec in self.per_eng[e]:
                    if rec.kind != "dma" and rec.kind != "coll":
                        ms_no[e] += 1
                        rec.ms = ms_no[e]
            cnt = {e: 0 for e in ENGS}
            for rec in self.order:
                waits = {}
                for e2 in ENGS:
                    if e2 != rec.eng and cnt[e2] > 0:
                        waits[("e", e2)] = cnt[e2]
                for c2, n2 in rec.snap[1].items():
                    for r_ in range(CROT):
                        nr = (n2 + CROT - 1 - r_) // CROT
                        if nr > 0:
                            waits[("c", c2, r_)] = 16 * nr
                if rec.snap[2] > 0:
                    waits[("x",)] = rec.snap[2]
                rec.waits = waits
                if rec.kind == "eng" and rec.ms > 0:
                    cnt[rec.eng] = rec.ms
            return ms_no
        for rec in self.order:
            for d in rec.deps:
                if d.kind == "eng" and (d.eng != rec.eng or d.tiny or rec.kind != "eng"):
                    d.ms = -1
        ms_no = {e: 0 for e in ENGS}
        for e in ENGS:
            for rec in self.per_eng[e]:
                if rec.kind == "eng" and rec.ms == -1:
                    ms_no[e] += 1
                    rec.ms = ms_no[e]
        for rec in self.order:
            waits = {}
            for d in rec.deps:
                if d.kind == "eng":
                    if d.eng == rec.eng and rec.kind == "eng" and not d.tiny:
                        continue
                    k = ("e", d.eng)
                    waits[k] = max(waits.get(k, 0), d.ms)
                elif d.kind == "dma":
                    k = ("c", d.chan, d.cidx % CROT)
                    waits[k] = max(waits.get(k, 0), 16 * (d.cidx // CROT + 1))
                else:
                    k = ("x",)
                    waits[k] = max(waits.get(k, 0), d.cidx + 1)
            rec.waits = waits
        return ms_no

    def build_blocks(self, serial=False):
        ms_no = self.finalize(serial=serial)
        nc = self.nc
        esem = {e: nc.alloc_semaphore(f"sem_{e}") for e in ENGS}
        csem = {c: [nc.alloc_semaphore(f"sem_{c}{r}") for r in range(CROT)]
                for c in self.chans}
        xsem = nc.alloc_semaphore("sem_coll")

        def semof(key):
            if key[0] == "e":
                return esem[key[1]]
            if key[0] == "c":
                return csem[key[1]][key[2]]
            return xsem

        with nc.Block() as blk:
            def make(engname, rlist):
                def body(eng):
                    cur = {}
                    for rec in rlist:
                        for key, val in sorted(rec.waits.items()):
                            if cur.get(key, 0) < val:
                                eng.wait_ge(semof(key), val)
                                cur[key] = val
                        inst = rec.fn(eng)
                        if rec.kind == "dma":
                            inst.then_inc(csem[rec.chan][rec.cidx % CROT], 16)
                        elif rec.kind == "coll":
                            inst.then_inc(xsem, 1)
                        elif rec.ms > 0:
                            inst.then_inc(esem[rec.eng], 1)
                    if engname == "sp":
                        for c, meta in self.chans.items():
                            for r in range(CROT):
                                nr = (meta["n"] + CROT - 1 - r) // CROT
                                if nr:
                                    eng.wait_ge(csem[c][r], 16 * nr)
                        if self.n_coll:
                            eng.wait_ge(xsem, self.n_coll)
                        for e2 in ENGS:
                            if e2 != "sp" and ms_no[e2]:
                                eng.wait_ge(esem[e2], ms_no[e2])
                return body

            for e in ENGS:
                getattr(blk, BLK_ATTR[e])(make(e, self.per_eng[e]))



DBG_HSH = [False]
SERIAL_PA1 = [False]
DBG_PD = [False]
FORCE_TINY = [False]


def build_program(debug=(), phases=None, serial=False):
    DBG_HSH[0] = "hsh" in debug
    DBG_PD[0] = "pd" in debug
    nc = bass.Bass("TRN2", target_bir_lowering=False)
    b = B(nc)
    b.chan("cin", "sp")
    b.chan("cout", "act")
    b.chan("cw", "pool")
    b.chan("crow", "sp")
    b.chan("csc", "act")
    b.chan("cscw", "act")
    b.chan("cxb", "sp")
    b.chan("cxl", "sp")
    b.chan("cpk", "pool")
    b.chan("cpv", "pool")
    b.chan("cpr", "pool")

    def din(name, shape, dtype=F32):
        return nc.dram_tensor(name, list(shape), dtype, kind="ExternalInput").ap()

    NCH = len(chunk_grid(HTOK))
    x_in = din("x", [NCH * 128, DT * TC])
    w_in_T = din("w_in_T", [D, 2 * DH])
    w1_T = din("w1_T", [D, 2 * DH])
    w_out_T = din("w_out_T", [DH, D])
    w2_T = din("w2_T", [NE * DH, D])
    wk_T = din("wk_T", [D, D])
    wv_T = din("wv_T", [D, D])
    wr_T = din("wr_T", [D, D])
    wo_T = din("wo_T", [D, D])
    wge_T = din("wge_T", [D, 10])
    gb_in = din("gbias", [10, 1])
    tpar_in = din("tparams", [128, 6 * DT])

    out_d = nc.dram_tensor("out", [D, TOK], F32, kind="ExternalOutput").ap()

    def idram(name, shape, dtype=F32):
        return nc.dram_tensor(name, list(shape), dtype).ap()

    xln_d = idram("xln_d", [NCH * 128, DT * TC], BF16)
    hrt_d = idram("hrt_d", [NCH * 128, HT * TC], BF16)
    pA_d = idram("pA_d", [NCH * 128, DT * TC])
    pB_d = idram("pB_d", [NCH * 128, DT * TC])
    xr2_d = idram("xr2_d", [NCH * 128, DT * TC])
    wse_d = idram("wse_d", [NE, HTOK])
    wsehl_d = idram("wsehl_d", [4 * NCH, 4 * TC], BF16)
    wsehl_v = wsehl_d.rearrange("(p c) (e t) -> p c e t", c=NCH, e=4)
    xr1_d = idram("xr1_d", [D, HTOK])
    k_d = idram("k_d", [D, TOK])
    v_d = idram("v_d", [D, TOK])
    rs_d = idram("rs_d", [D, TOK])
    as_d = idram("as_d", [D, TOK])
    bs_d = idram("bs_d", [D, TOK])
    hsh_d = idram("hsh_d", [DH, HTOK], BF16)
    dcb_d = idram("dcb_d", [D, QW])
    pt_d = idram("pt_d", [D, QW])
    ag_in_d = idram("ag_in", [128, 16])
    ag_out_d = idram("ag_out", [8 * 128, 16])

    def jv(ap):
        return ap.rearrange("(j p) t -> p j t", p=128)

    def blk(ap, c):
        return ap[c * 128:(c + 1) * 128, :]

    def blkm(ap, c, m, w):
        return ap[c * 128:(c + 1) * 128, m * TC:m * TC + w]

    xr1vw = jv(xr1_d)
    outvw = jv(out_d)
    kvw = jv(k_d)
    vvw = jv(v_d)
    rsvw = jv(rs_d)

    PS = [nc.alloc_psum_tensor(f"psb{i}", [128, 512], F32) for i in range(8)]

    def pkk(i):
        return ("p", i)

    CA = RegionAlloc(b, SB_BASE, R1 - SB_BASE, "c")
    idn = CA.t("idn", [128, 128], F32)
    ones_k = CA.t("ones_k", [128, 1], BF16)
    ones_r = CA.t("ones_r", [1, 128], BF16)
    tpar = CA.t("tpar", [128, 6 * DT], F32)
    tmix = b.sb("tmix", [128, 3 * DT], F32, off=tpar.off0)
    tdec = b.sb("tdec", [128, DT], F32, off=tpar.off0 + 96)
    tfir = b.sb("tfir", [128, DT], F32, off=tpar.off0 + 128)
    selv = b.sb("selv", [128, DT], F32, off=tpar.off0 + 160)
    etd = CA.t("etd", [128, DT], F32)
    netd = CA.t("netd", [128, DT], F32)
    decC = CA.t("dec", [128, DT], F32)
    wge_f = CA.t("wge_f", [128, DT, 10], F32)
    wge_hi = CA.t("wge_hi", [128, DT, 10], BF16)
    wge_lo = CA.t("wge_lo", [128, DT, 10], BF16)
    gbias = CA.t("gbias", [10, 1], F32)
    SAB1 = CA.t("SAB1", [128, 16], F32)
    SAB2 = CA.t("SAB2", [128, 16], F32)
    SAB1Q = [CA.t(f"SAB1Q{i}", [128, 16], F32) for i in range(4)]
    dqt = CA.t("dqt", [128, DT], F32)
    stage = CA.t("stage", [128, 16], F32)
    gsb = CA.t("gsb", [128, DT, 16], F32)
    carry_a = CA.t("carry_a", [128, DT], F32)
    carry_b = CA.t("carry_b", [128, DT], F32)
    pcol = CA.t("pcol", [128, DT], F32)
    rowA = CA.t("rowA", [1, 512], F32)
    rowB = CA.t("rowB", [1, 512], F32)
    rowC = CA.t("rowC", [1, 512], F32)
    rowD = CA.t("rowD", [1, 512], F32)
    rowE = CA.t("rowE", [1, 512], F32)
    rhiA = CA.t("rhiA", [1, 512], BF16)
    rloA = CA.t("rloA", [1, 512], BF16)
    rhiB = CA.t("rhiB", [1, 512], BF16)
    rloB = CA.t("rloB", [1, 512], BF16)
    glog = CA.t("glog", [10, 512], F32)
    wse8 = CA.t("wse8", [8, 512], F32)
    ones8 = CA.t("ones8", [128, 8], F32)
    wse8h = CA.t("wse8h", [8, 512], BF16)
    wse8l = CA.t("wse8l", [8, 512], BF16)
    assert CA.off <= R1, CA.off
    pbs0 = rowA.off0
    pb_pin = [b.sb(f"pbs_pin{i}", [128, TC], F32, off=pbs0 + i * 1536) for i in range(3)]
    pb_wb = [[b.sb(f"pbs_wb{p}{e}", [128, TC], F32, off=pbs0 + 4608 + (2 * p + e) * 1536)
              for e in range(2)] for p in range(2)]
    pb_rows4 = b.sb("pbs_rows4", [1, 4, TC], BF16, off=pbs0 + 10752)
    assert pbs0 + 10752 + 4 * TC * 2 <= rowA.off0 + 16384
    tc1 = b.sb("tc1", [128, 128], F32, off=R3)
    tc2 = b.sb("tc2", [128, 128], F32, off=R3 + 512)

    def wslot(call):
        return (R1, R2) if call == 0 else (R2, R1)

    w1_sb, w_in_sb, w_out_sb, w2_sb = {}, {}, {}, {}
    for call in (0, 1):
        sA, sB = wslot(call)
        w1_sb[call] = b.sb(f"w1s{call}", [128, DT, 2 * DH], BF16, off=sA)
        w_in_sb[call] = b.sb(f"wins{call}", [128, DT, 2 * DH], BF16, off=sB)
        w_out_sb[call] = b.sb(f"wouts{call}", [128, HT, D], BF16, off=sA)
        for q in range(4):
            off = sB if q % 2 == 0 else sA
            w2_sb[(call, q)] = b.sb(f"w2s{call}_{q}", [128, 2, HT, D], BF16, off=off)
    wk_sb = b.sb("wk_sb", [128, DT, D], BF16, off=R2)
    wv_sb = b.sb("wv_sb", [128, DT, D], BF16, off=R2 + 16384)
    wr_sb = b.sb("wr_sb", [128, DT, D], BF16, off=R2 + 32768)
    rwb = b.sb("rwb", [128, DT, TOK], BF16, off=R1)
    wo_sb = b.sb("wo_sb", [128, DT, D], BF16, off=WO_OFF)

    def dma(chan, dst, src, r=(), w=(), slow=False):
        eng = b.chans[chan]["eng"]

        def fn(e):
            if slow:
                return e.dma_start(dst, src, allow_slow_non_contiguous=True)
            return e.dma_start(dst, src)
        return b.emit(eng, fn, r=r, w=w, chan=chan)

    def op(eng, fn, r=(), w=(), tiny=False):
        return b.emit(eng, fn, r=r, w=w, tiny=tiny or FORCE_TINY[0])

    def mm(bank, out_ap, passes, reads, start=True, stop=True):
        def fn(eng):
            n = len(passes)
            inst = None
            for i, (l, rr) in enumerate(passes):
                inst = eng.matmul(out_ap, l, rr, start=(start and i == 0),
                                  stop=(stop and i == n - 1))
            return inst
        b.emit("pe", fn, r=reads, w=(pkk(bank),))

    def emit_ln(srcT, w, xbf, xc, tmpT, dstT,
                bf_out=None, lo=None, rows2=None):
        src3 = srcT.t[:, :, 0:w]
        src_keys = srcT.keys
        tmp3 = tmpT.t[:, :, 0:w]
        tmp_keys = tmpT.keys
        dst3 = dstT.t[:, :, 0:w]
        dst_keys = dstT.keys
        src2 = srcT.t[:, :, :].rearrange("p j t -> p (j t)")
        op("act", lambda e: e.copy(xbf.ap, src2), r=(src_keys,), w=(xbf.keys,))
        op("act", lambda e: e.activation(xc.ap, src2, AF.Square),
           r=(src_keys,), w=(xc.keys,))
        xbf3 = xbf.t[:, :].rearrange("p (j t) -> p j t", j=DT)
        xc3 = xc.t[:, :].rearrange("p (j t) -> p j t", j=DT)
        xbf3 = xbf3[:, :, 0:w]
        xc3 = xc3[:, :, 0:w]
        mm(4, PS[4][0:1, 0:w], [(ones_k.ap, xbf3[:, j, :]) for j in range(DT)], (xbf.keys, ones_k.keys))
        mm(5, PS[5][0:1, 0:w], [(ones_k.ap, xc3[:, j, :]) for j in range(DT)], (xc.keys, ones_k.keys))
        op("dve", lambda e: e.tensor_scalar(rowA[0:1, 0:w], PS[4][0:1, 0:w], 1.0 / D, None, OP.mult),
           r=(pkk(4),), w=(rowA.keys,), tiny=True)
        op("dve", lambda e: e.tensor_scalar(rowB[0:1, 0:w], PS[5][0:1, 0:w], 1.0 / D, EPS, OP.mult, OP.add),
           r=(pkk(5),), w=(rowB.keys,), tiny=True)
        op("dve", lambda e: e.tensor_tensor(rowC[0:1, 0:w], rowA[0:1, 0:w], rowA[0:1, 0:w], OP.mult),
           r=(rowA.keys,), w=(rowC.keys,), tiny=True)
        op("dve", lambda e: e.tensor_tensor(rowD[0:1, 0:w], rowB[0:1, 0:w], rowC[0:1, 0:w], OP.subtract),
           r=(rowB.keys, rowC.keys), w=(rowD.keys,), tiny=True)
        op("act", lambda e: e.activation(rowC[0:1, 0:w], rowD[0:1, 0:w], AF.Sqrt),
           r=(rowD.keys,), w=(rowC.keys,), tiny=True)
        op("dve", lambda e: e.reciprocal(rowE[0:1, 0:w], rowC[0:1, 0:w]),
           r=(rowC.keys,), w=(rowE.keys,), tiny=True)
        op("dve", lambda e: e.tensor_tensor(rowD[0:1, 0:w], rowA[0:1, 0:w], rowE[0:1, 0:w], OP.mult),
           r=(rowA.keys, rowE.keys), w=(rowD.keys,), tiny=True)
        op("act", lambda e: e.copy(rhiA[0:1, 0:w], rowE[0:1, 0:w]), r=(rowE.keys,), w=(rhiA.keys,), tiny=True)
        op("dve", lambda e: e.tensor_tensor(rloA[0:1, 0:w], rowE[0:1, 0:w], rhiA[0:1, 0:w], OP.subtract),
           r=(rowE.keys, rhiA.keys), w=(rloA.keys,), tiny=True)
        op("act", lambda e: e.copy(rhiB[0:1, 0:w], rowD[0:1, 0:w]), r=(rowD.keys,), w=(rhiB.keys,), tiny=True)
        op("dve", lambda e: e.tensor_tensor(rloB[0:1, 0:w], rowD[0:1, 0:w], rhiB[0:1, 0:w], OP.subtract),
           r=(rowD.keys, rhiB.keys), w=(rloB.keys,), tiny=True)
        mm(6, PS[6][:, 0:w], [(ones_r.ap, rhiA[0:1, 0:w]), (ones_r.ap, rloA[0:1, 0:w])],
           (rhiA.keys, rloA.keys, ones_r.keys))
        mm(7, PS[7][:, 0:w], [(ones_r.ap, rhiB[0:1, 0:w]), (ones_r.ap, rloB[0:1, 0:w])],
           (rhiB.keys, rloB.keys, ones_r.keys))
        if rows2 is not None:
            rstd_t, mr_t = rows2
            op("act", lambda e: e.copy(rstd_t.t[:, 0:w], PS[6][:, 0:w]),
               r=(pkk(6),), w=(rstd_t.keys,))
            op("act", lambda e: e.copy(mr_t.t[:, 0:w], PS[7][:, 0:w]),
               r=(pkk(7),), w=(mr_t.keys,))
            for j in range(DT):
                op("pool", lambda e, j=j: e.tensor_tensor(tmp3[:, j, :], src3[:, j, :], rstd_t.t[:, 0:w], OP.mult),
                   r=(src_keys, rstd_t.keys), w=(tmp_keys,))
            for j in range(DT):
                op("dve", lambda e, j=j: e.tensor_tensor(dst3[:, j, :], tmp3[:, j, :], mr_t.t[:, 0:w], OP.subtract),
                   r=(tmp_keys, mr_t.keys), w=(dst_keys,))
        else:
            for j in range(DT):
                op("dve", lambda e, j=j: e.tensor_tensor(tmp3[:, j, :], src3[:, j, :], PS[6][:, 0:w], OP.mult),
                   r=(src_keys, pkk(6)), w=(tmp_keys,))
            for j in range(DT):
                op("dve", lambda e, j=j: e.tensor_tensor(dst3[:, j, :], tmp3[:, j, :], PS[7][:, 0:w], OP.subtract),
                   r=(tmp_keys, pkk(7)), w=(dst_keys,))
        if bf_out is not None:
            dst2 = dstT.t[:, :, :].rearrange("p j t -> p (j t)")
            op("act", lambda e: e.copy(bf_out.ap, dst2), r=(dst_keys,), w=(bf_out.keys,))
        if lo is not None:
            bf3 = bf_out.t[:, :].rearrange("p (j t) -> p j t", j=DT)
            lo3 = lo.t[:, :].rearrange("p (j t) -> p j t", j=DT)
            for j in range(DT):
                op("dve", lambda e, j=j: e.tensor_tensor(lo3[:, j, 0:w], dst3[:, j, :], bf3[:, j, 0:w], OP.subtract),
                   r=(dst_keys, bf_out.keys), w=(lo.keys,))

    def emit_gating(t0, w, xbf3, xbf_keys, xlo3, xlo_keys, gsc):
        wgehi3 = wge_hi.t[:, :, :]
        wgelo3 = wge_lo.t[:, :, :]
        passes = [(wgehi3[:, j, :], xbf3[:, j, :]) for j in range(DT)]
        passes += [(wgehi3[:, j, :], xlo3[:, j, :]) for j in range(DT)]
        passes += [(wgelo3[:, j, :], xbf3[:, j, :]) for j in range(DT)]
        mm(4, PS[4][0:10, 0:w], passes, (wge_hi.keys, wge_lo.keys, xbf_keys, xlo_keys))
        op("dve", lambda e: e.tensor_scalar(glog[0:10, 0:w], PS[4][0:10, 0:w], gbias[0:10, 0:1], None, OP.add),
           r=(pkk(4), gbias.keys), w=(glog.keys,), tiny=True)
        nsub = (w + 127) // 128
        subs = []
        for s in range(nsub):
            tw = min(128, w - 128 * s)
            subs.append((s, tw, slice(128 * s, 128 * s + tw)) + tuple(gsc[s]))

        def step(emitter):
            for args in subs:
                emitter(*args)

        for (s, tw, sl, gt, gco, gm, gex_, gpx_, gz_, wtm) in subs:
            op("pe", lambda e, sl=sl, tw=tw: e.transpose(PS[5][0:tw, 0:10], glog[0:10, sl], idn[0:10, 0:10]),
               r=(glog.keys, idn.keys), w=(pkk(5),))
            op("dve", lambda e, tw=tw, gt=gt: e.tensor_copy(gt[0:tw, 0:10], PS[5][0:tw, 0:10]),
               r=(pkk(5),), w=(gt.keys,), tiny=True)
        step(lambda s, tw, sl, gt, gco, gm, gex_, gpx_, gz_, wtm:
             op("dve", lambda e, tw=tw, gt=gt, gco=gco: e.tensor_tensor(gco[0:tw, 0:1], gt[0:tw, 0:1], gt[0:tw, 1:2], OP.subtract),
                r=(gt.keys,), w=(gco.keys,), tiny=True))
        step(lambda s, tw, sl, gt, gco, gm, gex_, gpx_, gz_, wtm:
             op("dve", lambda e, tw=tw, gco=gco: e.scalar_tensor_tensor(gco[0:tw, 1:2], gco[0:tw, 0:1], -1.0, gco[0:tw, 0:1], OP.mult, OP.max),
                r=(gco.keys,), w=(gco.keys,), tiny=True))
        step(lambda s, tw, sl, gt, gco, gm, gex_, gpx_, gz_, wtm:
             op("act", lambda e, tw=tw, gco=gco: e.activation(gco[0:tw, 2:3], gco[0:tw, 1:2], AF.Sigmoid),
                r=(gco.keys,), w=(gco.keys,), tiny=True))
        step(lambda s, tw, sl, gt, gco, gm, gex_, gpx_, gz_, wtm:
             op("dve", lambda e, tw=tw, gco=gco: e.tensor_scalar(gco[0:tw, 4:5], gco[0:tw, 0:1], 0.0, None, OP.is_lt),
                r=(gco.keys,), w=(gco.keys,), tiny=True))
        step(lambda s, tw, sl, gt, gco, gm, gex_, gpx_, gz_, wtm:
             op("dve", lambda e, tw=tw, gco=gco, gm=gm: e.tensor_scalar(gm[0:tw, 0:4], ones8[0:tw, 0:4], gco[0:tw, 4:5], -1.0, OP.mult, OP.mult),
                r=(ones8.keys, gco.keys), w=(gm.keys,), tiny=True))
        step(lambda s, tw, sl, gt, gco, gm, gex_, gpx_, gz_, wtm:
             op("dve", lambda e, tw=tw, gco=gco, gm=gm: e.tensor_scalar(gm[0:tw, 4:8], ones8[0:tw, 0:4], gco[0:tw, 4:5], -1.0, OP.subtract, OP.mult),
                r=(ones8.keys, gco.keys), w=(gm.keys,), tiny=True))
        step(lambda s, tw, sl, gt, gco, gm, gex_, gpx_, gz_, wtm:
             op("dve", lambda e, tw=tw, gt=gt, gm=gm, gz_=gz_: e.scalar_tensor_tensor(gz_[0:tw, 0:8], gm[0:tw, 0:8], BIGM, gt[0:tw, 2:10], OP.mult, OP.add),
                r=(gm.keys, gt.keys), w=(gz_.keys,), tiny=True))
        step(lambda s, tw, sl, gt, gco, gm, gex_, gpx_, gz_, wtm:
             op("dve", lambda e, tw=tw, gco=gco, gz_=gz_: e.reduce_max(gco[0:tw, 5:6], gz_[0:tw, 0:8], AX, negate=True),
                r=(gz_.keys,), w=(gco.keys,), tiny=True))
        step(lambda s, tw, sl, gt, gco, gm, gex_, gpx_, gz_, wtm:
             op("act", lambda e, tw=tw, gco=gco, gex_=gex_, gz_=gz_: e.activation(gex_[0:tw, 0:8], gz_[0:tw, 0:8], AF.Exp, bias=gco[0:tw, 5:6]),
                r=(gz_.keys, gco.keys), w=(gex_.keys,), tiny=True))
        step(lambda s, tw, sl, gt, gco, gm, gex_, gpx_, gz_, wtm:
             op("dve", lambda e, tw=tw, gco=gco, gex_=gex_: e.reduce_sum(gco[0:tw, 6:7], gex_[0:tw, 0:8], AX),
                r=(gex_.keys,), w=(gco.keys,), tiny=True))
        step(lambda s, tw, sl, gt, gco, gm, gex_, gpx_, gz_, wtm:
             op("dve", lambda e, tw=tw, gco=gco: e.reciprocal(gco[0:tw, 7:8], gco[0:tw, 6:7]),
                r=(gco.keys,), w=(gco.keys,), tiny=True))
        step(lambda s, tw, sl, gt, gco, gm, gex_, gpx_, gz_, wtm:
             op("dve", lambda e, tw=tw, gco=gco: e.tensor_tensor(gco[0:tw, 4:5], gco[0:tw, 2:3], gco[0:tw, 7:8], OP.mult),
                r=(gco.keys,), w=(gco.keys,), tiny=True))
        step(lambda s, tw, sl, gt, gco, gm, gex_, gpx_, gz_, wtm:
             op("dve", lambda e, tw=tw, gco=gco, gex_=gex_, gpx_=gpx_: e.tensor_scalar(gpx_[0:tw, 0:8], gex_[0:tw, 0:8], gco[0:tw, 4:5], None, OP.mult),
                r=(gex_.keys, gco.keys), w=(gpx_.keys,), tiny=True))
        step(lambda s, tw, sl, gt, gco, gm, gex_, gpx_, gz_, wtm:
             op("dve", lambda e, tw=tw, gco=gco, gpx_=gpx_: e.reduce_max(gco[0:tw, 0:1], gpx_[0:tw, 0:8], AX),
                r=(gpx_.keys,), w=(gco.keys,), tiny=True))
        step(lambda s, tw, sl, gt, gco, gm, gex_, gpx_, gz_, wtm:
             op("dve", lambda e, tw=tw, gco=gco, gex_=gex_, gpx_=gpx_: e.tensor_scalar(gex_[0:tw, 0:8], gpx_[0:tw, 0:8], gco[0:tw, 0:1], None, OP.is_equal),
                r=(gpx_.keys, gco.keys), w=(gex_.keys,), tiny=True))
        step(lambda s, tw, sl, gt, gco, gm, gex_, gpx_, gz_, wtm:
             op("dve", lambda e, tw=tw, gex_=gex_, gpx_=gpx_, gz_=gz_: e.scalar_tensor_tensor(gz_[0:tw, 0:8], gex_[0:tw, 0:8], -BIGM, gpx_[0:tw, 0:8], OP.mult, OP.add),
                r=(gex_.keys, gpx_.keys), w=(gz_.keys,), tiny=True))
        step(lambda s, tw, sl, gt, gco, gm, gex_, gpx_, gz_, wtm:
             op("dve", lambda e, tw=tw, gco=gco, gz_=gz_: e.reduce_max(gco[0:tw, 1:2], gz_[0:tw, 0:8], AX),
                r=(gz_.keys,), w=(gco.keys,), tiny=True))
        step(lambda s, tw, sl, gt, gco, gm, gex_, gpx_, gz_, wtm:
             op("dve", lambda e, tw=tw, gco=gco, gpx_=gpx_, wtm=wtm: e.scalar_tensor_tensor(wtm[0:tw, 0:8], gpx_[0:tw, 0:8], gco[0:tw, 1:2], gpx_[0:tw, 0:8], OP.is_ge, OP.mult),
                r=(gpx_.keys, gco.keys), w=(wtm.keys,), tiny=True))
        for (s, tw, sl, gt, gco, gm, gex_, gpx_, gz_, wtm) in subs:
            op("pe", lambda e, tw=tw, wtm=wtm: e.transpose(PS[6][0:8, 0:tw], wtm[0:tw, 0:8], idn[0:tw, 0:tw]),
               r=(wtm.keys, idn.keys), w=(pkk(6),))
            op("dve", lambda e, s=s, tw=tw: e.tensor_copy(wse8[0:8, 128 * s:128 * s + tw], PS[6][0:8, 0:tw]),
               r=(pkk(6),), w=(wse8.keys,), tiny=True)
        dma("crow", wse_d[0:NE, t0:t0 + w], wse8[0:8, 0:w], r=(wse8.keys,), w=(("d", "wse", t0),))
        op("act", lambda e, w=w: e.copy(wse8h[0:8, 0:w], wse8[0:8, 0:w]),
           r=(wse8.keys,), w=(wse8h.keys,), tiny=True)
        op("dve", lambda e, w=w: e.tensor_tensor(wse8l[0:8, 0:w], wse8[0:8, 0:w], wse8h[0:8, 0:w], OP.subtract),
           r=(wse8.keys, wse8h.keys), w=(wse8l.keys,), tiny=True)
        ci_g = t0 // TC
        for p_ in range(4):
            dma("crow", wsehl_v[p_, ci_g, 0:2, 0:w], wse8h[2 * p_:2 * p_ + 2, 0:w],
                r=(wse8h.keys,), w=(("d", "wsehl", t0, p_, 0),))
            dma("crow", wsehl_v[p_, ci_g, 2:4, 0:w], wse8l[2 * p_:2 * p_ + 2, 0:w],
                r=(wse8l.keys,), w=(("d", "wsehl", t0, p_, 1),))
    def emit_swiglu(wsb, xbf3, xbf_keys, hbuf3, h_keys, w, scr):
        for i in range(HT):
            ba = (2 * i) % 4
            bb = ba + 1
            mm(ba, PS[ba][:, 0:w],
               [(wsb.t[:, kt, 128 * i:128 * (i + 1)], xbf3[:, kt, :]) for kt in range(DT)],
               (wsb.keys, xbf_keys))
            mm(bb, PS[bb][:, 0:w],
               [(wsb.t[:, kt, 128 * (HT + i):128 * (HT + i + 1)], xbf3[:, kt, :]) for kt in range(DT)],
               (wsb.keys, xbf_keys))
            op("act", lambda e, ba=ba: e.activation(scr.t[:, 0:w], PS[ba][:, 0:w], AF.Silu),
               r=(pkk(ba),), w=(scr.keys,))
            op("dve", lambda e, bb=bb, i=i: e.tensor_tensor(hbuf3[:, i, :], scr.t[:, 0:w], PS[bb][:, 0:w], OP.mult),
               r=(scr.keys, pkk(bb)), w=(h_keys,))

    grid_h = chunk_grid(HTOK)
    grid_t = chunk_grid(TOK)
    PCW = 420
    pc_grid = [(t, min(PCW, HTOK - t)) for t in range(0, HTOK, PCW)]

    dma("crow", tpar.ap, tpar_in, w=(tpar.keys,))
    dma("crow", wge_f.t[:, :, :], wge_T.rearrange("(j p) n -> p j n", p=128), w=(wge_f.keys,))
    dma("crow", gbias.ap, gb_in, w=(gbias.keys,))
    op("pool", lambda e: e.iota(tc1.ap, [[1, 128]], channel_multiplier=0,
                                allow_small_or_imprecise_dtypes=True), w=(tc1.keys,))
    op("pool", lambda e: e.iota(tc2.ap, [[0, 128]], channel_multiplier=1,
                                allow_small_or_imprecise_dtypes=True), w=(tc2.keys,))
    op("dve", lambda e: e.tensor_tensor(idn.ap, tc1.ap, tc2.ap, OP.is_equal),
       r=(tc1.keys, tc2.keys), w=(idn.keys,))
    op("dve", lambda e: e.memset(ones_k.ap, 1.0), w=(ones_k.keys,), tiny=True)
    op("dve", lambda e: e.memset(ones_r.ap, 1.0), w=(ones_r.keys,), tiny=True)
    op("dve", lambda e: e.memset(ones8.ap, 1.0), w=(ones8.keys,), tiny=True)
    op("dve", lambda e: e.memset(pcol.ap, 0.0), w=(pcol.keys,), tiny=True)
    op("dve", lambda e: e.memset(carry_a.ap, 0.0), w=(carry_a.keys,), tiny=True)
    op("dve", lambda e: e.memset(carry_b.ap, 0.0), w=(carry_b.keys,), tiny=True)
    op("act", lambda e: e.activation(etd.ap, tdec.ap, AF.Exp), r=(tdec.keys,), w=(etd.keys,), tiny=True)
    op("dve", lambda e: e.tensor_scalar(netd.ap, etd.ap, -1.0, None, OP.mult),
       r=(etd.keys,), w=(netd.keys,), tiny=True)
    op("act", lambda e: e.activation(decC.ap, etd.ap, AF.Exp, scale=-1.0),
       r=(etd.keys,), w=(decC.keys,), tiny=True)
    op("act", lambda e: e.copy(wge_hi.t[:, :, :], wge_f.t[:, :, :]), r=(wge_f.keys,), w=(wge_hi.keys,), tiny=True)
    for j in range(DT):
        op("dve", lambda e, j=j: e.tensor_tensor(wge_lo.t[:, j, :], wge_f.t[:, j, :], wge_hi.t[:, j, :], OP.subtract),
           r=(wge_f.keys, wge_hi.keys), w=(wge_lo.keys,), tiny=True)

    def wload3(sbT, src, kt, m, tag):
        dma("cw", sbT.t[:, :, :], src.rearrange("(kt p) m -> p kt m", p=128),
            w=(sbT.keys,))

    wload3(w1_sb[0], w1_T, DT, 2 * DH, "w1_0")

    def emit_PA2(call):
        srcB = x_in if call == 0 else xr2_d
        stag = "x" if call == 0 else "xr2"
        grid = grid_h if call == 0 else grid_t
        A = RegionAlloc(b, R3, R3SZ, f"a2{call}")
        xb = A.t("xb", [128, DT, TC], F32)
        tmp = A.t("tmp", [128, DT, TC], F32)
        xbf = A.t("xbf", [128, DT * TC], BF16)
        xc = A.t("xc", [128, DT * TC], BF16)
        scr = A.t("scr", [128, TC], F32)
        hb = b.sb(f"a2{call}_hb", [128, HT, TC], BF16, off=tmp.off0)
        A.off = (A.off + 255) & ~255
        gsc = [tuple(A.t(f"g{nm}{s}", [128, 64], F32)
                     for nm in ("t", "c", "m", "e", "p", "z", "w"))
               for s in range(3)]
        lnr = (A.t("rstd", [128, TC], F32), A.t("mrow", [128, TC], F32))
        for ci, (t0, w) in enumerate(grid):
            dma("cin", xb.t[:, :, :].rearrange("p j t -> p (j t)"), blk(srcB, ci),
                r=(("d", stag, t0),), w=(xb.keys,))
            emit_ln(xb, w, xbf, xc, tmp, xb, bf_out=xbf, lo=xc)
            xbf3 = xbf.t[:, :].rearrange("p (j t) -> p j t", j=DT)[:, :, 0:w]
            xc3 = xc.t[:, :].rearrange("p (j t) -> p j t", j=DT)[:, :, 0:w]
            dma("cxl", blk(xln_d, ci), xbf.ap, r=(xbf.keys,), w=(("d", "xln", t0),))
            emit_gating(t0, w, xbf3, xbf.keys, xc3, xc.keys, gsc)
            emit_swiglu(w1_sb[call], xbf3, xbf.keys, hb.t[:, :, 0:w], hb.keys, w, scr)
            dma("cin", blk(hrt_d, ci), hb.t[:, :, :].rearrange("p j t -> p (j t)"),
                r=(hb.keys,), w=(("d", "hrt", t0),))
        wload3(w_in_sb[call], w_in_T, DT, 2 * DH, f"win{call}")

    def emit_PA1(call):
        grid = grid_h if call == 0 else grid_t
        A = RegionAlloc(b, R3, R3SZ, f"a1{call}")
        xlnb = A.t("xlnb", [128, DT, TC], BF16)
        xlnb2 = A.t("xlnb2", [128, DT, TC], BF16)
        hb = A.t("hb", [128, HT, TC], BF16)
        scr = A.t("scr", [128, TC], F32)
        pob = A.t("pob", [128, DT, TC], F32)
        wload3(w_out_sb[call], w_out_T, HT, D, f"wo{call}")
        bufs = (xlnb, xlnb2)
        prev_key = None
        for ci, (t0, w) in enumerate(grid):
            xl = bufs[ci % 2]
            rk = (("d", "xln", t0),) if (prev_key is None or not SERIAL_PA1[0]) else (("d", "xln", t0), prev_key)
            dma("cxl", xl.t[:, :, :].rearrange("p j t -> p (j t)"), blk(xln_d, ci), r=rk, w=(xl.keys,))
            prev_key = ("d", "pA", t0)
            xbf3 = xl.t[:, :, 0:w]
            emit_swiglu(w_in_sb[call], xbf3, xl.keys, hb.t[:, :, 0:w], hb.keys, w, scr)
            if call == 0 and DBG_HSH[0]:
                dma("cout", hsh_d.rearrange("(j p) t -> p j t", p=128)[:, :, t0:t0 + w],
                    hb.t[:, :, 0:w], r=(hb.keys,), w=(("d", "hsh", t0),))
            for m in range(DT):
                bank = m % 4
                mm(bank, PS[bank][:, 0:w],
                   [(w_out_sb[call].t[:, kt, 128 * m:128 * (m + 1)], hb.t[:, kt, 0:w]) for kt in range(HT)],
                   (w_out_sb[call].keys, hb.keys))
                op("dve", lambda e, bank=bank, m=m, w=w: e.tensor_copy(pob.t[:, m, 0:w], PS[bank][:, 0:w]),
                   r=(pkk(bank),), w=(pob.keys,))
            dma("cin", blk(pA_d, ci), pob.t[:, :, :].rearrange("p j t -> p (j t)"),
                r=(pob.keys,), w=(("d", "pA", t0),))

    HRB = HT * TC * 2
    assert HRB % 256 == 0
    pb_hrtb = [b.sb(f"pb_hrt{i}", [128, HT, TC], BF16, off=R3 + i * HRB) for i in range(2)]
    he0b = b.sb("pb_he0", [128, HT, TC], BF16, off=R3 + 2 * HRB)
    he1b = b.sb("pb_he1", [128, HT, TC], BF16, off=R3 + 3 * HRB)
    TCB = 1536
    pb_xq = [b.sb(f"pb_xq{i}", [128, TC], F32, off=R3 + 4 * HRB + i * TCB) for i in range(2)]
    pb_po = [b.sb(f"pb_po{i}", [128, TC], F32, off=R3 + 4 * HRB + (2 + i) * TCB) for i in range(2)]
    assert R3 + 4 * HRB + 4 * TCB <= SB_LIMIT

    def emit_PB_call(call):
        grid = grid_h if call == 0 else grid_t
        stag = "x" if call == 0 else "xr2"
        xsrcB = x_in if call == 0 else xr2_d
        n = len(grid)
        steps = [(q, ci) for q in range(4) for ci in range(n)]
        N = len(steps)

        def qinfo(q):
            pin_b = (pA_d, pB_d, pA_d, pB_d)[q]
            pint = ("pA", "pB", "pA", "pB")[q]
            if q < 3:
                pout_b = (pB_d, pA_d, pB_d)[q]
                poutt = ("pB", "pA", "pB")[q]
            else:
                pout_b = xr1vw if call == 0 else outvw
                poutt = "xr1" if call == 0 else "out"
            return pin_b, pint, pout_b, poutt

        def load_hrt(s):
            q, ci = steps[s]
            t0, w = grid[ci]
            dma("cin", pb_hrtb[s % 2].t[:, :, :].rearrange("p j t -> p (j t)"), blk(hrt_d, ci),
                r=(("d", "hrt", t0),), w=(pb_hrtb[s % 2].keys,))

        def rows_and_bcast(s):
            q, ci = steps[s]
            t0, w = grid[ci]
            p = s % 2
            row = q * NCH + ci
            dma("crow", pb_rows4.t[0:1, :, :].rearrange("p e t -> p (e t)"),
                wsehl_d[row:row + 1, :],
                r=(("d", "wsehl", t0, q, 0), ("d", "wsehl", t0, q, 1)),
                w=(pb_rows4.keys,))
            for ei in range(2):
                bank = 6 + ei
                mm(bank, PS[bank][:, 0:w],
                   [(ones_r.ap, pb_rows4.t[0:1, ei, 0:w]), (ones_r.ap, pb_rows4.t[0:1, 2 + ei, 0:w])],
                   (pb_rows4.keys, ones_r.keys))
                op("act", lambda e, bank=bank, p=p, ei=ei, w=w:
                   e.copy(pb_wb[p][ei].t[:, 0:w], PS[bank][:, 0:w]),
                   r=(pkk(bank),), w=(pb_wb[p][ei].keys,))

        def emit_he(s, ei):
            q, ci = steps[s]
            t0, w = grid[ci]
            p = s % 2
            heX = (he0b, he1b)[ei]
            hr = pb_hrtb[p]
            wbX = pb_wb[p][ei]
            eng = "dve" if ei == 0 else "pool"
            for ht in range(HT):
                op(eng, lambda e, ht=ht, hr=hr, wbX=wbX, heX=heX, w=w:
                   e.tensor_tensor(heX.t[:, ht, 0:w], hr.t[:, ht, 0:w], wbX.t[:, 0:w], OP.mult),
                   r=(hr.keys, wbX.keys), w=(heX.keys,))

        load_hrt(0)
        rows_and_bcast(0)
        emit_he(0, 0)
        emit_he(0, 1)

        for s in range(N):
            q, ci = steps[s]
            t0, w = grid[ci]
            pin_b, pint, pout_b, poutt = qinfo(q)
            wq = w2_sb[(call, q)]
            if ci == 0:
                if q < 3:
                    nq = w2_sb[(call, q + 1)]
                    dma("cw", nq.t[:, :, :, :],
                        w2_T[(2 * q + 2) * DH:(2 * q + 4) * DH, :].rearrange(
                            "(e kt p) m -> p e kt m", p=128, kt=HT),
                        w=(nq.keys,))
                elif call == 0:
                    dma("cw", wk_sb.t[:, :, :], wk_T.rearrange("(kt p) m -> p kt m", p=128), w=(wk_sb.keys,))
                    dma("cw", wv_sb.t[:, :, :], wv_T.rearrange("(kt p) m -> p kt m", p=128), w=(wv_sb.keys,))
                    dma("cw", wr_sb.t[:, :, :], wr_T.rearrange("(kt p) m -> p kt m", p=128), w=(wr_sb.keys,))
            if s + 1 < N:
                load_hrt(s + 1)
                rows_and_bcast(s + 1)
            for m in range(DT):
                if m < 3:
                    dma("cout", pb_pin[m].t[:, 0:w], blkm(pin_b, ci, m, w),
                        r=(("d", pint, t0),), w=(pb_pin[m].keys,))
                mm(m, PS[m][:, 0:w],
                   [(wq.t[:, 0, kt, 128 * m:128 * (m + 1)], he0b.t[:, kt, 0:w]) for kt in range(HT)],
                   (wq.keys, he0b.keys), start=True, stop=False)
            if s + 1 < N:
                emit_he(s + 1, 0)
            for m in range(DT):
                if q == 3:
                    xqX = pb_xq[m % 2]
                    dma("cout", xqX.t[:, 0:w], blkm(xsrcB, ci, m, w),
                        r=(("d", stag, t0),), w=(xqX.keys,))
                mm(m, PS[m][:, 0:w],
                   [(wq.t[:, 1, kt, 128 * m:128 * (m + 1)], he1b.t[:, kt, 0:w]) for kt in range(HT)],
                   (wq.keys, he1b.keys), start=False, stop=True)
                pinX = pb_pin[m % 3]
                poX = pb_po[m % 2]
                if q < 3:
                    op("dve", lambda e, m=m, pinX=pinX, poX=poX, w=w:
                       e.scalar_tensor_tensor(poX.t[:, 0:w], PS[m][:, 0:w], 1.0, pinX.t[:, 0:w], OP.mult, OP.add),
                       r=(pkk(m), pinX.keys), w=(poX.keys,))
                else:
                    xqX = pb_xq[m % 2]
                    op("dve", lambda e, m=m, xqX=xqX, poX=poX, w=w:
                       e.scalar_tensor_tensor(poX.t[:, 0:w], PS[m][:, 0:w], 0.5, xqX.t[:, 0:w], OP.mult, OP.add),
                       r=(pkk(m), xqX.keys), w=(poX.keys,))
                    op("dve", lambda e, pinX=pinX, poX=poX, xqX=xqX, w=w:
                       e.scalar_tensor_tensor(xqX.t[:, 0:w], pinX.t[:, 0:w], 0.5, poX.t[:, 0:w], OP.mult, OP.add),
                       r=(pinX.keys, poX.keys), w=(xqX.keys,))
                    poX = xqX
                if q < 3:
                    dst = blkm(pout_b, ci, m, w)
                else:
                    dst = pout_b[:, m, t0:t0 + w]
                dma("cout", dst, poX.t[:, 0:w], r=(poX.keys,), w=(("d", poutt, t0),))
                if m + 3 < DT:
                    dma("cout", pb_pin[m % 3].t[:, 0:w], blkm(pin_b, ci, m + 3, w),
                        r=(("d", pint, t0),), w=(pb_pin[m % 3].keys,))
            if s + 1 < N:
                emit_he(s + 1, 1)

    def emit_q0_load(call):
        wq = w2_sb[(call, 0)]
        dma("cw", wq.t[:, :, :, :],
            w2_T[0:2 * DH, :].rearrange("(e kt p) m -> p e kt m", p=128, kt=HT),
            w=(wq.keys,))

    def emit_PC():
        A = RegionAlloc(b, R3, R3SZ, "pc")
        xb2 = [A.t(f"xb{i}", [128, DT, PCW], F32) for i in range(2)]
        tmp = A.t("tmp", [128, DT, PCW], F32)
        xbf = A.t("xbf", [128, DT * PCW], BF16)
        xc = A.t("xc", [128, DT * PCW], BF16)
        xprev = b.sb("pc_xprev", [128, DT, PCW], F32, off=R1)
        mixA = b.sb("pc_mixA", [128, DT, PCW], BF16, off=R1 + 14336)
        mixB = b.sb("pc_mixB", [128, DT, PCW], BF16, off=R1 + 21504)
        evacs = [b.sb(f"pc_evac{i}", [128, DT, PCW], F32, off=R1 + 28672 + i * 14336)
                 for i in range(2)]
        lnr = (b.sb("pc_rstd", [128, PCW], F32, off=R1 + 57344),
               b.sb("pc_mrow", [128, PCW], F32, off=R1 + 59392))
        wsbs = (wk_sb, wv_sb, wr_sb)
        dsts = (kvw, vvw, rsvw)
        dtags = ("k", "v", "rs")
        xr1_all = tuple(("d", "xr1", t0) for (t0, _w) in grid_h)
        for ci, (t0, w) in enumerate(pc_grid):
            xb = xb2[ci % 2]
            rk = tuple(k for k, (kt0, kw) in zip(xr1_all, grid_h)
                       if kt0 < t0 + w and kt0 + kw > t0)
            dma("cin", xb[:, :, 0:w], xr1vw[:, :, t0:t0 + w], r=rk, w=(xb.keys,))
            emit_ln(xb, w, xbf, xc, tmp, xb)
            for j in range(DT):
                op("pool", lambda e, j=j, w=w, xb=xb: e.tensor_copy(xprev.t[:, j, 1:w], xb.t[:, j, 0:w - 1]),
                   r=(xb.keys,), w=(xprev.keys,))
            pcol3 = pcol.t[:, :].rearrange("p (j t) -> p j t", t=1)
            op("dve", lambda e, pcol3=pcol3: e.tensor_copy(xprev.t[:, :, 0:1], pcol3),
               r=(pcol.keys,), w=(xprev.keys,), tiny=True)
            op("dve", lambda e, w=w, pcol3=pcol3, xb=xb: e.tensor_copy(pcol3, xb.t[:, :, w - 1:w]),
               r=(xb.keys,), w=(pcol.keys,), tiny=True)
            for j in range(DT):
                op("dve", lambda e, j=j, w=w, xb=xb: e.tensor_tensor(tmp.t[:, j, 0:w], xb.t[:, j, 0:w], xprev.t[:, j, 0:w], OP.subtract),
                   r=(xb.keys, xprev.keys), w=(tmp.keys,))
            if t0 == 0:
                s0, c0, cw = 1, 0, w - 1
            else:
                s0, c0, cw = 0, t0 - 1, w
            for i in range(3):
                mx = (mixA, mixB)[i % 2]
                evac = evacs[i % 2]
                for j in range(DT):
                    op("dve", lambda e, i=i, j=j, mx=mx, w=w: e.scalar_tensor_tensor(
                        mx.t[:, j, 0:w], tmp.t[:, j, 0:w], tmix[:, 8 * i + j:8 * i + j + 1],
                        xprev.t[:, j, 0:w], OP.mult, OP.add),
                       r=(tmp.keys, xprev.keys, tmix.keys), w=(mx.keys,))
                for m in range(DT):
                    bank = m % 4
                    mm(bank, PS[bank][:, 0:w],
                       [(wsbs[i].t[:, kt, 128 * m:128 * (m + 1)], mx.t[:, kt, 0:w]) for kt in range(DT)],
                       (wsbs[i].keys, mx.keys))
                    if i == 0:
                        op("act", lambda e, bank=bank, m=m, w=w, evac=evac: e.copy(evac.t[:, m, 0:w], PS[bank][:, 0:w]),
                           r=(pkk(bank),), w=(evac.keys,))
                    elif i == 1:
                        op("dve", lambda e, bank=bank, m=m, w=w, evac=evac: e.tensor_copy(evac.t[:, m, 0:w], PS[bank][:, 0:w]),
                           r=(pkk(bank),), w=(evac.keys,))
                    else:
                        op("act", lambda e, bank=bank, m=m, w=w, evac=evac: e.activation(evac.t[:, m, 0:w], PS[bank][:, 0:w], AF.Sigmoid),
                           r=(pkk(bank),), w=(evac.keys,))
                if i < 2:
                    dma(("cpk", "cpv")[i], dsts[i][:, :, c0:c0 + cw], evac.t[:, :, s0:s0 + cw],
                        r=(evac.keys,), w=(("d", dtags[i], t0),))
                else:
                    dma("cpr", dsts[i][:, 0:4, c0:c0 + cw], evac.t[:, 0:4, s0:s0 + cw],
                        r=(evac.keys,), w=(("d", "rs", t0, 0),))
                    dma("cpr", dsts[i][:, 4:8, c0:c0 + cw], evac.t[:, 4:8, s0:s0 + cw],
                        r=(evac.keys,), w=(("d", "rs", t0, 1),))
        return

    def pc_cov(q0, tag, rs=False):
        out = []
        for (t0, wq_) in pc_grid:
            c0 = 0 if t0 == 0 else t0 - 1
            cw_ = wq_ - 1 if t0 == 0 else wq_
            if c0 < q0 + QW and c0 + cw_ > q0:
                out.append(t0)
        if rs:
            return tuple(("d", tag, t0, h) for t0 in out for h in (0, 1))
        return tuple(("d", tag, t0) for t0 in out)

    def emit_PD():
        A = RegionAlloc(b, R3, WO_OFF - R3, "pd")
        kt_ = A.t("kt", [128, QW], F32)
        vt_ = A.t("vt", [128, QW], F32)
        ekt = A.t("ekt", [128, QW], F32)
        pt_ = A.t("pt", [128, QW], F32)
        ones1k = A.t("ones1k", [128, QW], F32)
        decb = A.t("decb", [128, QW], F32)
        abuf = A.t("abuf", [128, QW], F32)
        bbuf = A.t("bbuf", [128, QW], F32)
        dma("cw", wo_sb.t[:, :, :], wo_T.rearrange("(kt p) m -> p kt m", p=128), w=(wo_sb.keys,))
        op("dve", lambda e: e.memset(ones1k.ap, 1.0), w=(ones1k.keys,), tiny=True)

        for qi in range(TOK // QW):
            q0 = qi * QW
            kkeys = pc_cov(q0, "k")
            vkeys = pc_cov(q0, "v")
            for j in range(DT):
                op("dve", lambda e, j=j: e.tensor_scalar(decb.ap, ones1k.ap, decC[:, j:j + 1], None, OP.mult),
                   r=(ones1k.keys, decC.keys), w=(decb.keys,))
                if DBG_PD[0] and qi == 0:
                    dma("cout", dcb_d[128 * j:128 * (j + 1), :], decb.ap, r=(decb.keys,), w=(("d", "dcb", j),))
                dma("csc", kt_.ap, k_d[128 * j:128 * (j + 1), q0:q0 + QW], r=kkeys, w=(kt_.keys,))
                dma("csc", vt_.ap, v_d[128 * j:128 * (j + 1), q0:q0 + QW], r=vkeys, w=(vt_.keys,))
                op("act", lambda e: e.activation(ekt.ap, kt_.ap, AF.Exp), r=(kt_.keys,), w=(ekt.keys,))
                op("pool", lambda e: e.tensor_tensor(pt_.ap, ekt.ap, vt_.ap, OP.mult),
                   r=(ekt.keys, vt_.keys), w=(pt_.keys,))
                if DBG_PD[0] and qi == 0:
                    dma("cout", pt_d[128 * j:128 * (j + 1), :], pt_.ap, r=(pt_.keys,), w=(("d", "ptd", j),))
                op("dve", lambda e, j=j: e.tensor_tensor_scan(abuf.ap, decb.ap, pt_.ap, carry_a[:, j:j + 1], OP.mult, OP.add),
                   r=(decb.keys, pt_.keys, carry_a.keys), w=(abuf.keys,))
                op("dve", lambda e, j=j: e.tensor_tensor_scan(bbuf.ap, decb.ap, ekt.ap, carry_b[:, j:j + 1], OP.mult, OP.add),
                   r=(decb.keys, ekt.keys, carry_b.keys), w=(bbuf.keys,))
                op("dve", lambda e, j=j: e.tensor_copy(carry_a[:, j:j + 1], abuf[:, QW - 1:QW]),
                   r=(abuf.keys,), w=(carry_a.keys,), tiny=True)
                op("dve", lambda e, j=j: e.tensor_copy(carry_b[:, j:j + 1], bbuf[:, QW - 1:QW]),
                   r=(bbuf.keys,), w=(carry_b.keys,), tiny=True)
                dma("cscw", as_d[128 * j:128 * (j + 1), q0:q0 + QW], abuf.ap,
                    r=(abuf.keys,), w=(("d", "as", j, qi),))
                dma("cscw", bs_d[128 * j:128 * (j + 1), q0:q0 + QW], bbuf.ap,
                    r=(bbuf.keys,), w=(("d", "bs", j, qi),))
        op("dve", lambda e: e.tensor_copy(stage[:, 0:8], carry_a.ap), r=(carry_a.keys,), w=(stage.keys,), tiny=True)
        op("dve", lambda e: e.tensor_copy(stage[:, 8:16], carry_b.ap), r=(carry_b.keys,), w=(stage.keys,), tiny=True)
        dma("crow", ag_in_d, stage.ap, r=(stage.keys,), w=(("d", "agin"),))
        b.emit("pool", lambda e: e.collective_compute(
            "AllGather", OP.bypass, replica_groups=[list(range(8))],
            ins=[ag_in_d.opt()], outs=[ag_out_d.opt()]),
            r=(("d", "agin"),), w=(("d", "agout"),), coll=True)
        dma("crow", gsb.t[:, :, :], ag_out_d.rearrange("(r p) c -> p r c", p=128),
            r=(("d", "agout"),), w=(gsb.keys,))
        op("dve", lambda e: e.tensor_scalar(SAB2.ap, gsb.t[:, 0, :], selv[:, 0:1], None, OP.mult),
           r=(gsb.keys, selv.keys), w=(SAB2.keys,), tiny=True)
        for rr in range(1, 8):
            src_, dst_ = (SAB2, SAB1) if rr % 2 == 1 else (SAB1, SAB2)
            op("dve", lambda e, rr=rr, src_=src_, dst_=dst_: e.scalar_tensor_tensor(
                dst_.ap, gsb.t[:, rr, :], selv[:, rr:rr + 1], src_.ap, OP.mult, OP.add),
               r=(gsb.keys, selv.keys, src_.keys), w=(dst_.keys,), tiny=True)
        op("dve", lambda e: e.tensor_copy(SAB1Q[0].ap, SAB1.ap), r=(SAB1.keys,), w=(SAB1Q[0].keys,), tiny=True)
        for qi_ in range(1, 4):
            op("act", lambda e, qi_=qi_: e.activation(dqt.ap, netd.ap, AF.Exp, scale=float(1024 * qi_)),
               r=(netd.keys,), w=(dqt.keys,), tiny=True)
            op("dve", lambda e, qi_=qi_: e.tensor_tensor(SAB1Q[qi_][:, 0:8], SAB1[:, 0:8], dqt.ap, OP.mult),
               r=(SAB1.keys, dqt.keys), w=(SAB1Q[qi_].keys,), tiny=True)
            op("dve", lambda e, qi_=qi_: e.tensor_tensor(SAB1Q[qi_][:, 8:16], SAB1[:, 8:16], dqt.ap, OP.mult),
               r=(SAB1.keys, dqt.keys), w=(SAB1Q[qi_].keys,), tiny=True)

    def emit_PE():
        names = ("kt", "vt", "rt", "ap", "bp", "dpw", "eku", "tq")
        PEB = [{nm: b.sb(f"pe_{nm}{p}", [128, QW], F32, off=R2 + i * 4096 + p * 32768)
                for i, nm in enumerate(names)} for p in (0, 1)]
        iq = b.sb("pe_iq", [128, QW], F32, off=R3)
        dpw0 = b.sb("pe_dpwtab", [128, QW], F32, off=R3 + 4096)
        op("pool", lambda e: e.iota(iq.ap, [[1, QW]], base=0, channel_multiplier=0,
                                    allow_small_or_imprecise_dtypes=True), w=(iq.keys,))
        for j in range(DT):
            for qi in range(TOK // QW):
                q0 = qi * QW
                kkey = pc_cov(q0, "k")
                vkey = pc_cov(q0, "v")
                rkey = pc_cov(q0, "rs", rs=True)
                P = PEB[(j * (TOK // QW) + qi) % 2]
                kt_, vt_, rt_ = P["kt"], P["vt"], P["rt"]
                ap_, bp_, dpw, eku, tq = P["ap"], P["bp"], P["dpw"], P["eku"], P["tq"]
                if qi == 0:
                    op("dve", lambda e, j=j, tq=tq: e.tensor_scalar(tq.ap, iq.ap, netd[:, j:j + 1], None, OP.mult),
                       r=(iq.keys, netd.keys), w=(tq.keys,))
                    op("act", lambda e, tq=tq: e.activation(dpw0.ap, tq.ap, AF.Exp),
                       r=(tq.keys,), w=(dpw0.keys,))
                akey = (("d", "as", j, qi), ("d", "as", j, max(0, qi - 1)))
                bkey = (("d", "bs", j, qi), ("d", "bs", j, max(0, qi - 1)))
                dma("csc", kt_.ap, k_d[128 * j:128 * (j + 1), q0:q0 + QW], r=kkey, w=(kt_.keys,))
                dma("csc", vt_.ap, v_d[128 * j:128 * (j + 1), q0:q0 + QW], r=vkey, w=(vt_.keys,))
                dma("csc", rt_.ap, rs_d[128 * j:128 * (j + 1), q0:q0 + QW], r=rkey, w=(rt_.keys,))
                if q0 == 0:
                    op("dve", lambda e, ap_=ap_: e.memset(ap_[:, 0:1], 0.0), w=(ap_.keys,), tiny=True)
                    op("dve", lambda e, bp_=bp_: e.memset(bp_[:, 0:1], 0.0), w=(bp_.keys,), tiny=True)
                    dma("csc", ap_[:, 1:QW], as_d[128 * j:128 * (j + 1), 0:QW - 1], r=akey, w=(ap_.keys,))
                    dma("csc", bp_[:, 1:QW], bs_d[128 * j:128 * (j + 1), 0:QW - 1], r=bkey, w=(bp_.keys,))
                else:
                    dma("csc", ap_.ap, as_d[128 * j:128 * (j + 1), q0 - 1:q0 + QW - 1], r=akey, w=(ap_.keys,))
                    dma("csc", bp_.ap, bs_d[128 * j:128 * (j + 1), q0 - 1:q0 + QW - 1], r=bkey, w=(bp_.keys,))
                op("act", lambda e, j=j, eku=eku, kt_=kt_: e.activation(eku.ap, kt_.ap, AF.Exp, bias=tfir[:, j:j + 1]),
                   r=(kt_.keys, tfir.keys), w=(eku.keys,))
                op("pool", lambda e, tq=tq, eku=eku, vt_=vt_: e.tensor_tensor(tq.ap, eku.ap, vt_.ap, OP.mult),
                   r=(eku.keys, vt_.keys), w=(tq.keys,))
                sq = SAB1Q[qi]
                op("dve", lambda e, j=j, kt_=kt_, ap_=ap_, sq=sq: e.scalar_tensor_tensor(kt_.ap, dpw0.ap, sq[:, j:j + 1], ap_.ap, OP.mult, OP.add),
                   r=(dpw0.keys, sq.keys, ap_.keys), w=(kt_.keys,))
                op("dve", lambda e, j=j, vt_=vt_, bp_=bp_, sq=sq: e.scalar_tensor_tensor(vt_.ap, dpw0.ap, sq[:, 8 + j:8 + j + 1], bp_.ap, OP.mult, OP.add),
                   r=(dpw0.keys, sq.keys, bp_.keys), w=(vt_.keys,))
                op("pool", lambda e, ap_=ap_, kt_=kt_, tq=tq: e.tensor_tensor(ap_.ap, kt_.ap, tq.ap, OP.add),
                   r=(kt_.keys, tq.keys), w=(ap_.keys,))
                op("pool", lambda e, bp_=bp_, vt_=vt_, eku=eku: e.tensor_tensor(bp_.ap, vt_.ap, eku.ap, OP.add),
                   r=(vt_.keys, eku.keys), w=(bp_.keys,))
                op("dve", lambda e, dpw=dpw, bp_=bp_: e.tensor_scalar(dpw.ap, bp_.ap, 1e-6, None, OP.max),
                   r=(bp_.keys,), w=(dpw.keys,))
                op("dve", lambda e, tq=tq, dpw=dpw: e.reciprocal(tq.ap, dpw.ap), r=(dpw.keys,), w=(tq.keys,))
                op("pool", lambda e, kt_=kt_, ap_=ap_, tq=tq: e.tensor_tensor(kt_.ap, ap_.ap, tq.ap, OP.mult),
                   r=(ap_.keys, tq.keys), w=(kt_.keys,))
                op("dve", lambda e, j=j, q0=q0, kt_=kt_, rt_=rt_: e.tensor_tensor(rwb.t[:, j, q0:q0 + QW], kt_.ap, rt_.ap, OP.mult),
                   r=(kt_.keys, rt_.keys), w=(rwb.keys,))


    def emit_PF():
        A = RegionAlloc(b, R3, WO_OFF - R3, "pf")
        xbs = [A.t(f"xb{i}", [128, DT, TC], F32) for i in range(2)]
        ob = A.t("ob", [128, DT, TC], F32)
        xr1keys = tuple(("d", "xr1", t0) for (t0, _w) in grid_h)
        for ci, (t0, w) in enumerate(grid_t):
            xb = xbs[ci % 2]
            dma("cxb", xb[:, 0:4, 0:w], xr1vw[:, 0:4, t0 + 1:t0 + 1 + w],
                r=xr1keys, w=(xb.keys,))
            dma("cxb", xb[:, 4:8, 0:w], xr1vw[:, 4:8, t0 + 1:t0 + 1 + w],
                r=xr1keys, w=(xb.keys,))
            for m in range(DT):
                bank = m % 4
                mm(bank, PS[bank][:, 0:w],
                   [(wo_sb.t[:, kt, 128 * m:128 * (m + 1)], rwb.t[:, kt, t0:t0 + w]) for kt in range(DT)],
                   (wo_sb.keys, rwb.keys))
                op("dve", lambda e, bank=bank, m=m, w=w, xb=xb: e.tensor_tensor(ob.t[:, m, 0:w], PS[bank][:, 0:w], xb.t[:, m, 0:w], OP.add),
                   r=(pkk(bank), xb.keys), w=(ob.keys,))
            dma("cin", blk(xr2_d, ci), ob.t[:, :, :].rearrange("p j t -> p (j t)"),
                r=(ob.keys,), w=(("d", "xr2", t0),))
        wload3(w1_sb[1], w1_T, DT, 2 * DH, "w1_1")

    allph = ["A2", "A1", "B", "C", "D", "E", "F", "A2b", "A1b", "Bb"]
    ph = allph if phases is None else phases
    if "A2" in ph:
        emit_PA2(0)
    if "A1" in ph:
        emit_PA1(0)
    if "B" in ph:
        emit_q0_load(0)
        emit_PB_call(0)
    if "C" in ph:
        emit_PC()
    if "D" in ph:
        emit_PD()
    if "E" in ph:
        emit_PE()
    if "F" in ph:
        emit_PF()
    if "A2b" in ph:
        emit_PA2(1)
    if "A1b" in ph:
        emit_PA1(1)
    if "Bb" in ph:
        emit_q0_load(1)
        emit_PB_call(1)

    dbg_src = {"xln1": (xln_d, "xln", grid_h), "wse": (wse_d, "wse", grid_h),
               "hsh": (hsh_d, "hsh", grid_h), "pB": (pB_d, "pB", grid_h),
               "as": (as_d, "as", None), "bs": (bs_d, "bs", None),
               "dcb": (dcb_d, "dcb", None), "ptd": (pt_d, "ptd", None),
               "pA": (pA_d, "pA", grid_h), "xr1": (xr1_d, "xr1", grid_h),
               "hrt": (hrt_d, "hrt", grid_h),
               "k": (k_d, "k", pc_grid), "v": (v_d, "v", pc_grid), "rs": (rs_d, "rs", pc_grid),
               "xr2": (xr2_d, "xr2", grid_t)}
    if "rwb" in debug:
        dd = nc.dram_tensor("dbg_rwb", [128, DT * TOK], BF16, kind="ExternalOutput").ap()
        dma("cout", dd, rwb.t[:, :, :].rearrange("p j t -> p (j t)"), r=(rwb.keys,), w=())
    for nm in debug:
        if nm not in dbg_src:
            continue
        src, tag, grid = dbg_src[nm]
        shp = list(src.tensor.shape) if hasattr(src, "tensor") else None
        dt = BF16 if nm in ("xln1", "hrt", "hsh") else F32
        dd = nc.dram_tensor("dbg_" + nm, [src.shape[0], src.shape[1]], dt, kind="ExternalOutput").ap()
        if nm in ("dcb", "ptd"):
            keys = tuple(("d", tag, j) for j in range(DT))
        elif nm in ("as", "bs"):
            keys = tuple(("d", tag, j, qi) for j in range(DT) for qi in range(TOK // QW))
        elif nm == "rs":
            keys = tuple(("d", "rs", t0, h) for (t0, _w) in grid for h in (0, 1))
        else:
            keys = tuple(("d", tag, t0) for (t0, _w) in grid)
        dma("cout", dd, src, r=keys, w=())

    b.build_blocks(serial=serial)
    nc._b = b
    return nc



def _prep(inputs):
    f = np.float32
    xs = np.asarray(inputs["L0_attn_x"], f)
    w_in = np.asarray(inputs["L0_ffn_w_in"], f)
    w_out = np.asarray(inputs["L0_ffn_w_out"], f)
    w1 = np.asarray(inputs["L0_ffn_w1"], f)
    w2 = np.asarray(inputs["L0_ffn_w2"], f)
    wg = np.asarray(inputs["L0_ffn_wg"], f)
    we = np.asarray(inputs["L0_ffn_we"], f)
    gb = np.asarray(inputs["L0_ffn_gb"], f)
    eb = np.asarray(inputs["L0_ffn_eb"], f)
    wk = np.asarray(inputs["L0_attn_Wk"], f)
    wv = np.asarray(inputs["L0_attn_Wv"], f)
    wr = np.asarray(inputs["L0_attn_Wr"], f)
    wo = np.asarray(inputs["L0_attn_Wo"], f)
    td = np.asarray(inputs["L0_attn_time_decay"], f)
    tf = np.asarray(inputs["L0_attn_time_first"], f)
    tmk = np.asarray(inputs["L0_attn_time_mix_k"], f)
    tmv = np.asarray(inputs["L0_attn_time_mix_v"], f)
    tmr = np.asarray(inputs["L0_attn_time_mix_r"], f)

    def colmaj(vec):
        return np.ascontiguousarray(vec.reshape(DT, 128).T)

    tmix_a = np.concatenate([colmaj(tmk), colmaj(tmv), colmaj(tmr)], axis=1)
    tpar_base = np.concatenate([tmix_a, colmaj(td), colmaj(tf)], axis=1)
    shared = {
        "w_in_T": np.ascontiguousarray(w_in.T),
        "w1_T": np.ascontiguousarray(w1.T),
        "w_out_T": np.ascontiguousarray(w_out.T),
        "w2_T": np.ascontiguousarray(w2.transpose(0, 2, 1).reshape(NE * DH, D)),
        "wk_T": np.ascontiguousarray(wk.T),
        "wv_T": np.ascontiguousarray(wv.T),
        "wr_T": np.ascontiguousarray(wr.T),
        "wo_T": np.ascontiguousarray(wo.T),
        "wge_T": np.ascontiguousarray(np.concatenate([wg, we], 0).T),
        "gbias": np.ascontiguousarray(np.concatenate([gb, eb])[:, None]),
    }
    grid = []
    t = 0
    while t < HTOK:
        grid.append((t, min(TC, HTOK - t)))
        t += TC
    nch = len(grid)
    in_maps = []
    for c in range(8):
        bb, h = divmod(c, 2)
        seg = xs[bb, h * TOK:(h + 1) * TOK]
        halo = xs[bb, h * TOK - 1][None] if h == 1 else np.zeros((1, D), f)
        shard = np.concatenate([halo, seg], 0).T
        sh3 = shard.reshape(DT, 128, HTOK).transpose(1, 0, 2)
        xblk = np.zeros((nch, 128, DT, TC), f)
        for ci, (t0, w) in enumerate(grid):
            xblk[ci, :, :, 0:w] = sh3[:, :, t0:t0 + w]
        sel = np.zeros((128, DT), f)
        if h == 1:
            sel[:, c - 1] = 1.0
        m = dict(shared)
        m["x"] = np.ascontiguousarray(xblk.reshape(nch * 128, DT * TC))
        m["tparams"] = np.ascontiguousarray(np.concatenate([tpar_base, sel], axis=1))
        in_maps.append(m)
    return in_maps


_NC_CACHE = {}


def kernel(**inputs):
    in_maps = _prep(inputs)
    key = "main"
    if key not in _NC_CACHE:
        _NC_CACHE[key] = build_program()
    nc = _NC_CACHE[key]
    res = run_bass_kernel_spmd(nc, in_maps, core_ids=list(range(8)))
    out = np.empty((4, 8192, D), np.float32)
    for c in range(8):
        bb, h = divmod(c, 2)
        out[bb, h * TOK:(h + 1) * TOK] = np.asarray(res.results[c]["out"]).T
    return out


# version 26
# speedup vs baseline: 1.2290x; 1.0087x over previous
import numpy as np
import concourse.bass as bass
import concourse.mybir as mybir
from concourse.bass_utils import run_bass_kernel_spmd

F32 = mybir.dt.float32
BF16 = mybir.dt.bfloat16
OP = mybir.AluOpType
AF = mybir.ActivationFunctionType
AX = mybir.AxisListType.X

D = 1024
DT = 8
DH = 2048
HT = 16
NE = 8
TOK = 4096
HTOK = 4097
TC = 368
QW = 1024
BIGM = 30000.0
EPS = 1e-5

ENGS = ("pe", "act", "dve", "pool", "sp")
BLK_ATTR = {"pe": "tensor", "act": "scalar", "dve": "vector", "pool": "gpsimd", "sp": "sync"}
GRAN = 256
CROT = 4
SB_BASE = 16640
SB_LIMIT = 228352

R1 = SB_BASE + 26624
R2 = R1 + 65536
R3 = R2 + 65536
R3SZ = SB_LIMIT - R3
WO_OFF = R3 + 36864


def chunk_grid(total):
    out, t = [], 0
    while t < total:
        out.append((t, min(TC, total - t)))
        t += TC
    return out


class Rec:
    __slots__ = ("eng", "fn", "kind", "chan", "cidx", "ms", "deps", "waits", "gi", "snap", "tiny")

    def __init__(self, eng, fn, kind, chan, cidx, gi):
        self.eng, self.fn, self.kind, self.chan, self.cidx, self.gi = eng, fn, kind, chan, cidx, gi
        self.ms = 0
        self.deps = None
        self.waits = None


class ST:

    def __init__(self, t, off, nbytes):
        self.t = t
        self.off0 = off
        g0, g1 = off // GRAN, (off + nbytes + GRAN - 1) // GRAN
        self.keys = tuple(("s", g) for g in range(g0, g1))

    def __getitem__(self, idx):
        return self.t[idx]

    @property
    def ap(self):
        return self.t[:]


class RegionAlloc:
    def __init__(self, b, base, size, tag):
        self.b, self.base, self.end, self.off, self.tag = b, base, base + size, base, tag

    def t(self, name, shape, dtype):
        esz = 2 if dtype == BF16 else 4
        nb = int(np.prod(shape[1:])) * esz
        off = (self.off + 31) & ~31
        assert off + nb <= self.end, f"{self.tag}:{name} overflow {off}+{nb}>{self.end}"
        self.off = off + nb
        return self.b.sb(f"{self.tag}_{name}", shape, dtype, off=off)


class B:
    def __init__(self, nc):
        self.nc = nc
        self.order = []
        self.per_eng = {e: [] for e in ENGS}
        self.chans = {}
        self.writers = {}
        self.readers = {}
        self.n_coll = 0
        self._names = set()

    def sb(self, name, shape, dtype, off):
        assert name not in self._names, name
        self._names.add(name)
        esz = 2 if dtype == BF16 else 4
        nb = int(np.prod(shape[1:])) * esz
        assert off + nb <= SB_LIMIT, f"sbuf overflow {name}: {off}+{nb}"
        t = self.nc.alloc_sbuf_tensor_at(name, list(shape), dtype, offset=off)
        return ST(t, off, nb)

    def chan(self, name, eng):
        self.chans[name] = {"eng": eng, "n": 0}

    @staticmethod
    def _flat(keys):
        out = []
        for k in keys:
            if isinstance(k, tuple) and k and isinstance(k[0], tuple):
                out.extend(k)
            else:
                out.append(k)
        return out

    def emit(self, eng, fn, r=(), w=(), chan=None, coll=False, tiny=False):
        kind = "coll" if coll else ("dma" if chan else "eng")
        cidx = None
        if chan:
            cidx = self.chans[chan]["n"]
            self.chans[chan]["n"] += 1
        if coll:
            cidx = self.n_coll
            self.n_coll += 1
        rec = Rec(eng, fn, kind, chan, cidx, len(self.order))
        rec.tiny = tiny
        rec.snap = ({e: sum(1 for x in self.per_eng[e] if x.kind != "dma") for e in ENGS},
                    {c: m["n"] for c, m in self.chans.items()}, self.n_coll)
        rset, wset = self._flat(r), self._flat(w)
        deps = set()
        for k in rset:
            p = self.writers.get(k)
            if p is not None:
                deps.add(p)
        for k in wset:
            p = self.writers.get(k)
            if p is not None:
                deps.add(p)
            for q in self.readers.get(k, ()):
                deps.add(q)
        for k in rset:
            self.readers.setdefault(k, []).append(rec)
        for k in wset:
            self.writers[k] = rec
            self.readers[k] = []
        deps.discard(rec)
        rec.deps = deps
        self.order.append(rec)
        self.per_eng[eng].append(rec)
        return rec

    def finalize(self, serial=False):
        if serial:
            ms_no = {e: 0 for e in ENGS}
            eng_ms = {}
            for e in ENGS:
                for rec in self.per_eng[e]:
                    if rec.kind != "dma" and rec.kind != "coll":
                        ms_no[e] += 1
                        rec.ms = ms_no[e]
            cnt = {e: 0 for e in ENGS}
            for rec in self.order:
                waits = {}
                for e2 in ENGS:
                    if e2 != rec.eng and cnt[e2] > 0:
                        waits[("e", e2)] = cnt[e2]
                for c2, n2 in rec.snap[1].items():
                    for r_ in range(CROT):
                        nr = (n2 + CROT - 1 - r_) // CROT
                        if nr > 0:
                            waits[("c", c2, r_)] = 16 * nr
                if rec.snap[2] > 0:
                    waits[("x",)] = rec.snap[2]
                rec.waits = waits
                if rec.kind == "eng" and rec.ms > 0:
                    cnt[rec.eng] = rec.ms
            return ms_no
        for rec in self.order:
            for d in rec.deps:
                if d.kind == "eng" and (d.eng != rec.eng or d.tiny or rec.kind != "eng"):
                    d.ms = -1
        ms_no = {e: 0 for e in ENGS}
        for e in ENGS:
            for rec in self.per_eng[e]:
                if rec.kind == "eng" and rec.ms == -1:
                    ms_no[e] += 1
                    rec.ms = ms_no[e]
        for rec in self.order:
            waits = {}
            for d in rec.deps:
                if d.kind == "eng":
                    if d.eng == rec.eng and rec.kind == "eng" and not d.tiny:
                        continue
                    k = ("e", d.eng)
                    waits[k] = max(waits.get(k, 0), d.ms)
                elif d.kind == "dma":
                    k = ("c", d.chan, d.cidx % CROT)
                    waits[k] = max(waits.get(k, 0), 16 * (d.cidx // CROT + 1))
                else:
                    k = ("x",)
                    waits[k] = max(waits.get(k, 0), d.cidx + 1)
            rec.waits = waits
        return ms_no

    def build_blocks(self, serial=False):
        ms_no = self.finalize(serial=serial)
        nc = self.nc
        esem = {e: nc.alloc_semaphore(f"sem_{e}") for e in ENGS}
        csem = {c: [nc.alloc_semaphore(f"sem_{c}{r}") for r in range(CROT)]
                for c in self.chans}
        xsem = nc.alloc_semaphore("sem_coll")

        def semof(key):
            if key[0] == "e":
                return esem[key[1]]
            if key[0] == "c":
                return csem[key[1]][key[2]]
            return xsem

        with nc.Block() as blk:
            def make(engname, rlist):
                def body(eng):
                    cur = {}
                    for rec in rlist:
                        for key, val in sorted(rec.waits.items()):
                            if cur.get(key, 0) < val:
                                eng.wait_ge(semof(key), val)
                                cur[key] = val
                        inst = rec.fn(eng)
                        if rec.kind == "dma":
                            inst.then_inc(csem[rec.chan][rec.cidx % CROT], 16)
                        elif rec.kind == "coll":
                            inst.then_inc(xsem, 1)
                        elif rec.ms > 0:
                            inst.then_inc(esem[rec.eng], 1)
                    if engname == "sp":
                        for c, meta in self.chans.items():
                            for r in range(CROT):
                                nr = (meta["n"] + CROT - 1 - r) // CROT
                                if nr:
                                    eng.wait_ge(csem[c][r], 16 * nr)
                        if self.n_coll:
                            eng.wait_ge(xsem, self.n_coll)
                        for e2 in ENGS:
                            if e2 != "sp" and ms_no[e2]:
                                eng.wait_ge(esem[e2], ms_no[e2])
                return body

            for e in ENGS:
                getattr(blk, BLK_ATTR[e])(make(e, self.per_eng[e]))



DBG_HSH = [False]
SERIAL_PA1 = [False]
DBG_PD = [False]
FORCE_TINY = [False]


def build_program(debug=(), phases=None, serial=False):
    DBG_HSH[0] = "hsh" in debug
    DBG_PD[0] = "pd" in debug
    nc = bass.Bass("TRN2", target_bir_lowering=False)
    b = B(nc)
    b.chan("cin", "sp")
    b.chan("cout", "act")
    b.chan("cw", "pool")
    b.chan("crow", "sp")
    b.chan("csc", "act")
    b.chan("cscw", "act")
    b.chan("cxb", "sp")
    b.chan("cxl", "sp")
    b.chan("cpk", "pool")
    b.chan("cpv", "pool")
    b.chan("cpr", "pool")

    def din(name, shape, dtype=F32):
        return nc.dram_tensor(name, list(shape), dtype, kind="ExternalInput").ap()

    NCH = len(chunk_grid(HTOK))
    x_in = din("x", [NCH * 128, DT * TC])
    w_in_T = din("w_in_T", [D, 2 * DH])
    w1_T = din("w1_T", [D, 2 * DH])
    w_out_T = din("w_out_T", [DH, D])
    w2_T = din("w2_T", [NE * DH, D])
    wk_T = din("wk_T", [D, D])
    wv_T = din("wv_T", [D, D])
    wr_T = din("wr_T", [D, D])
    wo_T = din("wo_T", [D, D])
    wge_T = din("wge_T", [D, 10])
    gb_in = din("gbias", [10, 1])
    tpar_in = din("tparams", [128, 6 * DT])

    out_d = nc.dram_tensor("out", [D, TOK], F32, kind="ExternalOutput").ap()

    def idram(name, shape, dtype=F32):
        return nc.dram_tensor(name, list(shape), dtype).ap()

    xln_d = idram("xln_d", [NCH * 128, DT * TC], BF16)
    hrt_d = idram("hrt_d", [NCH * 128, HT * TC], BF16)
    pA_d = idram("pA_d", [NCH * 128, DT * TC])
    pB_d = idram("pB_d", [NCH * 128, DT * TC])
    xr2_d = idram("xr2_d", [NCH * 128, DT * TC])
    wse_d = idram("wse_d", [NE, HTOK])
    wsehl_d = idram("wsehl_d", [4 * NCH, 4 * TC], BF16)
    wsehl_v = wsehl_d.rearrange("(p c) (e t) -> p c e t", c=NCH, e=4)
    xr1_d = idram("xr1_d", [D, HTOK])
    k_d = idram("k_d", [D, TOK])
    v_d = idram("v_d", [D, TOK])
    rs_d = idram("rs_d", [D, TOK])
    as_d = idram("as_d", [D, TOK])
    bs_d = idram("bs_d", [D, TOK])
    hsh_d = idram("hsh_d", [DH, HTOK], BF16)
    dcb_d = idram("dcb_d", [D, QW])
    pt_d = idram("pt_d", [D, QW])
    ag_in_d = idram("ag_in", [128, 16])
    ag_out_d = idram("ag_out", [8 * 128, 16])

    def jv(ap):
        return ap.rearrange("(j p) t -> p j t", p=128)

    def blk(ap, c):
        return ap[c * 128:(c + 1) * 128, :]

    def blkm(ap, c, m, w):
        return ap[c * 128:(c + 1) * 128, m * TC:m * TC + w]

    xr1vw = jv(xr1_d)
    outvw = jv(out_d)
    kvw = jv(k_d)
    vvw = jv(v_d)
    rsvw = jv(rs_d)

    PS = [nc.alloc_psum_tensor(f"psb{i}", [128, 512], F32) for i in range(8)]

    def pkk(i):
        return ("p", i)

    CA = RegionAlloc(b, SB_BASE, R1 - SB_BASE, "c")
    idn = CA.t("idn", [128, 128], F32)
    ones_k = CA.t("ones_k", [128, 1], BF16)
    ones_r = CA.t("ones_r", [1, 128], BF16)
    tpar = CA.t("tpar", [128, 6 * DT], F32)
    tmix = b.sb("tmix", [128, 3 * DT], F32, off=tpar.off0)
    tdec = b.sb("tdec", [128, DT], F32, off=tpar.off0 + 96)
    tfir = b.sb("tfir", [128, DT], F32, off=tpar.off0 + 128)
    selv = b.sb("selv", [128, DT], F32, off=tpar.off0 + 160)
    etd = CA.t("etd", [128, DT], F32)
    netd = CA.t("netd", [128, DT], F32)
    decC = CA.t("dec", [128, DT], F32)
    wge_f = CA.t("wge_f", [128, DT, 10], F32)
    wge_hi = CA.t("wge_hi", [128, DT, 10], BF16)
    wge_lo = CA.t("wge_lo", [128, DT, 10], BF16)
    gbias = CA.t("gbias", [10, 1], F32)
    SAB1 = CA.t("SAB1", [128, 16], F32)
    SAB2 = CA.t("SAB2", [128, 16], F32)
    stage = CA.t("stage", [128, 16], F32)
    gsb = CA.t("gsb", [128, DT, 16], F32)
    carry_a = CA.t("carry_a", [128, DT], F32)
    carry_b = CA.t("carry_b", [128, DT], F32)
    pcol = CA.t("pcol", [128, DT], F32)
    rowA = CA.t("rowA", [1, 512], F32)
    rowB = CA.t("rowB", [1, 512], F32)
    rowC = CA.t("rowC", [1, 512], F32)
    rowD = CA.t("rowD", [1, 512], F32)
    rowE = CA.t("rowE", [1, 512], F32)
    rhiA = CA.t("rhiA", [1, 512], BF16)
    rloA = CA.t("rloA", [1, 512], BF16)
    rhiB = CA.t("rhiB", [1, 512], BF16)
    rloB = CA.t("rloB", [1, 512], BF16)
    glog = CA.t("glog", [10, 512], F32)
    wse8 = CA.t("wse8", [8, 512], F32)
    ones8 = CA.t("ones8", [128, 8], F32)
    wse8h = CA.t("wse8h", [8, 512], BF16)
    wse8l = CA.t("wse8l", [8, 512], BF16)
    assert CA.off <= R1, CA.off
    pbs0 = rowA.off0
    pb_pin = [b.sb(f"pbs_pin{i}", [128, TC], F32, off=pbs0 + i * 1536) for i in range(3)]
    pb_wb = [[b.sb(f"pbs_wb{p}{e}", [128, TC], F32, off=pbs0 + 4608 + (2 * p + e) * 1536)
              for e in range(2)] for p in range(2)]
    pb_rows4 = b.sb("pbs_rows4", [1, 4, TC], BF16, off=pbs0 + 10752)
    assert pbs0 + 10752 + 4 * TC * 2 <= rowA.off0 + 16384
    tc1 = b.sb("tc1", [128, 128], F32, off=R3)
    tc2 = b.sb("tc2", [128, 128], F32, off=R3 + 512)

    def wslot(call):
        return (R1, R2) if call == 0 else (R2, R1)

    w1_sb, w_in_sb, w_out_sb, w2_sb = {}, {}, {}, {}
    for call in (0, 1):
        sA, sB = wslot(call)
        w1_sb[call] = b.sb(f"w1s{call}", [128, DT, 2 * DH], BF16, off=sA)
        w_in_sb[call] = b.sb(f"wins{call}", [128, DT, 2 * DH], BF16, off=sB)
        w_out_sb[call] = b.sb(f"wouts{call}", [128, HT, D], BF16, off=sA)
        for q in range(4):
            off = sB if q % 2 == 0 else sA
            w2_sb[(call, q)] = b.sb(f"w2s{call}_{q}", [128, 2, HT, D], BF16, off=off)
    wk_sb = b.sb("wk_sb", [128, DT, D], BF16, off=R2)
    wv_sb = b.sb("wv_sb", [128, DT, D], BF16, off=R2 + 16384)
    wr_sb = b.sb("wr_sb", [128, DT, D], BF16, off=R2 + 32768)
    rwb = b.sb("rwb", [128, DT, TOK], BF16, off=R1)
    wo_sb = b.sb("wo_sb", [128, DT, D], BF16, off=WO_OFF)

    def dma(chan, dst, src, r=(), w=(), slow=False):
        eng = b.chans[chan]["eng"]

        def fn(e):
            if slow:
                return e.dma_start(dst, src, allow_slow_non_contiguous=True)
            return e.dma_start(dst, src)
        return b.emit(eng, fn, r=r, w=w, chan=chan)

    def op(eng, fn, r=(), w=(), tiny=False):
        return b.emit(eng, fn, r=r, w=w, tiny=tiny or FORCE_TINY[0])

    def mm(bank, out_ap, passes, reads, start=True, stop=True):
        def fn(eng):
            n = len(passes)
            inst = None
            for i, (l, rr) in enumerate(passes):
                inst = eng.matmul(out_ap, l, rr, start=(start and i == 0),
                                  stop=(stop and i == n - 1))
            return inst
        b.emit("pe", fn, r=reads, w=(pkk(bank),))

    def emit_ln(srcT, w, xbf, xc, tmpT, dstT,
                bf_out=None, lo=None, rows2=None):
        src3 = srcT.t[:, :, 0:w]
        src_keys = srcT.keys
        tmp3 = tmpT.t[:, :, 0:w]
        tmp_keys = tmpT.keys
        dst3 = dstT.t[:, :, 0:w]
        dst_keys = dstT.keys
        src2 = srcT.t[:, :, :].rearrange("p j t -> p (j t)")
        op("act", lambda e: e.copy(xbf.ap, src2), r=(src_keys,), w=(xbf.keys,))
        op("dve", lambda e: e.tensor_tensor(xc.ap, src2, src2, OP.mult),
           r=(src_keys,), w=(xc.keys,))
        xbf3 = xbf.t[:, :].rearrange("p (j t) -> p j t", j=DT)
        xc3 = xc.t[:, :].rearrange("p (j t) -> p j t", j=DT)
        xbf3 = xbf3[:, :, 0:w]
        xc3 = xc3[:, :, 0:w]
        mm(4, PS[4][0:1, 0:w], [(ones_k.ap, xbf3[:, j, :]) for j in range(DT)], (xbf.keys, ones_k.keys))
        mm(5, PS[5][0:1, 0:w], [(ones_k.ap, xc3[:, j, :]) for j in range(DT)], (xc.keys, ones_k.keys))
        op("dve", lambda e: e.tensor_scalar(rowA[0:1, 0:w], PS[4][0:1, 0:w], 1.0 / D, None, OP.mult),
           r=(pkk(4),), w=(rowA.keys,), tiny=True)
        op("dve", lambda e: e.tensor_scalar(rowB[0:1, 0:w], PS[5][0:1, 0:w], 1.0 / D, EPS, OP.mult, OP.add),
           r=(pkk(5),), w=(rowB.keys,), tiny=True)
        op("dve", lambda e: e.tensor_tensor(rowC[0:1, 0:w], rowA[0:1, 0:w], rowA[0:1, 0:w], OP.mult),
           r=(rowA.keys,), w=(rowC.keys,), tiny=True)
        op("dve", lambda e: e.tensor_tensor(rowD[0:1, 0:w], rowB[0:1, 0:w], rowC[0:1, 0:w], OP.subtract),
           r=(rowB.keys, rowC.keys), w=(rowD.keys,), tiny=True)
        op("act", lambda e: e.activation(rowC[0:1, 0:w], rowD[0:1, 0:w], AF.Sqrt),
           r=(rowD.keys,), w=(rowC.keys,), tiny=True)
        op("dve", lambda e: e.reciprocal(rowE[0:1, 0:w], rowC[0:1, 0:w]),
           r=(rowC.keys,), w=(rowE.keys,), tiny=True)
        op("dve", lambda e: e.tensor_tensor(rowD[0:1, 0:w], rowA[0:1, 0:w], rowE[0:1, 0:w], OP.mult),
           r=(rowA.keys, rowE.keys), w=(rowD.keys,), tiny=True)
        op("act", lambda e: e.copy(rhiA[0:1, 0:w], rowE[0:1, 0:w]), r=(rowE.keys,), w=(rhiA.keys,), tiny=True)
        op("dve", lambda e: e.tensor_tensor(rloA[0:1, 0:w], rowE[0:1, 0:w], rhiA[0:1, 0:w], OP.subtract),
           r=(rowE.keys, rhiA.keys), w=(rloA.keys,), tiny=True)
        op("act", lambda e: e.copy(rhiB[0:1, 0:w], rowD[0:1, 0:w]), r=(rowD.keys,), w=(rhiB.keys,), tiny=True)
        op("dve", lambda e: e.tensor_tensor(rloB[0:1, 0:w], rowD[0:1, 0:w], rhiB[0:1, 0:w], OP.subtract),
           r=(rowD.keys, rhiB.keys), w=(rloB.keys,), tiny=True)
        mm(6, PS[6][:, 0:w], [(ones_r.ap, rhiA[0:1, 0:w]), (ones_r.ap, rloA[0:1, 0:w])],
           (rhiA.keys, rloA.keys, ones_r.keys))
        mm(7, PS[7][:, 0:w], [(ones_r.ap, rhiB[0:1, 0:w]), (ones_r.ap, rloB[0:1, 0:w])],
           (rhiB.keys, rloB.keys, ones_r.keys))
        if rows2 is not None:
            rstd_t, mr_t = rows2
            op("act", lambda e: e.copy(rstd_t.t[:, 0:w], PS[6][:, 0:w]),
               r=(pkk(6),), w=(rstd_t.keys,))
            op("act", lambda e: e.copy(mr_t.t[:, 0:w], PS[7][:, 0:w]),
               r=(pkk(7),), w=(mr_t.keys,))
            for j in range(DT):
                op("pool", lambda e, j=j: e.tensor_tensor(tmp3[:, j, :], src3[:, j, :], rstd_t.t[:, 0:w], OP.mult),
                   r=(src_keys, rstd_t.keys), w=(tmp_keys,))
            for j in range(DT):
                op("dve", lambda e, j=j: e.tensor_tensor(dst3[:, j, :], tmp3[:, j, :], mr_t.t[:, 0:w], OP.subtract),
                   r=(tmp_keys, mr_t.keys), w=(dst_keys,))
        else:
            for j in range(DT):
                op("dve", lambda e, j=j: e.tensor_tensor(tmp3[:, j, :], src3[:, j, :], PS[6][:, 0:w], OP.mult),
                   r=(src_keys, pkk(6)), w=(tmp_keys,))
            for j in range(DT):
                op("dve", lambda e, j=j: e.tensor_tensor(dst3[:, j, :], tmp3[:, j, :], PS[7][:, 0:w], OP.subtract),
                   r=(tmp_keys, pkk(7)), w=(dst_keys,))
        if bf_out is not None:
            dst2 = dstT.t[:, :, :].rearrange("p j t -> p (j t)")
            op("act", lambda e: e.copy(bf_out.ap, dst2), r=(dst_keys,), w=(bf_out.keys,))
        if lo is not None:
            bf3 = bf_out.t[:, :].rearrange("p (j t) -> p j t", j=DT)
            lo3 = lo.t[:, :].rearrange("p (j t) -> p j t", j=DT)
            for j in range(DT):
                op("dve", lambda e, j=j: e.tensor_tensor(lo3[:, j, 0:w], dst3[:, j, :], bf3[:, j, 0:w], OP.subtract),
                   r=(dst_keys, bf_out.keys), w=(lo.keys,))

    def emit_gating(t0, w, xbf3, xbf_keys, xlo3, xlo_keys, gsc):
        wgehi3 = wge_hi.t[:, :, :]
        wgelo3 = wge_lo.t[:, :, :]
        passes = [(wgehi3[:, j, :], xbf3[:, j, :]) for j in range(DT)]
        passes += [(wgehi3[:, j, :], xlo3[:, j, :]) for j in range(DT)]
        passes += [(wgelo3[:, j, :], xbf3[:, j, :]) for j in range(DT)]
        mm(4, PS[4][0:10, 0:w], passes, (wge_hi.keys, wge_lo.keys, xbf_keys, xlo_keys))
        op("dve", lambda e: e.tensor_scalar(glog[0:10, 0:w], PS[4][0:10, 0:w], gbias[0:10, 0:1], None, OP.add),
           r=(pkk(4), gbias.keys), w=(glog.keys,), tiny=True)
        nsub = (w + 127) // 128
        subs = []
        for s in range(nsub):
            tw = min(128, w - 128 * s)
            subs.append((s, tw, slice(128 * s, 128 * s + tw)) + tuple(gsc[s]))

        def step(emitter):
            for args in subs:
                emitter(*args)

        for (s, tw, sl, gt, gco, gm, gex_, gpx_, gz_, wtm) in subs:
            op("pe", lambda e, sl=sl, tw=tw: e.transpose(PS[5][0:tw, 0:10], glog[0:10, sl], idn[0:10, 0:10]),
               r=(glog.keys, idn.keys), w=(pkk(5),))
            op("dve", lambda e, tw=tw, gt=gt: e.tensor_copy(gt[0:tw, 0:10], PS[5][0:tw, 0:10]),
               r=(pkk(5),), w=(gt.keys,), tiny=True)
        step(lambda s, tw, sl, gt, gco, gm, gex_, gpx_, gz_, wtm:
             op("dve", lambda e, tw=tw, gt=gt, gco=gco: e.tensor_tensor(gco[0:tw, 0:1], gt[0:tw, 0:1], gt[0:tw, 1:2], OP.subtract),
                r=(gt.keys,), w=(gco.keys,), tiny=True))
        step(lambda s, tw, sl, gt, gco, gm, gex_, gpx_, gz_, wtm:
             op("dve", lambda e, tw=tw, gco=gco: e.scalar_tensor_tensor(gco[0:tw, 1:2], gco[0:tw, 0:1], -1.0, gco[0:tw, 0:1], OP.mult, OP.max),
                r=(gco.keys,), w=(gco.keys,), tiny=True))
        step(lambda s, tw, sl, gt, gco, gm, gex_, gpx_, gz_, wtm:
             op("act", lambda e, tw=tw, gco=gco: e.activation(gco[0:tw, 2:3], gco[0:tw, 1:2], AF.Sigmoid),
                r=(gco.keys,), w=(gco.keys,), tiny=True))
        step(lambda s, tw, sl, gt, gco, gm, gex_, gpx_, gz_, wtm:
             op("dve", lambda e, tw=tw, gco=gco: e.tensor_scalar(gco[0:tw, 4:5], gco[0:tw, 0:1], 0.0, None, OP.is_lt),
                r=(gco.keys,), w=(gco.keys,), tiny=True))
        step(lambda s, tw, sl, gt, gco, gm, gex_, gpx_, gz_, wtm:
             op("dve", lambda e, tw=tw, gco=gco, gm=gm: e.tensor_scalar(gm[0:tw, 0:4], ones8[0:tw, 0:4], gco[0:tw, 4:5], -1.0, OP.mult, OP.mult),
                r=(ones8.keys, gco.keys), w=(gm.keys,), tiny=True))
        step(lambda s, tw, sl, gt, gco, gm, gex_, gpx_, gz_, wtm:
             op("dve", lambda e, tw=tw, gco=gco, gm=gm: e.tensor_scalar(gm[0:tw, 4:8], ones8[0:tw, 0:4], gco[0:tw, 4:5], -1.0, OP.subtract, OP.mult),
                r=(ones8.keys, gco.keys), w=(gm.keys,), tiny=True))
        step(lambda s, tw, sl, gt, gco, gm, gex_, gpx_, gz_, wtm:
             op("dve", lambda e, tw=tw, gt=gt, gm=gm, gz_=gz_: e.scalar_tensor_tensor(gz_[0:tw, 0:8], gm[0:tw, 0:8], BIGM, gt[0:tw, 2:10], OP.mult, OP.add),
                r=(gm.keys, gt.keys), w=(gz_.keys,), tiny=True))
        step(lambda s, tw, sl, gt, gco, gm, gex_, gpx_, gz_, wtm:
             op("dve", lambda e, tw=tw, gco=gco, gz_=gz_: e.reduce_max(gco[0:tw, 5:6], gz_[0:tw, 0:8], AX, negate=True),
                r=(gz_.keys,), w=(gco.keys,), tiny=True))
        step(lambda s, tw, sl, gt, gco, gm, gex_, gpx_, gz_, wtm:
             op("act", lambda e, tw=tw, gco=gco, gex_=gex_, gz_=gz_: e.activation(gex_[0:tw, 0:8], gz_[0:tw, 0:8], AF.Exp, bias=gco[0:tw, 5:6]),
                r=(gz_.keys, gco.keys), w=(gex_.keys,), tiny=True))
        step(lambda s, tw, sl, gt, gco, gm, gex_, gpx_, gz_, wtm:
             op("dve", lambda e, tw=tw, gco=gco, gex_=gex_: e.reduce_sum(gco[0:tw, 6:7], gex_[0:tw, 0:8], AX),
                r=(gex_.keys,), w=(gco.keys,), tiny=True))
        step(lambda s, tw, sl, gt, gco, gm, gex_, gpx_, gz_, wtm:
             op("dve", lambda e, tw=tw, gco=gco: e.reciprocal(gco[0:tw, 7:8], gco[0:tw, 6:7]),
                r=(gco.keys,), w=(gco.keys,), tiny=True))
        step(lambda s, tw, sl, gt, gco, gm, gex_, gpx_, gz_, wtm:
             op("dve", lambda e, tw=tw, gco=gco: e.tensor_tensor(gco[0:tw, 4:5], gco[0:tw, 2:3], gco[0:tw, 7:8], OP.mult),
                r=(gco.keys,), w=(gco.keys,), tiny=True))
        step(lambda s, tw, sl, gt, gco, gm, gex_, gpx_, gz_, wtm:
             op("dve", lambda e, tw=tw, gco=gco, gex_=gex_, gpx_=gpx_: e.tensor_scalar(gpx_[0:tw, 0:8], gex_[0:tw, 0:8], gco[0:tw, 4:5], None, OP.mult),
                r=(gex_.keys, gco.keys), w=(gpx_.keys,), tiny=True))
        step(lambda s, tw, sl, gt, gco, gm, gex_, gpx_, gz_, wtm:
             op("dve", lambda e, tw=tw, gco=gco, gpx_=gpx_: e.reduce_max(gco[0:tw, 0:1], gpx_[0:tw, 0:8], AX),
                r=(gpx_.keys,), w=(gco.keys,), tiny=True))
        step(lambda s, tw, sl, gt, gco, gm, gex_, gpx_, gz_, wtm:
             op("dve", lambda e, tw=tw, gco=gco, gex_=gex_, gpx_=gpx_: e.tensor_scalar(gex_[0:tw, 0:8], gpx_[0:tw, 0:8], gco[0:tw, 0:1], None, OP.is_equal),
                r=(gpx_.keys, gco.keys), w=(gex_.keys,), tiny=True))
        step(lambda s, tw, sl, gt, gco, gm, gex_, gpx_, gz_, wtm:
             op("dve", lambda e, tw=tw, gex_=gex_, gpx_=gpx_, gz_=gz_: e.scalar_tensor_tensor(gz_[0:tw, 0:8], gex_[0:tw, 0:8], -BIGM, gpx_[0:tw, 0:8], OP.mult, OP.add),
                r=(gex_.keys, gpx_.keys), w=(gz_.keys,), tiny=True))
        step(lambda s, tw, sl, gt, gco, gm, gex_, gpx_, gz_, wtm:
             op("dve", lambda e, tw=tw, gco=gco, gz_=gz_: e.reduce_max(gco[0:tw, 1:2], gz_[0:tw, 0:8], AX),
                r=(gz_.keys,), w=(gco.keys,), tiny=True))
        step(lambda s, tw, sl, gt, gco, gm, gex_, gpx_, gz_, wtm:
             op("dve", lambda e, tw=tw, gco=gco, gpx_=gpx_, wtm=wtm: e.scalar_tensor_tensor(wtm[0:tw, 0:8], gpx_[0:tw, 0:8], gco[0:tw, 1:2], gpx_[0:tw, 0:8], OP.is_ge, OP.mult),
                r=(gpx_.keys, gco.keys), w=(wtm.keys,), tiny=True))
        for (s, tw, sl, gt, gco, gm, gex_, gpx_, gz_, wtm) in subs:
            op("pe", lambda e, tw=tw, wtm=wtm: e.transpose(PS[6][0:8, 0:tw], wtm[0:tw, 0:8], idn[0:tw, 0:tw]),
               r=(wtm.keys, idn.keys), w=(pkk(6),))
            op("dve", lambda e, s=s, tw=tw: e.tensor_copy(wse8[0:8, 128 * s:128 * s + tw], PS[6][0:8, 0:tw]),
               r=(pkk(6),), w=(wse8.keys,), tiny=True)
        dma("crow", wse_d[0:NE, t0:t0 + w], wse8[0:8, 0:w], r=(wse8.keys,), w=(("d", "wse", t0),))
        op("act", lambda e, w=w: e.copy(wse8h[0:8, 0:w], wse8[0:8, 0:w]),
           r=(wse8.keys,), w=(wse8h.keys,), tiny=True)
        op("dve", lambda e, w=w: e.tensor_tensor(wse8l[0:8, 0:w], wse8[0:8, 0:w], wse8h[0:8, 0:w], OP.subtract),
           r=(wse8.keys, wse8h.keys), w=(wse8l.keys,), tiny=True)
        ci_g = t0 // TC
        for p_ in range(4):
            dma("crow", wsehl_v[p_, ci_g, 0:2, 0:w], wse8h[2 * p_:2 * p_ + 2, 0:w],
                r=(wse8h.keys,), w=(("d", "wsehl", t0, p_, 0),))
            dma("crow", wsehl_v[p_, ci_g, 2:4, 0:w], wse8l[2 * p_:2 * p_ + 2, 0:w],
                r=(wse8l.keys,), w=(("d", "wsehl", t0, p_, 1),))
    def emit_swiglu(wsb, xbf3, xbf_keys, hbuf3, h_keys, w, scr):
        for i in range(HT):
            ba = (2 * i) % 4
            bb = ba + 1
            mm(ba, PS[ba][:, 0:w],
               [(wsb.t[:, kt, 128 * i:128 * (i + 1)], xbf3[:, kt, :]) for kt in range(DT)],
               (wsb.keys, xbf_keys))
            mm(bb, PS[bb][:, 0:w],
               [(wsb.t[:, kt, 128 * (HT + i):128 * (HT + i + 1)], xbf3[:, kt, :]) for kt in range(DT)],
               (wsb.keys, xbf_keys))
            op("act", lambda e, ba=ba: e.activation(scr.t[:, 0:w], PS[ba][:, 0:w], AF.Silu),
               r=(pkk(ba),), w=(scr.keys,))
            op("dve", lambda e, bb=bb, i=i: e.tensor_tensor(hbuf3[:, i, :], scr.t[:, 0:w], PS[bb][:, 0:w], OP.mult),
               r=(scr.keys, pkk(bb)), w=(h_keys,))

    grid_h = chunk_grid(HTOK)
    grid_t = chunk_grid(TOK)
    PCW = 420
    pc_grid = [(t, min(PCW, HTOK - t)) for t in range(0, HTOK, PCW)]

    dma("crow", tpar.ap, tpar_in, w=(tpar.keys,))
    dma("crow", wge_f.t[:, :, :], wge_T.rearrange("(j p) n -> p j n", p=128), w=(wge_f.keys,))
    dma("crow", gbias.ap, gb_in, w=(gbias.keys,))
    op("pool", lambda e: e.iota(tc1.ap, [[1, 128]], channel_multiplier=0,
                                allow_small_or_imprecise_dtypes=True), w=(tc1.keys,))
    op("pool", lambda e: e.iota(tc2.ap, [[0, 128]], channel_multiplier=1,
                                allow_small_or_imprecise_dtypes=True), w=(tc2.keys,))
    op("dve", lambda e: e.tensor_tensor(idn.ap, tc1.ap, tc2.ap, OP.is_equal),
       r=(tc1.keys, tc2.keys), w=(idn.keys,))
    op("dve", lambda e: e.memset(ones_k.ap, 1.0), w=(ones_k.keys,), tiny=True)
    op("dve", lambda e: e.memset(ones_r.ap, 1.0), w=(ones_r.keys,), tiny=True)
    op("dve", lambda e: e.memset(ones8.ap, 1.0), w=(ones8.keys,), tiny=True)
    op("dve", lambda e: e.memset(pcol.ap, 0.0), w=(pcol.keys,), tiny=True)
    op("dve", lambda e: e.memset(carry_a.ap, 0.0), w=(carry_a.keys,), tiny=True)
    op("dve", lambda e: e.memset(carry_b.ap, 0.0), w=(carry_b.keys,), tiny=True)
    op("act", lambda e: e.activation(etd.ap, tdec.ap, AF.Exp), r=(tdec.keys,), w=(etd.keys,), tiny=True)
    op("dve", lambda e: e.tensor_scalar(netd.ap, etd.ap, -1.0, None, OP.mult),
       r=(etd.keys,), w=(netd.keys,), tiny=True)
    op("act", lambda e: e.activation(decC.ap, etd.ap, AF.Exp, scale=-1.0),
       r=(etd.keys,), w=(decC.keys,), tiny=True)
    op("act", lambda e: e.copy(wge_hi.t[:, :, :], wge_f.t[:, :, :]), r=(wge_f.keys,), w=(wge_hi.keys,), tiny=True)
    for j in range(DT):
        op("dve", lambda e, j=j: e.tensor_tensor(wge_lo.t[:, j, :], wge_f.t[:, j, :], wge_hi.t[:, j, :], OP.subtract),
           r=(wge_f.keys, wge_hi.keys), w=(wge_lo.keys,), tiny=True)

    def wload3(sbT, src, kt, m, tag):
        dma("cw", sbT.t[:, :, :], src.rearrange("(kt p) m -> p kt m", p=128),
            w=(sbT.keys,))

    wload3(w1_sb[0], w1_T, DT, 2 * DH, "w1_0")

    def emit_PA2(call):
        srcB = x_in if call == 0 else xr2_d
        stag = "x" if call == 0 else "xr2"
        grid = grid_h if call == 0 else grid_t
        A = RegionAlloc(b, R3, R3SZ, f"a2{call}")
        xb = A.t("xb", [128, DT, TC], F32)
        tmp = A.t("tmp", [128, DT, TC], F32)
        xbf = A.t("xbf", [128, DT * TC], BF16)
        xc = A.t("xc", [128, DT * TC], BF16)
        scr = A.t("scr", [128, TC], F32)
        hb = b.sb(f"a2{call}_hb", [128, HT, TC], BF16, off=tmp.off0)
        A.off = (A.off + 255) & ~255
        gsc = [tuple(A.t(f"g{nm}{s}", [128, 64], F32)
                     for nm in ("t", "c", "m", "e", "p", "z", "w"))
               for s in range(3)]
        lnr = (A.t("rstd", [128, TC], F32), A.t("mrow", [128, TC], F32))
        for ci, (t0, w) in enumerate(grid):
            dma("cin", xb.t[:, :, :].rearrange("p j t -> p (j t)"), blk(srcB, ci),
                r=(("d", stag, t0),), w=(xb.keys,))
            emit_ln(xb, w, xbf, xc, tmp, xb, bf_out=xbf, lo=xc)
            xbf3 = xbf.t[:, :].rearrange("p (j t) -> p j t", j=DT)[:, :, 0:w]
            xc3 = xc.t[:, :].rearrange("p (j t) -> p j t", j=DT)[:, :, 0:w]
            dma("cxl", blk(xln_d, ci), xbf.ap, r=(xbf.keys,), w=(("d", "xln", t0),))
            emit_gating(t0, w, xbf3, xbf.keys, xc3, xc.keys, gsc)
            emit_swiglu(w1_sb[call], xbf3, xbf.keys, hb.t[:, :, 0:w], hb.keys, w, scr)
            dma("cin", blk(hrt_d, ci), hb.t[:, :, :].rearrange("p j t -> p (j t)"),
                r=(hb.keys,), w=(("d", "hrt", t0),))
        wload3(w_in_sb[call], w_in_T, DT, 2 * DH, f"win{call}")

    def emit_PA1(call):
        grid = grid_h if call == 0 else grid_t
        A = RegionAlloc(b, R3, R3SZ, f"a1{call}")
        xlnb = A.t("xlnb", [128, DT, TC], BF16)
        xlnb2 = A.t("xlnb2", [128, DT, TC], BF16)
        hb = A.t("hb", [128, HT, TC], BF16)
        scr = A.t("scr", [128, TC], F32)
        pob = A.t("pob", [128, DT, TC], F32)
        wload3(w_out_sb[call], w_out_T, HT, D, f"wo{call}")
        bufs = (xlnb, xlnb2)
        prev_key = None
        for ci, (t0, w) in enumerate(grid):
            xl = bufs[ci % 2]
            rk = (("d", "xln", t0),) if (prev_key is None or not SERIAL_PA1[0]) else (("d", "xln", t0), prev_key)
            dma("cxl", xl.t[:, :, :].rearrange("p j t -> p (j t)"), blk(xln_d, ci), r=rk, w=(xl.keys,))
            prev_key = ("d", "pA", t0)
            xbf3 = xl.t[:, :, 0:w]
            emit_swiglu(w_in_sb[call], xbf3, xl.keys, hb.t[:, :, 0:w], hb.keys, w, scr)
            if call == 0 and DBG_HSH[0]:
                dma("cout", hsh_d.rearrange("(j p) t -> p j t", p=128)[:, :, t0:t0 + w],
                    hb.t[:, :, 0:w], r=(hb.keys,), w=(("d", "hsh", t0),))
            for m in range(DT):
                bank = m % 4
                mm(bank, PS[bank][:, 0:w],
                   [(w_out_sb[call].t[:, kt, 128 * m:128 * (m + 1)], hb.t[:, kt, 0:w]) for kt in range(HT)],
                   (w_out_sb[call].keys, hb.keys))
                op("dve", lambda e, bank=bank, m=m, w=w: e.tensor_copy(pob.t[:, m, 0:w], PS[bank][:, 0:w]),
                   r=(pkk(bank),), w=(pob.keys,))
            dma("cin", blk(pA_d, ci), pob.t[:, :, :].rearrange("p j t -> p (j t)"),
                r=(pob.keys,), w=(("d", "pA", t0),))

    HRB = HT * TC * 2
    assert HRB % 256 == 0
    pb_hrtb = [b.sb(f"pb_hrt{i}", [128, HT, TC], BF16, off=R3 + i * HRB) for i in range(2)]
    he0b = b.sb("pb_he0", [128, HT, TC], BF16, off=R3 + 2 * HRB)
    he1b = b.sb("pb_he1", [128, HT, TC], BF16, off=R3 + 3 * HRB)
    TCB = 1536
    pb_xq = [b.sb(f"pb_xq{i}", [128, TC], F32, off=R3 + 4 * HRB + i * TCB) for i in range(2)]
    pb_po = [b.sb(f"pb_po{i}", [128, TC], F32, off=R3 + 4 * HRB + (2 + i) * TCB) for i in range(2)]
    assert R3 + 4 * HRB + 4 * TCB <= SB_LIMIT

    def emit_PB_call(call):
        grid = grid_h if call == 0 else grid_t
        stag = "x" if call == 0 else "xr2"
        xsrcB = x_in if call == 0 else xr2_d
        n = len(grid)
        steps = [(q, ci) for q in range(4) for ci in range(n)]
        N = len(steps)

        def qinfo(q):
            pin_b = (pA_d, pB_d, pA_d, pB_d)[q]
            pint = ("pA", "pB", "pA", "pB")[q]
            if q < 3:
                pout_b = (pB_d, pA_d, pB_d)[q]
                poutt = ("pB", "pA", "pB")[q]
            else:
                pout_b = xr1vw if call == 0 else outvw
                poutt = "xr1" if call == 0 else "out"
            return pin_b, pint, pout_b, poutt

        def load_hrt(s):
            q, ci = steps[s]
            t0, w = grid[ci]
            dma("cin", pb_hrtb[s % 2].t[:, :, :].rearrange("p j t -> p (j t)"), blk(hrt_d, ci),
                r=(("d", "hrt", t0),), w=(pb_hrtb[s % 2].keys,))

        def rows_and_bcast(s):
            q, ci = steps[s]
            t0, w = grid[ci]
            p = s % 2
            row = q * NCH + ci
            dma("crow", pb_rows4.t[0:1, :, :].rearrange("p e t -> p (e t)"),
                wsehl_d[row:row + 1, :],
                r=(("d", "wsehl", t0, q, 0), ("d", "wsehl", t0, q, 1)),
                w=(pb_rows4.keys,))
            for ei in range(2):
                bank = 6 + ei
                mm(bank, PS[bank][:, 0:w],
                   [(ones_r.ap, pb_rows4.t[0:1, ei, 0:w]), (ones_r.ap, pb_rows4.t[0:1, 2 + ei, 0:w])],
                   (pb_rows4.keys, ones_r.keys))
                op("act", lambda e, bank=bank, p=p, ei=ei, w=w:
                   e.copy(pb_wb[p][ei].t[:, 0:w], PS[bank][:, 0:w]),
                   r=(pkk(bank),), w=(pb_wb[p][ei].keys,))

        def emit_he(s, ei):
            q, ci = steps[s]
            t0, w = grid[ci]
            p = s % 2
            heX = (he0b, he1b)[ei]
            hr = pb_hrtb[p]
            wbX = pb_wb[p][ei]
            eng = "dve" if ei == 0 else "pool"
            for ht in range(HT):
                op(eng, lambda e, ht=ht, hr=hr, wbX=wbX, heX=heX, w=w:
                   e.tensor_tensor(heX.t[:, ht, 0:w], hr.t[:, ht, 0:w], wbX.t[:, 0:w], OP.mult),
                   r=(hr.keys, wbX.keys), w=(heX.keys,))

        load_hrt(0)
        rows_and_bcast(0)
        emit_he(0, 0)
        emit_he(0, 1)

        for s in range(N):
            q, ci = steps[s]
            t0, w = grid[ci]
            pin_b, pint, pout_b, poutt = qinfo(q)
            wq = w2_sb[(call, q)]
            if ci == 0:
                if q < 3:
                    nq = w2_sb[(call, q + 1)]
                    dma("cw", nq.t[:, :, :, :],
                        w2_T[(2 * q + 2) * DH:(2 * q + 4) * DH, :].rearrange(
                            "(e kt p) m -> p e kt m", p=128, kt=HT),
                        w=(nq.keys,))
                elif call == 0:
                    dma("cw", wk_sb.t[:, :, :], wk_T.rearrange("(kt p) m -> p kt m", p=128), w=(wk_sb.keys,))
                    dma("cw", wv_sb.t[:, :, :], wv_T.rearrange("(kt p) m -> p kt m", p=128), w=(wv_sb.keys,))
                    dma("cw", wr_sb.t[:, :, :], wr_T.rearrange("(kt p) m -> p kt m", p=128), w=(wr_sb.keys,))
            if s + 1 < N:
                load_hrt(s + 1)
                rows_and_bcast(s + 1)
            for m in range(DT):
                if m < 3:
                    dma("cout", pb_pin[m].t[:, 0:w], blkm(pin_b, ci, m, w),
                        r=(("d", pint, t0),), w=(pb_pin[m].keys,))
                mm(m, PS[m][:, 0:w],
                   [(wq.t[:, 0, kt, 128 * m:128 * (m + 1)], he0b.t[:, kt, 0:w]) for kt in range(HT)],
                   (wq.keys, he0b.keys), start=True, stop=False)
            if s + 1 < N:
                emit_he(s + 1, 0)
            for m in range(DT):
                if q == 3:
                    xqX = pb_xq[m % 2]
                    dma("cout", xqX.t[:, 0:w], blkm(xsrcB, ci, m, w),
                        r=(("d", stag, t0),), w=(xqX.keys,))
                mm(m, PS[m][:, 0:w],
                   [(wq.t[:, 1, kt, 128 * m:128 * (m + 1)], he1b.t[:, kt, 0:w]) for kt in range(HT)],
                   (wq.keys, he1b.keys), start=False, stop=True)
                pinX = pb_pin[m % 3]
                poX = pb_po[m % 2]
                if q < 3:
                    op("dve", lambda e, m=m, pinX=pinX, poX=poX, w=w:
                       e.scalar_tensor_tensor(poX.t[:, 0:w], PS[m][:, 0:w], 1.0, pinX.t[:, 0:w], OP.mult, OP.add),
                       r=(pkk(m), pinX.keys), w=(poX.keys,))
                else:
                    xqX = pb_xq[m % 2]
                    op("dve", lambda e, m=m, xqX=xqX, poX=poX, w=w:
                       e.scalar_tensor_tensor(poX.t[:, 0:w], PS[m][:, 0:w], 0.5, xqX.t[:, 0:w], OP.mult, OP.add),
                       r=(pkk(m), xqX.keys), w=(poX.keys,))
                    op("dve", lambda e, pinX=pinX, poX=poX, xqX=xqX, w=w:
                       e.scalar_tensor_tensor(xqX.t[:, 0:w], pinX.t[:, 0:w], 0.5, poX.t[:, 0:w], OP.mult, OP.add),
                       r=(pinX.keys, poX.keys), w=(xqX.keys,))
                    poX = xqX
                if q < 3:
                    dst = blkm(pout_b, ci, m, w)
                else:
                    dst = pout_b[:, m, t0:t0 + w]
                dma("cout", dst, poX.t[:, 0:w], r=(poX.keys,), w=(("d", poutt, t0),))
                if m + 3 < DT:
                    dma("cout", pb_pin[m % 3].t[:, 0:w], blkm(pin_b, ci, m + 3, w),
                        r=(("d", pint, t0),), w=(pb_pin[m % 3].keys,))
            if s + 1 < N:
                emit_he(s + 1, 1)

    def emit_q0_load(call):
        wq = w2_sb[(call, 0)]
        dma("cw", wq.t[:, :, :, :],
            w2_T[0:2 * DH, :].rearrange("(e kt p) m -> p e kt m", p=128, kt=HT),
            w=(wq.keys,))

    def emit_PC():
        A = RegionAlloc(b, R3, R3SZ, "pc")
        xb2 = [A.t(f"xb{i}", [128, DT, PCW], F32) for i in range(2)]
        tmp = A.t("tmp", [128, DT, PCW], F32)
        xbf = A.t("xbf", [128, DT * PCW], BF16)
        xc = A.t("xc", [128, DT * PCW], BF16)
        xprev = b.sb("pc_xprev", [128, DT, PCW], F32, off=R1)
        mixA = b.sb("pc_mixA", [128, DT, PCW], BF16, off=R1 + 14336)
        mixB = b.sb("pc_mixB", [128, DT, PCW], BF16, off=R1 + 21504)
        evacs = [b.sb(f"pc_evac{i}", [128, DT, PCW], F32, off=R1 + 28672 + i * 14336)
                 for i in range(2)]
        lnr = (b.sb("pc_rstd", [128, PCW], F32, off=R1 + 57344),
               b.sb("pc_mrow", [128, PCW], F32, off=R1 + 59392))
        wsbs = (wk_sb, wv_sb, wr_sb)
        dsts = (kvw, vvw, rsvw)
        dtags = ("k", "v", "rs")
        xr1_all = tuple(("d", "xr1", t0) for (t0, _w) in grid_h)
        for ci, (t0, w) in enumerate(pc_grid):
            xb = xb2[ci % 2]
            rk = tuple(k for k, (kt0, kw) in zip(xr1_all, grid_h)
                       if kt0 < t0 + w and kt0 + kw > t0)
            dma("cin", xb[:, :, 0:w], xr1vw[:, :, t0:t0 + w], r=rk, w=(xb.keys,))
            emit_ln(xb, w, xbf, xc, tmp, xb)
            for j in range(DT):
                op("pool", lambda e, j=j, w=w, xb=xb: e.tensor_copy(xprev.t[:, j, 1:w], xb.t[:, j, 0:w - 1]),
                   r=(xb.keys,), w=(xprev.keys,))
            pcol3 = pcol.t[:, :].rearrange("p (j t) -> p j t", t=1)
            op("dve", lambda e, pcol3=pcol3: e.tensor_copy(xprev.t[:, :, 0:1], pcol3),
               r=(pcol.keys,), w=(xprev.keys,), tiny=True)
            op("dve", lambda e, w=w, pcol3=pcol3, xb=xb: e.tensor_copy(pcol3, xb.t[:, :, w - 1:w]),
               r=(xb.keys,), w=(pcol.keys,), tiny=True)
            for j in range(DT):
                op("dve", lambda e, j=j, w=w, xb=xb: e.tensor_tensor(tmp.t[:, j, 0:w], xb.t[:, j, 0:w], xprev.t[:, j, 0:w], OP.subtract),
                   r=(xb.keys, xprev.keys), w=(tmp.keys,))
            if t0 == 0:
                s0, c0, cw = 1, 0, w - 1
            else:
                s0, c0, cw = 0, t0 - 1, w
            for i in range(3):
                mx = (mixA, mixB)[i % 2]
                evac = evacs[i % 2]
                for j in range(DT):
                    op("dve", lambda e, i=i, j=j, mx=mx, w=w: e.scalar_tensor_tensor(
                        mx.t[:, j, 0:w], tmp.t[:, j, 0:w], tmix[:, 8 * i + j:8 * i + j + 1],
                        xprev.t[:, j, 0:w], OP.mult, OP.add),
                       r=(tmp.keys, xprev.keys, tmix.keys), w=(mx.keys,))
                for m in range(DT):
                    bank = m % 4
                    mm(bank, PS[bank][:, 0:w],
                       [(wsbs[i].t[:, kt, 128 * m:128 * (m + 1)], mx.t[:, kt, 0:w]) for kt in range(DT)],
                       (wsbs[i].keys, mx.keys))
                    if i == 0:
                        op("act", lambda e, bank=bank, m=m, w=w, evac=evac: e.copy(evac.t[:, m, 0:w], PS[bank][:, 0:w]),
                           r=(pkk(bank),), w=(evac.keys,))
                    elif i == 1:
                        op("dve", lambda e, bank=bank, m=m, w=w, evac=evac: e.tensor_copy(evac.t[:, m, 0:w], PS[bank][:, 0:w]),
                           r=(pkk(bank),), w=(evac.keys,))
                    else:
                        op("act", lambda e, bank=bank, m=m, w=w, evac=evac: e.activation(evac.t[:, m, 0:w], PS[bank][:, 0:w], AF.Sigmoid),
                           r=(pkk(bank),), w=(evac.keys,))
                if i < 2:
                    dma(("cpk", "cpv")[i], dsts[i][:, :, c0:c0 + cw], evac.t[:, :, s0:s0 + cw],
                        r=(evac.keys,), w=(("d", dtags[i], t0),))
                else:
                    dma("cpr", dsts[i][:, 0:4, c0:c0 + cw], evac.t[:, 0:4, s0:s0 + cw],
                        r=(evac.keys,), w=(("d", "rs", t0, 0),))
                    dma("cpr", dsts[i][:, 4:8, c0:c0 + cw], evac.t[:, 4:8, s0:s0 + cw],
                        r=(evac.keys,), w=(("d", "rs", t0, 1),))
        return

    def pc_cov(q0, tag, rs=False):
        out = []
        for (t0, wq_) in pc_grid:
            c0 = 0 if t0 == 0 else t0 - 1
            cw_ = wq_ - 1 if t0 == 0 else wq_
            if c0 < q0 + QW and c0 + cw_ > q0:
                out.append(t0)
        if rs:
            return tuple(("d", tag, t0, h) for t0 in out for h in (0, 1))
        return tuple(("d", tag, t0) for t0 in out)

    def emit_PD():
        A = RegionAlloc(b, R3, WO_OFF - R3, "pd")
        kt_ = A.t("kt", [128, QW], F32)
        vt_ = A.t("vt", [128, QW], F32)
        ekt = A.t("ekt", [128, QW], F32)
        pt_ = A.t("pt", [128, QW], F32)
        ones1k = A.t("ones1k", [128, QW], F32)
        decb = A.t("decb", [128, QW], F32)
        abuf = A.t("abuf", [128, QW], F32)
        bbuf = A.t("bbuf", [128, QW], F32)
        dma("cw", wo_sb.t[:, :, :], wo_T.rearrange("(kt p) m -> p kt m", p=128), w=(wo_sb.keys,))
        op("dve", lambda e: e.memset(ones1k.ap, 1.0), w=(ones1k.keys,), tiny=True)

        for qi in range(TOK // QW):
            q0 = qi * QW
            kkeys = pc_cov(q0, "k")
            vkeys = pc_cov(q0, "v")
            for j in range(DT):
                op("dve", lambda e, j=j: e.tensor_scalar(decb.ap, ones1k.ap, decC[:, j:j + 1], None, OP.mult),
                   r=(ones1k.keys, decC.keys), w=(decb.keys,))
                if DBG_PD[0] and qi == 0:
                    dma("cout", dcb_d[128 * j:128 * (j + 1), :], decb.ap, r=(decb.keys,), w=(("d", "dcb", j),))
                dma("csc", kt_.ap, k_d[128 * j:128 * (j + 1), q0:q0 + QW], r=kkeys, w=(kt_.keys,))
                dma("csc", vt_.ap, v_d[128 * j:128 * (j + 1), q0:q0 + QW], r=vkeys, w=(vt_.keys,))
                op("act", lambda e: e.activation(ekt.ap, kt_.ap, AF.Exp), r=(kt_.keys,), w=(ekt.keys,))
                op("pool", lambda e: e.tensor_tensor(pt_.ap, ekt.ap, vt_.ap, OP.mult),
                   r=(ekt.keys, vt_.keys), w=(pt_.keys,))
                if DBG_PD[0] and qi == 0:
                    dma("cout", pt_d[128 * j:128 * (j + 1), :], pt_.ap, r=(pt_.keys,), w=(("d", "ptd", j),))
                op("dve", lambda e, j=j: e.tensor_tensor_scan(abuf.ap, decb.ap, pt_.ap, carry_a[:, j:j + 1], OP.mult, OP.add),
                   r=(decb.keys, pt_.keys, carry_a.keys), w=(abuf.keys,))
                op("dve", lambda e, j=j: e.tensor_tensor_scan(bbuf.ap, decb.ap, ekt.ap, carry_b[:, j:j + 1], OP.mult, OP.add),
                   r=(decb.keys, ekt.keys, carry_b.keys), w=(bbuf.keys,))
                op("dve", lambda e, j=j: e.tensor_copy(carry_a[:, j:j + 1], abuf[:, QW - 1:QW]),
                   r=(abuf.keys,), w=(carry_a.keys,), tiny=True)
                op("dve", lambda e, j=j: e.tensor_copy(carry_b[:, j:j + 1], bbuf[:, QW - 1:QW]),
                   r=(bbuf.keys,), w=(carry_b.keys,), tiny=True)
                dma("cscw", as_d[128 * j:128 * (j + 1), q0:q0 + QW], abuf.ap,
                    r=(abuf.keys,), w=(("d", "as", j, qi),))
                dma("cscw", bs_d[128 * j:128 * (j + 1), q0:q0 + QW], bbuf.ap,
                    r=(bbuf.keys,), w=(("d", "bs", j, qi),))
        op("dve", lambda e: e.tensor_copy(stage[:, 0:8], carry_a.ap), r=(carry_a.keys,), w=(stage.keys,), tiny=True)
        op("dve", lambda e: e.tensor_copy(stage[:, 8:16], carry_b.ap), r=(carry_b.keys,), w=(stage.keys,), tiny=True)
        dma("crow", ag_in_d, stage.ap, r=(stage.keys,), w=(("d", "agin"),))
        b.emit("pool", lambda e: e.collective_compute(
            "AllGather", OP.bypass, replica_groups=[list(range(8))],
            ins=[ag_in_d.opt()], outs=[ag_out_d.opt()]),
            r=(("d", "agin"),), w=(("d", "agout"),), coll=True)
        dma("crow", gsb.t[:, :, :], ag_out_d.rearrange("(r p) c -> p r c", p=128),
            r=(("d", "agout"),), w=(gsb.keys,))
        op("dve", lambda e: e.tensor_scalar(SAB2.ap, gsb.t[:, 0, :], selv[:, 0:1], None, OP.mult),
           r=(gsb.keys, selv.keys), w=(SAB2.keys,), tiny=True)
        for rr in range(1, 8):
            src_, dst_ = (SAB2, SAB1) if rr % 2 == 1 else (SAB1, SAB2)
            op("dve", lambda e, rr=rr, src_=src_, dst_=dst_: e.scalar_tensor_tensor(
                dst_.ap, gsb.t[:, rr, :], selv[:, rr:rr + 1], src_.ap, OP.mult, OP.add),
               r=(gsb.keys, selv.keys, src_.keys), w=(dst_.keys,), tiny=True)

    def emit_PE():
        names = ("kt", "vt", "rt", "ap", "bp", "dpw", "eku", "tq")
        PEB = [{nm: b.sb(f"pe_{nm}{p}", [128, QW], F32, off=R2 + i * 4096 + p * 32768)
                for i, nm in enumerate(names)} for p in (0, 1)]
        iqs = [b.sb(f"pe_iq{p}", [128, QW], F32, off=R3 + p * 4096) for p in (0, 1)]
        for qi in range(TOK // QW):
            q0 = qi * QW
            kkey = pc_cov(q0, "k")
            vkey = pc_cov(q0, "v")
            rkey = pc_cov(q0, "rs", rs=True)
            iq = iqs[qi % 2]
            op("pool", lambda e, q0=q0, iq=iq: e.iota(iq.ap, [[1, QW]], base=q0, channel_multiplier=0,
                                                      allow_small_or_imprecise_dtypes=True), w=(iq.keys,))
            for j in range(DT):
                P = PEB[(qi * DT + j) % 2]
                kt_, vt_, rt_ = P["kt"], P["vt"], P["rt"]
                ap_, bp_, dpw, eku, tq = P["ap"], P["bp"], P["dpw"], P["eku"], P["tq"]
                akey = (("d", "as", j, qi), ("d", "as", j, max(0, qi - 1)))
                bkey = (("d", "bs", j, qi), ("d", "bs", j, max(0, qi - 1)))
                dma("csc", kt_.ap, k_d[128 * j:128 * (j + 1), q0:q0 + QW], r=kkey, w=(kt_.keys,))
                dma("csc", vt_.ap, v_d[128 * j:128 * (j + 1), q0:q0 + QW], r=vkey, w=(vt_.keys,))
                dma("csc", rt_.ap, rs_d[128 * j:128 * (j + 1), q0:q0 + QW], r=rkey, w=(rt_.keys,))
                if q0 == 0:
                    op("dve", lambda e, ap_=ap_: e.memset(ap_[:, 0:1], 0.0), w=(ap_.keys,), tiny=True)
                    op("dve", lambda e, bp_=bp_: e.memset(bp_[:, 0:1], 0.0), w=(bp_.keys,), tiny=True)
                    dma("csc", ap_[:, 1:QW], as_d[128 * j:128 * (j + 1), 0:QW - 1], r=akey, w=(ap_.keys,))
                    dma("csc", bp_[:, 1:QW], bs_d[128 * j:128 * (j + 1), 0:QW - 1], r=bkey, w=(bp_.keys,))
                else:
                    dma("csc", ap_.ap, as_d[128 * j:128 * (j + 1), q0 - 1:q0 + QW - 1], r=akey, w=(ap_.keys,))
                    dma("csc", bp_.ap, bs_d[128 * j:128 * (j + 1), q0 - 1:q0 + QW - 1], r=bkey, w=(bp_.keys,))
                op("dve", lambda e, j=j, tq=tq, iq=iq: e.tensor_scalar(tq.ap, iq.ap, netd[:, j:j + 1], None, OP.mult),
                   r=(iq.keys, netd.keys), w=(tq.keys,))
                op("act", lambda e, dpw=dpw, tq=tq: e.activation(dpw.ap, tq.ap, AF.Exp), r=(tq.keys,), w=(dpw.keys,))
                op("act", lambda e, j=j, eku=eku, kt_=kt_: e.activation(eku.ap, kt_.ap, AF.Exp, bias=tfir[:, j:j + 1]),
                   r=(kt_.keys, tfir.keys), w=(eku.keys,))
                op("pool", lambda e, tq=tq, eku=eku, vt_=vt_: e.tensor_tensor(tq.ap, eku.ap, vt_.ap, OP.mult),
                   r=(eku.keys, vt_.keys), w=(tq.keys,))
                op("dve", lambda e, j=j, kt_=kt_, dpw=dpw, ap_=ap_: e.scalar_tensor_tensor(kt_.ap, dpw.ap, SAB1[:, j:j + 1], ap_.ap, OP.mult, OP.add),
                   r=(dpw.keys, SAB1.keys, ap_.keys), w=(kt_.keys,))
                op("dve", lambda e, j=j, vt_=vt_, dpw=dpw, bp_=bp_: e.scalar_tensor_tensor(vt_.ap, dpw.ap, SAB1[:, 8 + j:8 + j + 1], bp_.ap, OP.mult, OP.add),
                   r=(dpw.keys, SAB1.keys, bp_.keys), w=(vt_.keys,))
                op("pool", lambda e, ap_=ap_, kt_=kt_, tq=tq: e.tensor_tensor(ap_.ap, kt_.ap, tq.ap, OP.add),
                   r=(kt_.keys, tq.keys), w=(ap_.keys,))
                op("pool", lambda e, bp_=bp_, vt_=vt_, eku=eku: e.tensor_tensor(bp_.ap, vt_.ap, eku.ap, OP.add),
                   r=(vt_.keys, eku.keys), w=(bp_.keys,))
                op("dve", lambda e, dpw=dpw, bp_=bp_: e.tensor_scalar(dpw.ap, bp_.ap, 1e-6, None, OP.max),
                   r=(bp_.keys,), w=(dpw.keys,))
                op("dve", lambda e, tq=tq, dpw=dpw: e.reciprocal(tq.ap, dpw.ap), r=(dpw.keys,), w=(tq.keys,))
                op("pool", lambda e, kt_=kt_, ap_=ap_, tq=tq: e.tensor_tensor(kt_.ap, ap_.ap, tq.ap, OP.mult),
                   r=(ap_.keys, tq.keys), w=(kt_.keys,))
                op("dve", lambda e, j=j, q0=q0, kt_=kt_, rt_=rt_: e.tensor_tensor(rwb.t[:, j, q0:q0 + QW], kt_.ap, rt_.ap, OP.mult),
                   r=(kt_.keys, rt_.keys), w=(rwb.keys,))


    def emit_PF():
        A = RegionAlloc(b, R3, WO_OFF - R3, "pf")
        xbs = [A.t(f"xb{i}", [128, DT, TC], F32) for i in range(2)]
        ob = A.t("ob", [128, DT, TC], F32)
        xr1keys = tuple(("d", "xr1", t0) for (t0, _w) in grid_h)
        for ci, (t0, w) in enumerate(grid_t):
            xb = xbs[ci % 2]
            dma("cxb", xb[:, 0:4, 0:w], xr1vw[:, 0:4, t0 + 1:t0 + 1 + w],
                r=xr1keys, w=(xb.keys,))
            dma("cxb", xb[:, 4:8, 0:w], xr1vw[:, 4:8, t0 + 1:t0 + 1 + w],
                r=xr1keys, w=(xb.keys,))
            for m in range(DT):
                bank = m % 4
                mm(bank, PS[bank][:, 0:w],
                   [(wo_sb.t[:, kt, 128 * m:128 * (m + 1)], rwb.t[:, kt, t0:t0 + w]) for kt in range(DT)],
                   (wo_sb.keys, rwb.keys))
                op("dve", lambda e, bank=bank, m=m, w=w, xb=xb: e.tensor_tensor(ob.t[:, m, 0:w], PS[bank][:, 0:w], xb.t[:, m, 0:w], OP.add),
                   r=(pkk(bank), xb.keys), w=(ob.keys,))
            dma("cin", blk(xr2_d, ci), ob.t[:, :, :].rearrange("p j t -> p (j t)"),
                r=(ob.keys,), w=(("d", "xr2", t0),))
        wload3(w1_sb[1], w1_T, DT, 2 * DH, "w1_1")

    allph = ["A2", "A1", "B", "C", "D", "E", "F", "A2b", "A1b", "Bb"]
    ph = allph if phases is None else phases
    if "A2" in ph:
        emit_PA2(0)
    if "A1" in ph:
        emit_PA1(0)
    if "B" in ph:
        emit_q0_load(0)
        emit_PB_call(0)
    if "C" in ph:
        emit_PC()
    if "D" in ph:
        emit_PD()
    if "E" in ph:
        emit_PE()
    if "F" in ph:
        emit_PF()
    if "A2b" in ph:
        emit_PA2(1)
    if "A1b" in ph:
        emit_PA1(1)
    if "Bb" in ph:
        emit_q0_load(1)
        emit_PB_call(1)

    dbg_src = {"xln1": (xln_d, "xln", grid_h), "wse": (wse_d, "wse", grid_h),
               "hsh": (hsh_d, "hsh", grid_h), "pB": (pB_d, "pB", grid_h),
               "as": (as_d, "as", None), "bs": (bs_d, "bs", None),
               "dcb": (dcb_d, "dcb", None), "ptd": (pt_d, "ptd", None),
               "pA": (pA_d, "pA", grid_h), "xr1": (xr1_d, "xr1", grid_h),
               "hrt": (hrt_d, "hrt", grid_h),
               "k": (k_d, "k", pc_grid), "v": (v_d, "v", pc_grid), "rs": (rs_d, "rs", pc_grid),
               "xr2": (xr2_d, "xr2", grid_t)}
    if "rwb" in debug:
        dd = nc.dram_tensor("dbg_rwb", [128, DT * TOK], BF16, kind="ExternalOutput").ap()
        dma("cout", dd, rwb.t[:, :, :].rearrange("p j t -> p (j t)"), r=(rwb.keys,), w=())
    for nm in debug:
        if nm not in dbg_src:
            continue
        src, tag, grid = dbg_src[nm]
        shp = list(src.tensor.shape) if hasattr(src, "tensor") else None
        dt = BF16 if nm in ("xln1", "hrt", "hsh") else F32
        dd = nc.dram_tensor("dbg_" + nm, [src.shape[0], src.shape[1]], dt, kind="ExternalOutput").ap()
        if nm in ("dcb", "ptd"):
            keys = tuple(("d", tag, j) for j in range(DT))
        elif nm in ("as", "bs"):
            keys = tuple(("d", tag, j, qi) for j in range(DT) for qi in range(TOK // QW))
        elif nm == "rs":
            keys = tuple(("d", "rs", t0, h) for (t0, _w) in grid for h in (0, 1))
        else:
            keys = tuple(("d", tag, t0) for (t0, _w) in grid)
        dma("cout", dd, src, r=keys, w=())

    b.build_blocks(serial=serial)
    nc._b = b
    return nc



def _prep(inputs):
    f = np.float32
    xs = np.asarray(inputs["L0_attn_x"], f)
    w_in = np.asarray(inputs["L0_ffn_w_in"], f)
    w_out = np.asarray(inputs["L0_ffn_w_out"], f)
    w1 = np.asarray(inputs["L0_ffn_w1"], f)
    w2 = np.asarray(inputs["L0_ffn_w2"], f)
    wg = np.asarray(inputs["L0_ffn_wg"], f)
    we = np.asarray(inputs["L0_ffn_we"], f)
    gb = np.asarray(inputs["L0_ffn_gb"], f)
    eb = np.asarray(inputs["L0_ffn_eb"], f)
    wk = np.asarray(inputs["L0_attn_Wk"], f)
    wv = np.asarray(inputs["L0_attn_Wv"], f)
    wr = np.asarray(inputs["L0_attn_Wr"], f)
    wo = np.asarray(inputs["L0_attn_Wo"], f)
    td = np.asarray(inputs["L0_attn_time_decay"], f)
    tf = np.asarray(inputs["L0_attn_time_first"], f)
    tmk = np.asarray(inputs["L0_attn_time_mix_k"], f)
    tmv = np.asarray(inputs["L0_attn_time_mix_v"], f)
    tmr = np.asarray(inputs["L0_attn_time_mix_r"], f)

    def colmaj(vec):
        return np.ascontiguousarray(vec.reshape(DT, 128).T)

    tmix_a = np.concatenate([colmaj(tmk), colmaj(tmv), colmaj(tmr)], axis=1)
    tpar_base = np.concatenate([tmix_a, colmaj(td), colmaj(tf)], axis=1)
    shared = {
        "w_in_T": np.ascontiguousarray(w_in.T),
        "w1_T": np.ascontiguousarray(w1.T),
        "w_out_T": np.ascontiguousarray(w_out.T),
        "w2_T": np.ascontiguousarray(w2.transpose(0, 2, 1).reshape(NE * DH, D)),
        "wk_T": np.ascontiguousarray(wk.T),
        "wv_T": np.ascontiguousarray(wv.T),
        "wr_T": np.ascontiguousarray(wr.T),
        "wo_T": np.ascontiguousarray(wo.T),
        "wge_T": np.ascontiguousarray(np.concatenate([wg, we], 0).T),
        "gbias": np.ascontiguousarray(np.concatenate([gb, eb])[:, None]),
    }
    grid = []
    t = 0
    while t < HTOK:
        grid.append((t, min(TC, HTOK - t)))
        t += TC
    nch = len(grid)
    in_maps = []
    for c in range(8):
        bb, h = divmod(c, 2)
        seg = xs[bb, h * TOK:(h + 1) * TOK]
        halo = xs[bb, h * TOK - 1][None] if h == 1 else np.zeros((1, D), f)
        shard = np.concatenate([halo, seg], 0).T
        sh3 = shard.reshape(DT, 128, HTOK).transpose(1, 0, 2)
        xblk = np.zeros((nch, 128, DT, TC), f)
        for ci, (t0, w) in enumerate(grid):
            xblk[ci, :, :, 0:w] = sh3[:, :, t0:t0 + w]
        sel = np.zeros((128, DT), f)
        if h == 1:
            sel[:, c - 1] = 1.0
        m = dict(shared)
        m["x"] = np.ascontiguousarray(xblk.reshape(nch * 128, DT * TC))
        m["tparams"] = np.ascontiguousarray(np.concatenate([tpar_base, sel], axis=1))
        in_maps.append(m)
    return in_maps


_NC_CACHE = {}


def kernel(**inputs):
    in_maps = _prep(inputs)
    key = "main"
    if key not in _NC_CACHE:
        _NC_CACHE[key] = build_program()
    nc = _NC_CACHE[key]
    res = run_bass_kernel_spmd(nc, in_maps, core_ids=list(range(8)))
    out = np.empty((4, 8192, D), np.float32)
    for c in range(8):
        bb, h = divmod(c, 2)
        out[bb, h * TOK:(h + 1) * TOK] = np.asarray(res.results[c]["out"]).T
    return out


# version 27
# speedup vs baseline: 1.2338x; 1.0039x over previous
import numpy as np
import concourse.bass as bass
import concourse.mybir as mybir
from concourse.bass_utils import run_bass_kernel_spmd

F32 = mybir.dt.float32
BF16 = mybir.dt.bfloat16
OP = mybir.AluOpType
AF = mybir.ActivationFunctionType
AX = mybir.AxisListType.X

D = 1024
DT = 8
DH = 2048
HT = 16
NE = 8
TOK = 4096
HTOK = 4097
TC = 368
QW = 1024
BIGM = 30000.0
EPS = 1e-5

ENGS = ("pe", "act", "dve", "pool", "sp")
BLK_ATTR = {"pe": "tensor", "act": "scalar", "dve": "vector", "pool": "gpsimd", "sp": "sync"}
GRAN = 256
CROT = 4
SB_BASE = 16640
SB_LIMIT = 228352

R1 = SB_BASE + 26624
R2 = R1 + 65536
R3 = R2 + 65536
R3SZ = SB_LIMIT - R3
WO_OFF = R3 + 36864


def chunk_grid(total):
    out, t = [], 0
    while t < total:
        out.append((t, min(TC, total - t)))
        t += TC
    return out


class Rec:
    __slots__ = ("eng", "fn", "kind", "chan", "cidx", "ms", "deps", "waits", "gi", "snap", "tiny")

    def __init__(self, eng, fn, kind, chan, cidx, gi):
        self.eng, self.fn, self.kind, self.chan, self.cidx, self.gi = eng, fn, kind, chan, cidx, gi
        self.ms = 0
        self.deps = None
        self.waits = None


class ST:

    def __init__(self, t, off, nbytes):
        self.t = t
        self.off0 = off
        g0, g1 = off // GRAN, (off + nbytes + GRAN - 1) // GRAN
        self.keys = tuple(("s", g) for g in range(g0, g1))

    def __getitem__(self, idx):
        return self.t[idx]

    @property
    def ap(self):
        return self.t[:]


class RegionAlloc:
    def __init__(self, b, base, size, tag):
        self.b, self.base, self.end, self.off, self.tag = b, base, base + size, base, tag

    def t(self, name, shape, dtype):
        esz = 2 if dtype == BF16 else 4
        nb = int(np.prod(shape[1:])) * esz
        off = (self.off + 31) & ~31
        assert off + nb <= self.end, f"{self.tag}:{name} overflow {off}+{nb}>{self.end}"
        self.off = off + nb
        return self.b.sb(f"{self.tag}_{name}", shape, dtype, off=off)


class B:
    def __init__(self, nc):
        self.nc = nc
        self.order = []
        self.per_eng = {e: [] for e in ENGS}
        self.chans = {}
        self.writers = {}
        self.readers = {}
        self.n_coll = 0
        self._names = set()

    def sb(self, name, shape, dtype, off):
        assert name not in self._names, name
        self._names.add(name)
        esz = 2 if dtype == BF16 else 4
        nb = int(np.prod(shape[1:])) * esz
        assert off + nb <= SB_LIMIT, f"sbuf overflow {name}: {off}+{nb}"
        t = self.nc.alloc_sbuf_tensor_at(name, list(shape), dtype, offset=off)
        return ST(t, off, nb)

    def chan(self, name, eng):
        self.chans[name] = {"eng": eng, "n": 0}

    @staticmethod
    def _flat(keys):
        out = []
        for k in keys:
            if isinstance(k, tuple) and k and isinstance(k[0], tuple):
                out.extend(k)
            else:
                out.append(k)
        return out

    def emit(self, eng, fn, r=(), w=(), chan=None, coll=False, tiny=False):
        kind = "coll" if coll else ("dma" if chan else "eng")
        cidx = None
        if chan:
            cidx = self.chans[chan]["n"]
            self.chans[chan]["n"] += 1
        if coll:
            cidx = self.n_coll
            self.n_coll += 1
        rec = Rec(eng, fn, kind, chan, cidx, len(self.order))
        rec.tiny = tiny
        rec.snap = ({e: sum(1 for x in self.per_eng[e] if x.kind != "dma") for e in ENGS},
                    {c: m["n"] for c, m in self.chans.items()}, self.n_coll)
        rset, wset = self._flat(r), self._flat(w)
        deps = set()
        for k in rset:
            p = self.writers.get(k)
            if p is not None:
                deps.add(p)
        for k in wset:
            p = self.writers.get(k)
            if p is not None:
                deps.add(p)
            for q in self.readers.get(k, ()):
                deps.add(q)
        for k in rset:
            self.readers.setdefault(k, []).append(rec)
        for k in wset:
            self.writers[k] = rec
            self.readers[k] = []
        deps.discard(rec)
        rec.deps = deps
        self.order.append(rec)
        self.per_eng[eng].append(rec)
        return rec

    def finalize(self, serial=False):
        if serial:
            ms_no = {e: 0 for e in ENGS}
            eng_ms = {}
            for e in ENGS:
                for rec in self.per_eng[e]:
                    if rec.kind != "dma" and rec.kind != "coll":
                        ms_no[e] += 1
                        rec.ms = ms_no[e]
            cnt = {e: 0 for e in ENGS}
            for rec in self.order:
                waits = {}
                for e2 in ENGS:
                    if e2 != rec.eng and cnt[e2] > 0:
                        waits[("e", e2)] = cnt[e2]
                for c2, n2 in rec.snap[1].items():
                    for r_ in range(CROT):
                        nr = (n2 + CROT - 1 - r_) // CROT
                        if nr > 0:
                            waits[("c", c2, r_)] = 16 * nr
                if rec.snap[2] > 0:
                    waits[("x",)] = rec.snap[2]
                rec.waits = waits
                if rec.kind == "eng" and rec.ms > 0:
                    cnt[rec.eng] = rec.ms
            return ms_no
        for rec in self.order:
            for d in rec.deps:
                if d.kind == "eng" and (d.eng != rec.eng or d.tiny or rec.kind != "eng"):
                    d.ms = -1
        ms_no = {e: 0 for e in ENGS}
        for e in ENGS:
            for rec in self.per_eng[e]:
                if rec.kind == "eng" and rec.ms == -1:
                    ms_no[e] += 1
                    rec.ms = ms_no[e]
        for rec in self.order:
            waits = {}
            for d in rec.deps:
                if d.kind == "eng":
                    if d.eng == rec.eng and rec.kind == "eng" and not d.tiny:
                        continue
                    k = ("e", d.eng)
                    waits[k] = max(waits.get(k, 0), d.ms)
                elif d.kind == "dma":
                    k = ("c", d.chan, d.cidx % CROT)
                    waits[k] = max(waits.get(k, 0), 16 * (d.cidx // CROT + 1))
                else:
                    k = ("x",)
                    waits[k] = max(waits.get(k, 0), d.cidx + 1)
            rec.waits = waits
        return ms_no

    def build_blocks(self, serial=False):
        ms_no = self.finalize(serial=serial)
        nc = self.nc
        esem = {e: nc.alloc_semaphore(f"sem_{e}") for e in ENGS}
        csem = {c: [nc.alloc_semaphore(f"sem_{c}{r}") for r in range(CROT)]
                for c in self.chans}
        xsem = nc.alloc_semaphore("sem_coll")

        def semof(key):
            if key[0] == "e":
                return esem[key[1]]
            if key[0] == "c":
                return csem[key[1]][key[2]]
            return xsem

        with nc.Block() as blk:
            def make(engname, rlist):
                def body(eng):
                    cur = {}
                    for rec in rlist:
                        for key, val in sorted(rec.waits.items()):
                            if cur.get(key, 0) < val:
                                eng.wait_ge(semof(key), val)
                                cur[key] = val
                        inst = rec.fn(eng)
                        if rec.kind == "dma":
                            inst.then_inc(csem[rec.chan][rec.cidx % CROT], 16)
                        elif rec.kind == "coll":
                            inst.then_inc(xsem, 1)
                        elif rec.ms > 0:
                            inst.then_inc(esem[rec.eng], 1)
                    if engname == "sp":
                        for c, meta in self.chans.items():
                            for r in range(CROT):
                                nr = (meta["n"] + CROT - 1 - r) // CROT
                                if nr:
                                    eng.wait_ge(csem[c][r], 16 * nr)
                        if self.n_coll:
                            eng.wait_ge(xsem, self.n_coll)
                        for e2 in ENGS:
                            if e2 != "sp" and ms_no[e2]:
                                eng.wait_ge(esem[e2], ms_no[e2])
                return body

            for e in ENGS:
                getattr(blk, BLK_ATTR[e])(make(e, self.per_eng[e]))



DBG_HSH = [False]
SERIAL_PA1 = [False]
DBG_PD = [False]
FORCE_TINY = [False]


def build_program(debug=(), phases=None, serial=False):
    DBG_HSH[0] = "hsh" in debug
    DBG_PD[0] = "pd" in debug
    nc = bass.Bass("TRN2", target_bir_lowering=False)
    b = B(nc)
    b.chan("cin", "sp")
    b.chan("cout", "act")
    b.chan("cw", "pool")
    b.chan("crow", "sp")
    b.chan("csc", "act")
    b.chan("cscw", "act")
    b.chan("cxb", "sp")
    b.chan("cxl", "sp")
    b.chan("cpk", "pool")
    b.chan("cpv", "pool")
    b.chan("cpr", "pool")

    def din(name, shape, dtype=F32):
        return nc.dram_tensor(name, list(shape), dtype, kind="ExternalInput").ap()

    NCH = len(chunk_grid(HTOK))
    x_in = din("x", [NCH * 128, DT * TC])
    w_in_T = din("w_in_T", [D, 2 * DH])
    w1_T = din("w1_T", [D, 2 * DH])
    w_out_T = din("w_out_T", [DH, D])
    w2_T = din("w2_T", [NE * DH, D])
    wk_T = din("wk_T", [D, D])
    wv_T = din("wv_T", [D, D])
    wr_T = din("wr_T", [D, D])
    wo_T = din("wo_T", [D, D])
    wge_T = din("wge_T", [D, 10])
    gb_in = din("gbias", [10, 1])
    tpar_in = din("tparams", [128, 6 * DT])

    out_d = nc.dram_tensor("out", [D, TOK], F32, kind="ExternalOutput").ap()

    def idram(name, shape, dtype=F32):
        return nc.dram_tensor(name, list(shape), dtype).ap()

    xln_d = idram("xln_d", [NCH * 128, DT * TC], BF16)
    hrt_d = idram("hrt_d", [NCH * 128, HT * TC], BF16)
    pA_d = idram("pA_d", [NCH * 128, DT * TC])
    pB_d = idram("pB_d", [NCH * 128, DT * TC])
    xr2_d = idram("xr2_d", [NCH * 128, DT * TC])
    wse_d = idram("wse_d", [NE, HTOK])
    wsehl_d = idram("wsehl_d", [4 * NCH, 4 * TC], BF16)
    wsehl_v = wsehl_d.rearrange("(p c) (e t) -> p c e t", c=NCH, e=4)
    xr1_d = idram("xr1_d", [D, HTOK])
    k_d = idram("k_d", [D, TOK])
    v_d = idram("v_d", [D, TOK])
    rs_d = idram("rs_d", [D, TOK])
    as_d = idram("as_d", [D, TOK])
    bs_d = idram("bs_d", [D, TOK])
    hsh_d = idram("hsh_d", [DH, HTOK], BF16)
    dcb_d = idram("dcb_d", [D, QW])
    pt_d = idram("pt_d", [D, QW])
    ag_in_d = idram("ag_in", [128, 16])
    ag_out_d = idram("ag_out", [8 * 128, 16])

    def jv(ap):
        return ap.rearrange("(j p) t -> p j t", p=128)

    def blk(ap, c):
        return ap[c * 128:(c + 1) * 128, :]

    def blkm(ap, c, m, w):
        return ap[c * 128:(c + 1) * 128, m * TC:m * TC + w]

    xr1vw = jv(xr1_d)
    outvw = jv(out_d)
    kvw = jv(k_d)
    vvw = jv(v_d)
    rsvw = jv(rs_d)

    PS = [nc.alloc_psum_tensor(f"psb{i}", [128, 512], F32) for i in range(8)]

    def pkk(i):
        return ("p", i)

    CA = RegionAlloc(b, SB_BASE, R1 - SB_BASE, "c")
    idn = CA.t("idn", [128, 128], F32)
    ones_k = CA.t("ones_k", [128, 1], BF16)
    ones_r = CA.t("ones_r", [1, 128], BF16)
    tpar = CA.t("tpar", [128, 6 * DT], F32)
    tmix = b.sb("tmix", [128, 3 * DT], F32, off=tpar.off0)
    tdec = b.sb("tdec", [128, DT], F32, off=tpar.off0 + 96)
    tfir = b.sb("tfir", [128, DT], F32, off=tpar.off0 + 128)
    selv = b.sb("selv", [128, DT], F32, off=tpar.off0 + 160)
    etd = CA.t("etd", [128, DT], F32)
    netd = CA.t("netd", [128, DT], F32)
    decC = CA.t("dec", [128, DT], F32)
    wge_f = CA.t("wge_f", [128, DT, 10], F32)
    wge_hi = CA.t("wge_hi", [128, DT, 10], BF16)
    wge_lo = CA.t("wge_lo", [128, DT, 10], BF16)
    gbias = CA.t("gbias", [10, 1], F32)
    SAB1 = CA.t("SAB1", [128, 16], F32)
    SAB2 = CA.t("SAB2", [128, 16], F32)
    stage = CA.t("stage", [128, 16], F32)
    gsb = CA.t("gsb", [128, DT, 16], F32)
    carry_a = CA.t("carry_a", [128, DT], F32)
    carry_b = CA.t("carry_b", [128, DT], F32)
    pcol = CA.t("pcol", [128, DT], F32)
    rowA = CA.t("rowA", [1, 512], F32)
    rowB = CA.t("rowB", [1, 512], F32)
    rowC = CA.t("rowC", [1, 512], F32)
    rowD = CA.t("rowD", [1, 512], F32)
    rowE = CA.t("rowE", [1, 512], F32)
    rhiA = CA.t("rhiA", [1, 512], BF16)
    rloA = CA.t("rloA", [1, 512], BF16)
    rhiB = CA.t("rhiB", [1, 512], BF16)
    rloB = CA.t("rloB", [1, 512], BF16)
    glog = CA.t("glog", [10, 512], F32)
    wse8 = CA.t("wse8", [8, 512], F32)
    ones8 = CA.t("ones8", [128, 8], F32)
    wse8h = CA.t("wse8h", [8, 512], BF16)
    wse8l = CA.t("wse8l", [8, 512], BF16)
    assert CA.off <= R1, CA.off
    pbs0 = rowA.off0
    pb_pin = [b.sb(f"pbs_pin{i}", [128, TC], F32, off=(pbs0 + i * 1536) if i < 3 else (pbs0 + 13824)) for i in range(4)]
    pb_wb = [[b.sb(f"pbs_wb{p}{e}", [128, TC], F32, off=pbs0 + 4608 + (2 * p + e) * 1536)
              for e in range(2)] for p in range(2)]
    pb_rows4 = b.sb("pbs_rows4", [1, 4, TC], BF16, off=pbs0 + 10752)
    assert pbs0 + 10752 + 4 * TC * 2 <= rowA.off0 + 16384
    tc1 = b.sb("tc1", [128, 128], F32, off=R3)
    tc2 = b.sb("tc2", [128, 128], F32, off=R3 + 512)

    def wslot(call):
        return (R1, R2) if call == 0 else (R2, R1)

    w1_sb, w_in_sb, w_out_sb, w2_sb = {}, {}, {}, {}
    for call in (0, 1):
        sA, sB = wslot(call)
        w1_sb[call] = b.sb(f"w1s{call}", [128, DT, 2 * DH], BF16, off=sA)
        w_in_sb[call] = b.sb(f"wins{call}", [128, DT, 2 * DH], BF16, off=sB)
        w_out_sb[call] = b.sb(f"wouts{call}", [128, HT, D], BF16, off=sA)
        for q in range(4):
            off = sB if q % 2 == 0 else sA
            w2_sb[(call, q)] = b.sb(f"w2s{call}_{q}", [128, 2, HT, D], BF16, off=off)
    wk_sb = b.sb("wk_sb", [128, DT, D], BF16, off=R2)
    wv_sb = b.sb("wv_sb", [128, DT, D], BF16, off=R2 + 16384)
    wr_sb = b.sb("wr_sb", [128, DT, D], BF16, off=R2 + 32768)
    rwb = b.sb("rwb", [128, DT, TOK], BF16, off=R1)
    wo_sb = b.sb("wo_sb", [128, DT, D], BF16, off=WO_OFF)

    def dma(chan, dst, src, r=(), w=(), slow=False):
        eng = b.chans[chan]["eng"]

        def fn(e):
            if slow:
                return e.dma_start(dst, src, allow_slow_non_contiguous=True)
            return e.dma_start(dst, src)
        return b.emit(eng, fn, r=r, w=w, chan=chan)

    def op(eng, fn, r=(), w=(), tiny=False):
        return b.emit(eng, fn, r=r, w=w, tiny=tiny or FORCE_TINY[0])

    def mm(bank, out_ap, passes, reads, start=True, stop=True):
        def fn(eng):
            n = len(passes)
            inst = None
            for i, (l, rr) in enumerate(passes):
                inst = eng.matmul(out_ap, l, rr, start=(start and i == 0),
                                  stop=(stop and i == n - 1))
            return inst
        b.emit("pe", fn, r=reads, w=(pkk(bank),))

    def emit_ln(srcT, w, xbf, xc, tmpT, dstT,
                bf_out=None, lo=None, rows2=None):
        src3 = srcT.t[:, :, 0:w]
        src_keys = srcT.keys
        tmp3 = tmpT.t[:, :, 0:w]
        tmp_keys = tmpT.keys
        dst3 = dstT.t[:, :, 0:w]
        dst_keys = dstT.keys
        src2 = srcT.t[:, :, :].rearrange("p j t -> p (j t)")
        op("act", lambda e: e.copy(xbf.ap, src2), r=(src_keys,), w=(xbf.keys,))
        op("act", lambda e: e.activation(xc.ap, src2, AF.Square),
           r=(src_keys,), w=(xc.keys,))
        xbf3 = xbf.t[:, :].rearrange("p (j t) -> p j t", j=DT)
        xc3 = xc.t[:, :].rearrange("p (j t) -> p j t", j=DT)
        xbf3 = xbf3[:, :, 0:w]
        xc3 = xc3[:, :, 0:w]
        mm(4, PS[4][0:1, 0:w], [(ones_k.ap, xbf3[:, j, :]) for j in range(DT)], (xbf.keys, ones_k.keys))
        mm(5, PS[5][0:1, 0:w], [(ones_k.ap, xc3[:, j, :]) for j in range(DT)], (xc.keys, ones_k.keys))
        op("dve", lambda e: e.tensor_scalar(rowA[0:1, 0:w], PS[4][0:1, 0:w], 1.0 / D, None, OP.mult),
           r=(pkk(4),), w=(rowA.keys,), tiny=True)
        op("dve", lambda e: e.tensor_scalar(rowB[0:1, 0:w], PS[5][0:1, 0:w], 1.0 / D, EPS, OP.mult, OP.add),
           r=(pkk(5),), w=(rowB.keys,), tiny=True)
        op("dve", lambda e: e.tensor_tensor(rowC[0:1, 0:w], rowA[0:1, 0:w], rowA[0:1, 0:w], OP.mult),
           r=(rowA.keys,), w=(rowC.keys,), tiny=True)
        op("dve", lambda e: e.tensor_tensor(rowD[0:1, 0:w], rowB[0:1, 0:w], rowC[0:1, 0:w], OP.subtract),
           r=(rowB.keys, rowC.keys), w=(rowD.keys,), tiny=True)
        op("act", lambda e: e.activation(rowC[0:1, 0:w], rowD[0:1, 0:w], AF.Sqrt),
           r=(rowD.keys,), w=(rowC.keys,), tiny=True)
        op("dve", lambda e: e.reciprocal(rowE[0:1, 0:w], rowC[0:1, 0:w]),
           r=(rowC.keys,), w=(rowE.keys,), tiny=True)
        op("dve", lambda e: e.tensor_tensor(rowD[0:1, 0:w], rowA[0:1, 0:w], rowE[0:1, 0:w], OP.mult),
           r=(rowA.keys, rowE.keys), w=(rowD.keys,), tiny=True)
        op("act", lambda e: e.copy(rhiA[0:1, 0:w], rowE[0:1, 0:w]), r=(rowE.keys,), w=(rhiA.keys,), tiny=True)
        op("dve", lambda e: e.tensor_tensor(rloA[0:1, 0:w], rowE[0:1, 0:w], rhiA[0:1, 0:w], OP.subtract),
           r=(rowE.keys, rhiA.keys), w=(rloA.keys,), tiny=True)
        op("act", lambda e: e.copy(rhiB[0:1, 0:w], rowD[0:1, 0:w]), r=(rowD.keys,), w=(rhiB.keys,), tiny=True)
        op("dve", lambda e: e.tensor_tensor(rloB[0:1, 0:w], rowD[0:1, 0:w], rhiB[0:1, 0:w], OP.subtract),
           r=(rowD.keys, rhiB.keys), w=(rloB.keys,), tiny=True)
        mm(6, PS[6][:, 0:w], [(ones_r.ap, rhiA[0:1, 0:w]), (ones_r.ap, rloA[0:1, 0:w])],
           (rhiA.keys, rloA.keys, ones_r.keys))
        mm(7, PS[7][:, 0:w], [(ones_r.ap, rhiB[0:1, 0:w]), (ones_r.ap, rloB[0:1, 0:w])],
           (rhiB.keys, rloB.keys, ones_r.keys))
        if rows2 is not None:
            rstd_t, mr_t = rows2
            op("act", lambda e: e.copy(rstd_t.t[:, 0:w], PS[6][:, 0:w]),
               r=(pkk(6),), w=(rstd_t.keys,))
            op("act", lambda e: e.copy(mr_t.t[:, 0:w], PS[7][:, 0:w]),
               r=(pkk(7),), w=(mr_t.keys,))
            for j in range(DT):
                op("pool", lambda e, j=j: e.tensor_tensor(tmp3[:, j, :], src3[:, j, :], rstd_t.t[:, 0:w], OP.mult),
                   r=(src_keys, rstd_t.keys), w=(tmp_keys,))
            for j in range(DT):
                op("dve", lambda e, j=j: e.tensor_tensor(dst3[:, j, :], tmp3[:, j, :], mr_t.t[:, 0:w], OP.subtract),
                   r=(tmp_keys, mr_t.keys), w=(dst_keys,))
        else:
            for j in range(DT):
                op("dve", lambda e, j=j: e.tensor_tensor(tmp3[:, j, :], src3[:, j, :], PS[6][:, 0:w], OP.mult),
                   r=(src_keys, pkk(6)), w=(tmp_keys,))
            for j in range(DT):
                op("dve", lambda e, j=j: e.tensor_tensor(dst3[:, j, :], tmp3[:, j, :], PS[7][:, 0:w], OP.subtract),
                   r=(tmp_keys, pkk(7)), w=(dst_keys,))
        if bf_out is not None:
            dst2 = dstT.t[:, :, :].rearrange("p j t -> p (j t)")
            op("act", lambda e: e.copy(bf_out.ap, dst2), r=(dst_keys,), w=(bf_out.keys,))
        if lo is not None:
            bf3 = bf_out.t[:, :].rearrange("p (j t) -> p j t", j=DT)
            lo3 = lo.t[:, :].rearrange("p (j t) -> p j t", j=DT)
            for j in range(DT):
                op("dve", lambda e, j=j: e.tensor_tensor(lo3[:, j, 0:w], dst3[:, j, :], bf3[:, j, 0:w], OP.subtract),
                   r=(dst_keys, bf_out.keys), w=(lo.keys,))

    def emit_gating(t0, w, xbf3, xbf_keys, xlo3, xlo_keys, gsc):
        wgehi3 = wge_hi.t[:, :, :]
        wgelo3 = wge_lo.t[:, :, :]
        passes = [(wgehi3[:, j, :], xbf3[:, j, :]) for j in range(DT)]
        passes += [(wgehi3[:, j, :], xlo3[:, j, :]) for j in range(DT)]
        passes += [(wgelo3[:, j, :], xbf3[:, j, :]) for j in range(DT)]
        mm(4, PS[4][0:10, 0:w], passes, (wge_hi.keys, wge_lo.keys, xbf_keys, xlo_keys))
        op("dve", lambda e: e.tensor_scalar(glog[0:10, 0:w], PS[4][0:10, 0:w], gbias[0:10, 0:1], None, OP.add),
           r=(pkk(4), gbias.keys), w=(glog.keys,), tiny=True)
        nsub = (w + 127) // 128
        subs = []
        for s in range(nsub):
            tw = min(128, w - 128 * s)
            subs.append((s, tw, slice(128 * s, 128 * s + tw)) + tuple(gsc[s]))

        def step(emitter):
            for args in subs:
                emitter(*args)

        for (s, tw, sl, gt, gco, gm, gex_, gpx_, gz_, wtm) in subs:
            op("pe", lambda e, sl=sl, tw=tw: e.transpose(PS[5][0:tw, 0:10], glog[0:10, sl], idn[0:10, 0:10]),
               r=(glog.keys, idn.keys), w=(pkk(5),))
            op("dve", lambda e, tw=tw, gt=gt: e.tensor_copy(gt[0:tw, 0:10], PS[5][0:tw, 0:10]),
               r=(pkk(5),), w=(gt.keys,), tiny=True)
        step(lambda s, tw, sl, gt, gco, gm, gex_, gpx_, gz_, wtm:
             op("dve", lambda e, tw=tw, gt=gt, gco=gco: e.tensor_tensor(gco[0:tw, 0:1], gt[0:tw, 0:1], gt[0:tw, 1:2], OP.subtract),
                r=(gt.keys,), w=(gco.keys,), tiny=True))
        step(lambda s, tw, sl, gt, gco, gm, gex_, gpx_, gz_, wtm:
             op("dve", lambda e, tw=tw, gco=gco: e.scalar_tensor_tensor(gco[0:tw, 1:2], gco[0:tw, 0:1], -1.0, gco[0:tw, 0:1], OP.mult, OP.max),
                r=(gco.keys,), w=(gco.keys,), tiny=True))
        step(lambda s, tw, sl, gt, gco, gm, gex_, gpx_, gz_, wtm:
             op("act", lambda e, tw=tw, gco=gco: e.activation(gco[0:tw, 2:3], gco[0:tw, 1:2], AF.Sigmoid),
                r=(gco.keys,), w=(gco.keys,), tiny=True))
        step(lambda s, tw, sl, gt, gco, gm, gex_, gpx_, gz_, wtm:
             op("dve", lambda e, tw=tw, gco=gco: e.tensor_scalar(gco[0:tw, 4:5], gco[0:tw, 0:1], 0.0, None, OP.is_lt),
                r=(gco.keys,), w=(gco.keys,), tiny=True))
        step(lambda s, tw, sl, gt, gco, gm, gex_, gpx_, gz_, wtm:
             op("dve", lambda e, tw=tw, gco=gco, gm=gm: e.tensor_scalar(gm[0:tw, 0:4], ones8[0:tw, 0:4], gco[0:tw, 4:5], -1.0, OP.mult, OP.mult),
                r=(ones8.keys, gco.keys), w=(gm.keys,), tiny=True))
        step(lambda s, tw, sl, gt, gco, gm, gex_, gpx_, gz_, wtm:
             op("dve", lambda e, tw=tw, gco=gco, gm=gm: e.tensor_scalar(gm[0:tw, 4:8], ones8[0:tw, 0:4], gco[0:tw, 4:5], -1.0, OP.subtract, OP.mult),
                r=(ones8.keys, gco.keys), w=(gm.keys,), tiny=True))
        step(lambda s, tw, sl, gt, gco, gm, gex_, gpx_, gz_, wtm:
             op("dve", lambda e, tw=tw, gt=gt, gm=gm, gz_=gz_: e.scalar_tensor_tensor(gz_[0:tw, 0:8], gm[0:tw, 0:8], BIGM, gt[0:tw, 2:10], OP.mult, OP.add),
                r=(gm.keys, gt.keys), w=(gz_.keys,), tiny=True))
        step(lambda s, tw, sl, gt, gco, gm, gex_, gpx_, gz_, wtm:
             op("dve", lambda e, tw=tw, gco=gco, gz_=gz_: e.reduce_max(gco[0:tw, 5:6], gz_[0:tw, 0:8], AX, negate=True),
                r=(gz_.keys,), w=(gco.keys,), tiny=True))
        step(lambda s, tw, sl, gt, gco, gm, gex_, gpx_, gz_, wtm:
             op("act", lambda e, tw=tw, gco=gco, gex_=gex_, gz_=gz_: e.activation(gex_[0:tw, 0:8], gz_[0:tw, 0:8], AF.Exp, bias=gco[0:tw, 5:6]),
                r=(gz_.keys, gco.keys), w=(gex_.keys,), tiny=True))
        step(lambda s, tw, sl, gt, gco, gm, gex_, gpx_, gz_, wtm:
             op("dve", lambda e, tw=tw, gco=gco, gex_=gex_: e.reduce_sum(gco[0:tw, 6:7], gex_[0:tw, 0:8], AX),
                r=(gex_.keys,), w=(gco.keys,), tiny=True))
        step(lambda s, tw, sl, gt, gco, gm, gex_, gpx_, gz_, wtm:
             op("dve", lambda e, tw=tw, gco=gco: e.reciprocal(gco[0:tw, 7:8], gco[0:tw, 6:7]),
                r=(gco.keys,), w=(gco.keys,), tiny=True))
        step(lambda s, tw, sl, gt, gco, gm, gex_, gpx_, gz_, wtm:
             op("dve", lambda e, tw=tw, gco=gco: e.tensor_tensor(gco[0:tw, 4:5], gco[0:tw, 2:3], gco[0:tw, 7:8], OP.mult),
                r=(gco.keys,), w=(gco.keys,), tiny=True))
        step(lambda s, tw, sl, gt, gco, gm, gex_, gpx_, gz_, wtm:
             op("dve", lambda e, tw=tw, gco=gco, gex_=gex_, gpx_=gpx_: e.tensor_scalar(gpx_[0:tw, 0:8], gex_[0:tw, 0:8], gco[0:tw, 4:5], None, OP.mult),
                r=(gex_.keys, gco.keys), w=(gpx_.keys,), tiny=True))
        step(lambda s, tw, sl, gt, gco, gm, gex_, gpx_, gz_, wtm:
             op("dve", lambda e, tw=tw, gco=gco, gpx_=gpx_: e.reduce_max(gco[0:tw, 0:1], gpx_[0:tw, 0:8], AX),
                r=(gpx_.keys,), w=(gco.keys,), tiny=True))
        step(lambda s, tw, sl, gt, gco, gm, gex_, gpx_, gz_, wtm:
             op("dve", lambda e, tw=tw, gco=gco, gex_=gex_, gpx_=gpx_: e.tensor_scalar(gex_[0:tw, 0:8], gpx_[0:tw, 0:8], gco[0:tw, 0:1], None, OP.is_equal),
                r=(gpx_.keys, gco.keys), w=(gex_.keys,), tiny=True))
        step(lambda s, tw, sl, gt, gco, gm, gex_, gpx_, gz_, wtm:
             op("dve", lambda e, tw=tw, gex_=gex_, gpx_=gpx_, gz_=gz_: e.scalar_tensor_tensor(gz_[0:tw, 0:8], gex_[0:tw, 0:8], -BIGM, gpx_[0:tw, 0:8], OP.mult, OP.add),
                r=(gex_.keys, gpx_.keys), w=(gz_.keys,), tiny=True))
        step(lambda s, tw, sl, gt, gco, gm, gex_, gpx_, gz_, wtm:
             op("dve", lambda e, tw=tw, gco=gco, gz_=gz_: e.reduce_max(gco[0:tw, 1:2], gz_[0:tw, 0:8], AX),
                r=(gz_.keys,), w=(gco.keys,), tiny=True))
        step(lambda s, tw, sl, gt, gco, gm, gex_, gpx_, gz_, wtm:
             op("dve", lambda e, tw=tw, gco=gco, gpx_=gpx_, wtm=wtm: e.scalar_tensor_tensor(wtm[0:tw, 0:8], gpx_[0:tw, 0:8], gco[0:tw, 1:2], gpx_[0:tw, 0:8], OP.is_ge, OP.mult),
                r=(gpx_.keys, gco.keys), w=(wtm.keys,), tiny=True))
        for (s, tw, sl, gt, gco, gm, gex_, gpx_, gz_, wtm) in subs:
            op("pe", lambda e, tw=tw, wtm=wtm: e.transpose(PS[6][0:8, 0:tw], wtm[0:tw, 0:8], idn[0:tw, 0:tw]),
               r=(wtm.keys, idn.keys), w=(pkk(6),))
            op("dve", lambda e, s=s, tw=tw: e.tensor_copy(wse8[0:8, 128 * s:128 * s + tw], PS[6][0:8, 0:tw]),
               r=(pkk(6),), w=(wse8.keys,), tiny=True)
        dma("crow", wse_d[0:NE, t0:t0 + w], wse8[0:8, 0:w], r=(wse8.keys,), w=(("d", "wse", t0),))
        op("act", lambda e, w=w: e.copy(wse8h[0:8, 0:w], wse8[0:8, 0:w]),
           r=(wse8.keys,), w=(wse8h.keys,), tiny=True)
        op("dve", lambda e, w=w: e.tensor_tensor(wse8l[0:8, 0:w], wse8[0:8, 0:w], wse8h[0:8, 0:w], OP.subtract),
           r=(wse8.keys, wse8h.keys), w=(wse8l.keys,), tiny=True)
        ci_g = t0 // TC
        for p_ in range(4):
            dma("crow", wsehl_v[p_, ci_g, 0:2, 0:w], wse8h[2 * p_:2 * p_ + 2, 0:w],
                r=(wse8h.keys,), w=(("d", "wsehl", t0, p_, 0),))
            dma("crow", wsehl_v[p_, ci_g, 2:4, 0:w], wse8l[2 * p_:2 * p_ + 2, 0:w],
                r=(wse8l.keys,), w=(("d", "wsehl", t0, p_, 1),))
    def emit_swiglu(wsb, xbf3, xbf_keys, hbuf3, h_keys, w, scr):
        for i in range(HT):
            ba = (2 * i) % 4
            bb = ba + 1
            mm(ba, PS[ba][:, 0:w],
               [(wsb.t[:, kt, 128 * i:128 * (i + 1)], xbf3[:, kt, :]) for kt in range(DT)],
               (wsb.keys, xbf_keys))
            mm(bb, PS[bb][:, 0:w],
               [(wsb.t[:, kt, 128 * (HT + i):128 * (HT + i + 1)], xbf3[:, kt, :]) for kt in range(DT)],
               (wsb.keys, xbf_keys))
            op("act", lambda e, ba=ba: e.activation(scr.t[:, 0:w], PS[ba][:, 0:w], AF.Silu),
               r=(pkk(ba),), w=(scr.keys,))
            op("dve", lambda e, bb=bb, i=i: e.tensor_tensor(hbuf3[:, i, :], scr.t[:, 0:w], PS[bb][:, 0:w], OP.mult),
               r=(scr.keys, pkk(bb)), w=(h_keys,))

    grid_h = chunk_grid(HTOK)
    grid_t = chunk_grid(TOK)
    PCW = 420
    pc_grid = [(t, min(PCW, HTOK - t)) for t in range(0, HTOK, PCW)]

    dma("crow", tpar.ap, tpar_in, w=(tpar.keys,))
    dma("crow", wge_f.t[:, :, :], wge_T.rearrange("(j p) n -> p j n", p=128), w=(wge_f.keys,))
    dma("crow", gbias.ap, gb_in, w=(gbias.keys,))
    op("pool", lambda e: e.iota(tc1.ap, [[1, 128]], channel_multiplier=0,
                                allow_small_or_imprecise_dtypes=True), w=(tc1.keys,))
    op("pool", lambda e: e.iota(tc2.ap, [[0, 128]], channel_multiplier=1,
                                allow_small_or_imprecise_dtypes=True), w=(tc2.keys,))
    op("dve", lambda e: e.tensor_tensor(idn.ap, tc1.ap, tc2.ap, OP.is_equal),
       r=(tc1.keys, tc2.keys), w=(idn.keys,))
    op("dve", lambda e: e.memset(ones_k.ap, 1.0), w=(ones_k.keys,), tiny=True)
    op("dve", lambda e: e.memset(ones_r.ap, 1.0), w=(ones_r.keys,), tiny=True)
    op("dve", lambda e: e.memset(ones8.ap, 1.0), w=(ones8.keys,), tiny=True)
    op("dve", lambda e: e.memset(pcol.ap, 0.0), w=(pcol.keys,), tiny=True)
    op("dve", lambda e: e.memset(carry_a.ap, 0.0), w=(carry_a.keys,), tiny=True)
    op("dve", lambda e: e.memset(carry_b.ap, 0.0), w=(carry_b.keys,), tiny=True)
    op("act", lambda e: e.activation(etd.ap, tdec.ap, AF.Exp), r=(tdec.keys,), w=(etd.keys,), tiny=True)
    op("dve", lambda e: e.tensor_scalar(netd.ap, etd.ap, -1.0, None, OP.mult),
       r=(etd.keys,), w=(netd.keys,), tiny=True)
    op("act", lambda e: e.activation(decC.ap, etd.ap, AF.Exp, scale=-1.0),
       r=(etd.keys,), w=(decC.keys,), tiny=True)
    op("act", lambda e: e.copy(wge_hi.t[:, :, :], wge_f.t[:, :, :]), r=(wge_f.keys,), w=(wge_hi.keys,), tiny=True)
    for j in range(DT):
        op("dve", lambda e, j=j: e.tensor_tensor(wge_lo.t[:, j, :], wge_f.t[:, j, :], wge_hi.t[:, j, :], OP.subtract),
           r=(wge_f.keys, wge_hi.keys), w=(wge_lo.keys,), tiny=True)

    def wload3(sbT, src, kt, m, tag):
        dma("cw", sbT.t[:, :, :], src.rearrange("(kt p) m -> p kt m", p=128),
            w=(sbT.keys,))

    wload3(w1_sb[0], w1_T, DT, 2 * DH, "w1_0")

    def emit_PA2(call):
        srcB = x_in if call == 0 else xr2_d
        stag = "x" if call == 0 else "xr2"
        grid = grid_h if call == 0 else grid_t
        A = RegionAlloc(b, R3, R3SZ, f"a2{call}")
        xb = A.t("xb", [128, DT, TC], F32)
        tmp = A.t("tmp", [128, DT, TC], F32)
        xbf = A.t("xbf", [128, DT * TC], BF16)
        xc = A.t("xc", [128, DT * TC], BF16)
        scr = A.t("scr", [128, TC], F32)
        hb = b.sb(f"a2{call}_hb", [128, HT, TC], BF16, off=tmp.off0)
        A.off = (A.off + 255) & ~255
        gsc = [tuple(A.t(f"g{nm}{s}", [128, 64], F32)
                     for nm in ("t", "c", "m", "e", "p", "z", "w"))
               for s in range(3)]
        lnr = (A.t("rstd", [128, TC], F32), A.t("mrow", [128, TC], F32))
        for ci, (t0, w) in enumerate(grid):
            dma("cin", xb.t[:, :, :].rearrange("p j t -> p (j t)"), blk(srcB, ci),
                r=(("d", stag, t0),), w=(xb.keys,))
            emit_ln(xb, w, xbf, xc, tmp, xb, bf_out=xbf, lo=xc)
            xbf3 = xbf.t[:, :].rearrange("p (j t) -> p j t", j=DT)[:, :, 0:w]
            xc3 = xc.t[:, :].rearrange("p (j t) -> p j t", j=DT)[:, :, 0:w]
            dma("cxl", blk(xln_d, ci), xbf.ap, r=(xbf.keys,), w=(("d", "xln", t0),))
            emit_gating(t0, w, xbf3, xbf.keys, xc3, xc.keys, gsc)
            emit_swiglu(w1_sb[call], xbf3, xbf.keys, hb.t[:, :, 0:w], hb.keys, w, scr)
            dma("cin", blk(hrt_d, ci), hb.t[:, :, :].rearrange("p j t -> p (j t)"),
                r=(hb.keys,), w=(("d", "hrt", t0),))
        wload3(w_in_sb[call], w_in_T, DT, 2 * DH, f"win{call}")

    def emit_PA1(call):
        grid = grid_h if call == 0 else grid_t
        A = RegionAlloc(b, R3, R3SZ, f"a1{call}")
        xlnb = A.t("xlnb", [128, DT, TC], BF16)
        xlnb2 = A.t("xlnb2", [128, DT, TC], BF16)
        hb = A.t("hb", [128, HT, TC], BF16)
        scr = A.t("scr", [128, TC], F32)
        pob = A.t("pob", [128, DT, TC], F32)
        wload3(w_out_sb[call], w_out_T, HT, D, f"wo{call}")
        bufs = (xlnb, xlnb2)
        prev_key = None
        for ci, (t0, w) in enumerate(grid):
            xl = bufs[ci % 2]
            rk = (("d", "xln", t0),) if (prev_key is None or not SERIAL_PA1[0]) else (("d", "xln", t0), prev_key)
            dma("cxl", xl.t[:, :, :].rearrange("p j t -> p (j t)"), blk(xln_d, ci), r=rk, w=(xl.keys,))
            prev_key = ("d", "pA", t0)
            xbf3 = xl.t[:, :, 0:w]
            emit_swiglu(w_in_sb[call], xbf3, xl.keys, hb.t[:, :, 0:w], hb.keys, w, scr)
            if call == 0 and DBG_HSH[0]:
                dma("cout", hsh_d.rearrange("(j p) t -> p j t", p=128)[:, :, t0:t0 + w],
                    hb.t[:, :, 0:w], r=(hb.keys,), w=(("d", "hsh", t0),))
            for m in range(DT):
                bank = m % 4
                mm(bank, PS[bank][:, 0:w],
                   [(w_out_sb[call].t[:, kt, 128 * m:128 * (m + 1)], hb.t[:, kt, 0:w]) for kt in range(HT)],
                   (w_out_sb[call].keys, hb.keys))
                op("dve", lambda e, bank=bank, m=m, w=w: e.tensor_copy(pob.t[:, m, 0:w], PS[bank][:, 0:w]),
                   r=(pkk(bank),), w=(pob.keys,))
            dma("cin", blk(pA_d, ci), pob.t[:, :, :].rearrange("p j t -> p (j t)"),
                r=(pob.keys,), w=(("d", "pA", t0),))

    HRB = HT * TC * 2
    assert HRB % 256 == 0
    pb_hrtb = [b.sb(f"pb_hrt{i}", [128, HT, TC], BF16, off=R3 + i * HRB) for i in range(2)]
    he0b = b.sb("pb_he0", [128, HT, TC], BF16, off=R3 + 2 * HRB)
    he1b = b.sb("pb_he1", [128, HT, TC], BF16, off=R3 + 3 * HRB)
    TCB = 1536
    pb_xq = [b.sb(f"pb_xq{i}", [128, TC], F32, off=R3 + 4 * HRB + i * TCB) for i in range(2)]
    pb_po = [b.sb(f"pb_po{i}", [128, TC], F32, off=R3 + 4 * HRB + (2 + i) * TCB) for i in range(2)]
    assert R3 + 4 * HRB + 4 * TCB <= SB_LIMIT

    def emit_PB_call(call):
        grid = grid_h if call == 0 else grid_t
        stag = "x" if call == 0 else "xr2"
        xsrcB = x_in if call == 0 else xr2_d
        n = len(grid)
        steps = [(q, ci) for q in range(4) for ci in range(n)]
        N = len(steps)

        def qinfo(q):
            pin_b = (pA_d, pB_d, pA_d, pB_d)[q]
            pint = ("pA", "pB", "pA", "pB")[q]
            if q < 3:
                pout_b = (pB_d, pA_d, pB_d)[q]
                poutt = ("pB", "pA", "pB")[q]
            else:
                pout_b = xr1vw if call == 0 else outvw
                poutt = "xr1" if call == 0 else "out"
            return pin_b, pint, pout_b, poutt

        def load_hrt(s):
            q, ci = steps[s]
            t0, w = grid[ci]
            dma("cin", pb_hrtb[s % 2].t[:, :, :].rearrange("p j t -> p (j t)"), blk(hrt_d, ci),
                r=(("d", "hrt", t0),), w=(pb_hrtb[s % 2].keys,))

        def rows_and_bcast(s):
            q, ci = steps[s]
            t0, w = grid[ci]
            p = s % 2
            row = q * NCH + ci
            dma("crow", pb_rows4.t[0:1, :, :].rearrange("p e t -> p (e t)"),
                wsehl_d[row:row + 1, :],
                r=(("d", "wsehl", t0, q, 0), ("d", "wsehl", t0, q, 1)),
                w=(pb_rows4.keys,))
            for ei in range(2):
                bank = 6 + ei
                mm(bank, PS[bank][:, 0:w],
                   [(ones_r.ap, pb_rows4.t[0:1, ei, 0:w]), (ones_r.ap, pb_rows4.t[0:1, 2 + ei, 0:w])],
                   (pb_rows4.keys, ones_r.keys))
                op("act", lambda e, bank=bank, p=p, ei=ei, w=w:
                   e.copy(pb_wb[p][ei].t[:, 0:w], PS[bank][:, 0:w]),
                   r=(pkk(bank),), w=(pb_wb[p][ei].keys,))

        def emit_he(s, ei):
            q, ci = steps[s]
            t0, w = grid[ci]
            p = s % 2
            heX = (he0b, he1b)[ei]
            hr = pb_hrtb[p]
            wbX = pb_wb[p][ei]
            eng = "dve" if ei == 0 else "pool"
            for ht in range(HT):
                op(eng, lambda e, ht=ht, hr=hr, wbX=wbX, heX=heX, w=w:
                   e.tensor_tensor(heX.t[:, ht, 0:w], hr.t[:, ht, 0:w], wbX.t[:, 0:w], OP.mult),
                   r=(hr.keys, wbX.keys), w=(heX.keys,))

        load_hrt(0)
        rows_and_bcast(0)
        emit_he(0, 0)
        emit_he(0, 1)

        for s in range(N):
            q, ci = steps[s]
            t0, w = grid[ci]
            pin_b, pint, pout_b, poutt = qinfo(q)
            wq = w2_sb[(call, q)]
            if ci == 0:
                if q < 3:
                    nq = w2_sb[(call, q + 1)]
                    dma("cw", nq.t[:, :, :, :],
                        w2_T[(2 * q + 2) * DH:(2 * q + 4) * DH, :].rearrange(
                            "(e kt p) m -> p e kt m", p=128, kt=HT),
                        w=(nq.keys,))
                elif call == 0:
                    dma("cw", wk_sb.t[:, :, :], wk_T.rearrange("(kt p) m -> p kt m", p=128), w=(wk_sb.keys,))
                    dma("cw", wv_sb.t[:, :, :], wv_T.rearrange("(kt p) m -> p kt m", p=128), w=(wv_sb.keys,))
                    dma("cw", wr_sb.t[:, :, :], wr_T.rearrange("(kt p) m -> p kt m", p=128), w=(wr_sb.keys,))
            if s + 1 < N:
                load_hrt(s + 1)
                rows_and_bcast(s + 1)
            for m in range(DT):
                if m < 4:
                    dma("cout", pb_pin[m].t[:, 0:w], blkm(pin_b, ci, m, w),
                        r=(("d", pint, t0),), w=(pb_pin[m].keys,))
                mm(m, PS[m][:, 0:w],
                   [(wq.t[:, 0, kt, 128 * m:128 * (m + 1)], he0b.t[:, kt, 0:w]) for kt in range(HT)],
                   (wq.keys, he0b.keys), start=True, stop=False)
            if s + 1 < N:
                emit_he(s + 1, 0)
            for m in range(DT):
                if q == 3:
                    xqX = pb_xq[m % 2]
                    dma("cout", xqX.t[:, 0:w], blkm(xsrcB, ci, m, w),
                        r=(("d", stag, t0),), w=(xqX.keys,))
                mm(m, PS[m][:, 0:w],
                   [(wq.t[:, 1, kt, 128 * m:128 * (m + 1)], he1b.t[:, kt, 0:w]) for kt in range(HT)],
                   (wq.keys, he1b.keys), start=False, stop=True)
                pinX = pb_pin[m % 4]
                poX = pb_po[m % 2]
                if q < 3:
                    op("dve", lambda e, m=m, pinX=pinX, poX=poX, w=w:
                       e.scalar_tensor_tensor(poX.t[:, 0:w], PS[m][:, 0:w], 1.0, pinX.t[:, 0:w], OP.mult, OP.add),
                       r=(pkk(m), pinX.keys), w=(poX.keys,))
                else:
                    xqX = pb_xq[m % 2]
                    op("dve", lambda e, m=m, xqX=xqX, poX=poX, w=w:
                       e.scalar_tensor_tensor(poX.t[:, 0:w], PS[m][:, 0:w], 0.5, xqX.t[:, 0:w], OP.mult, OP.add),
                       r=(pkk(m), xqX.keys), w=(poX.keys,))
                    op("dve", lambda e, pinX=pinX, poX=poX, xqX=xqX, w=w:
                       e.scalar_tensor_tensor(xqX.t[:, 0:w], pinX.t[:, 0:w], 0.5, poX.t[:, 0:w], OP.mult, OP.add),
                       r=(pinX.keys, poX.keys), w=(xqX.keys,))
                    poX = xqX
                if q < 3:
                    dst = blkm(pout_b, ci, m, w)
                else:
                    dst = pout_b[:, m, t0:t0 + w]
                dma("cout", dst, poX.t[:, 0:w], r=(poX.keys,), w=(("d", poutt, t0),))
                if m + 4 < DT:
                    dma("cout", pb_pin[m % 4].t[:, 0:w], blkm(pin_b, ci, m + 4, w),
                        r=(("d", pint, t0),), w=(pb_pin[m % 4].keys,))
            if s + 1 < N:
                emit_he(s + 1, 1)

    def emit_q0_load(call):
        wq = w2_sb[(call, 0)]
        dma("cw", wq.t[:, :, :, :],
            w2_T[0:2 * DH, :].rearrange("(e kt p) m -> p e kt m", p=128, kt=HT),
            w=(wq.keys,))

    def emit_PC():
        A = RegionAlloc(b, R3, R3SZ, "pc")
        xb2 = [A.t(f"xb{i}", [128, DT, PCW], F32) for i in range(2)]
        tmp = A.t("tmp", [128, DT, PCW], F32)
        xbf = A.t("xbf", [128, DT * PCW], BF16)
        xc = A.t("xc", [128, DT * PCW], BF16)
        xprev = b.sb("pc_xprev", [128, DT, PCW], F32, off=R1)
        mixA = b.sb("pc_mixA", [128, DT, PCW], BF16, off=R1 + 14336)
        mixB = b.sb("pc_mixB", [128, DT, PCW], BF16, off=R1 + 21504)
        evacs = [b.sb(f"pc_evac{i}", [128, DT, PCW], F32, off=R1 + 28672 + i * 14336)
                 for i in range(2)]
        lnr = (b.sb("pc_rstd", [128, PCW], F32, off=R1 + 57344),
               b.sb("pc_mrow", [128, PCW], F32, off=R1 + 59392))
        wsbs = (wk_sb, wv_sb, wr_sb)
        dsts = (kvw, vvw, rsvw)
        dtags = ("k", "v", "rs")
        xr1_all = tuple(("d", "xr1", t0) for (t0, _w) in grid_h)
        for ci, (t0, w) in enumerate(pc_grid):
            xb = xb2[ci % 2]
            rk = tuple(k for k, (kt0, kw) in zip(xr1_all, grid_h)
                       if kt0 < t0 + w and kt0 + kw > t0)
            dma("cin", xb[:, :, 0:w], xr1vw[:, :, t0:t0 + w], r=rk, w=(xb.keys,))
            emit_ln(xb, w, xbf, xc, tmp, xb)
            for j in range(DT):
                op("pool", lambda e, j=j, w=w, xb=xb: e.tensor_copy(xprev.t[:, j, 1:w], xb.t[:, j, 0:w - 1]),
                   r=(xb.keys,), w=(xprev.keys,))
            pcol3 = pcol.t[:, :].rearrange("p (j t) -> p j t", t=1)
            op("dve", lambda e, pcol3=pcol3: e.tensor_copy(xprev.t[:, :, 0:1], pcol3),
               r=(pcol.keys,), w=(xprev.keys,), tiny=True)
            op("dve", lambda e, w=w, pcol3=pcol3, xb=xb: e.tensor_copy(pcol3, xb.t[:, :, w - 1:w]),
               r=(xb.keys,), w=(pcol.keys,), tiny=True)
            for j in range(DT):
                op("dve", lambda e, j=j, w=w, xb=xb: e.tensor_tensor(tmp.t[:, j, 0:w], xb.t[:, j, 0:w], xprev.t[:, j, 0:w], OP.subtract),
                   r=(xb.keys, xprev.keys), w=(tmp.keys,))
            if t0 == 0:
                s0, c0, cw = 1, 0, w - 1
            else:
                s0, c0, cw = 0, t0 - 1, w
            for i in range(3):
                mx = (mixA, mixB)[i % 2]
                evac = evacs[i % 2]
                for j in range(DT):
                    op("dve", lambda e, i=i, j=j, mx=mx, w=w: e.scalar_tensor_tensor(
                        mx.t[:, j, 0:w], tmp.t[:, j, 0:w], tmix[:, 8 * i + j:8 * i + j + 1],
                        xprev.t[:, j, 0:w], OP.mult, OP.add),
                       r=(tmp.keys, xprev.keys, tmix.keys), w=(mx.keys,))
                for m in range(DT):
                    bank = m % 4
                    mm(bank, PS[bank][:, 0:w],
                       [(wsbs[i].t[:, kt, 128 * m:128 * (m + 1)], mx.t[:, kt, 0:w]) for kt in range(DT)],
                       (wsbs[i].keys, mx.keys))
                    if i == 0:
                        op("act", lambda e, bank=bank, m=m, w=w, evac=evac: e.copy(evac.t[:, m, 0:w], PS[bank][:, 0:w]),
                           r=(pkk(bank),), w=(evac.keys,))
                    elif i == 1:
                        op("dve", lambda e, bank=bank, m=m, w=w, evac=evac: e.tensor_copy(evac.t[:, m, 0:w], PS[bank][:, 0:w]),
                           r=(pkk(bank),), w=(evac.keys,))
                    else:
                        op("act", lambda e, bank=bank, m=m, w=w, evac=evac: e.activation(evac.t[:, m, 0:w], PS[bank][:, 0:w], AF.Sigmoid),
                           r=(pkk(bank),), w=(evac.keys,))
                if i < 2:
                    dma(("cpk", "cpv")[i], dsts[i][:, :, c0:c0 + cw], evac.t[:, :, s0:s0 + cw],
                        r=(evac.keys,), w=(("d", dtags[i], t0),))
                else:
                    dma("cpr", dsts[i][:, 0:4, c0:c0 + cw], evac.t[:, 0:4, s0:s0 + cw],
                        r=(evac.keys,), w=(("d", "rs", t0, 0),))
                    dma("cpr", dsts[i][:, 4:8, c0:c0 + cw], evac.t[:, 4:8, s0:s0 + cw],
                        r=(evac.keys,), w=(("d", "rs", t0, 1),))
        return

    def pc_cov(q0, tag, rs=False):
        out = []
        for (t0, wq_) in pc_grid:
            c0 = 0 if t0 == 0 else t0 - 1
            cw_ = wq_ - 1 if t0 == 0 else wq_
            if c0 < q0 + QW and c0 + cw_ > q0:
                out.append(t0)
        if rs:
            return tuple(("d", tag, t0, h) for t0 in out for h in (0, 1))
        return tuple(("d", tag, t0) for t0 in out)

    def emit_PD():
        A = RegionAlloc(b, R3, WO_OFF - R3, "pd")
        kt_ = A.t("kt", [128, QW], F32)
        vt_ = A.t("vt", [128, QW], F32)
        ekt = A.t("ekt", [128, QW], F32)
        pt_ = A.t("pt", [128, QW], F32)
        ones1k = A.t("ones1k", [128, QW], F32)
        decb = A.t("decb", [128, QW], F32)
        abuf = A.t("abuf", [128, QW], F32)
        bbuf = A.t("bbuf", [128, QW], F32)
        dma("cw", wo_sb.t[:, :, :], wo_T.rearrange("(kt p) m -> p kt m", p=128), w=(wo_sb.keys,))
        op("dve", lambda e: e.memset(ones1k.ap, 1.0), w=(ones1k.keys,), tiny=True)

        for qi in range(TOK // QW):
            q0 = qi * QW
            kkeys = pc_cov(q0, "k")
            vkeys = pc_cov(q0, "v")
            for j in range(DT):
                op("dve", lambda e, j=j: e.tensor_scalar(decb.ap, ones1k.ap, decC[:, j:j + 1], None, OP.mult),
                   r=(ones1k.keys, decC.keys), w=(decb.keys,))
                if DBG_PD[0] and qi == 0:
                    dma("cout", dcb_d[128 * j:128 * (j + 1), :], decb.ap, r=(decb.keys,), w=(("d", "dcb", j),))
                dma("csc", kt_.ap, k_d[128 * j:128 * (j + 1), q0:q0 + QW], r=kkeys, w=(kt_.keys,))
                dma("csc", vt_.ap, v_d[128 * j:128 * (j + 1), q0:q0 + QW], r=vkeys, w=(vt_.keys,))
                op("act", lambda e: e.activation(ekt.ap, kt_.ap, AF.Exp), r=(kt_.keys,), w=(ekt.keys,))
                op("pool", lambda e: e.tensor_tensor(pt_.ap, ekt.ap, vt_.ap, OP.mult),
                   r=(ekt.keys, vt_.keys), w=(pt_.keys,))
                if DBG_PD[0] and qi == 0:
                    dma("cout", pt_d[128 * j:128 * (j + 1), :], pt_.ap, r=(pt_.keys,), w=(("d", "ptd", j),))
                op("dve", lambda e, j=j: e.tensor_tensor_scan(abuf.ap, decb.ap, pt_.ap, carry_a[:, j:j + 1], OP.mult, OP.add),
                   r=(decb.keys, pt_.keys, carry_a.keys), w=(abuf.keys,))
                op("dve", lambda e, j=j: e.tensor_tensor_scan(bbuf.ap, decb.ap, ekt.ap, carry_b[:, j:j + 1], OP.mult, OP.add),
                   r=(decb.keys, ekt.keys, carry_b.keys), w=(bbuf.keys,))
                op("dve", lambda e, j=j: e.tensor_copy(carry_a[:, j:j + 1], abuf[:, QW - 1:QW]),
                   r=(abuf.keys,), w=(carry_a.keys,), tiny=True)
                op("dve", lambda e, j=j: e.tensor_copy(carry_b[:, j:j + 1], bbuf[:, QW - 1:QW]),
                   r=(bbuf.keys,), w=(carry_b.keys,), tiny=True)
                dma("cscw", as_d[128 * j:128 * (j + 1), q0:q0 + QW], abuf.ap,
                    r=(abuf.keys,), w=(("d", "as", j, qi),))
                dma("cscw", bs_d[128 * j:128 * (j + 1), q0:q0 + QW], bbuf.ap,
                    r=(bbuf.keys,), w=(("d", "bs", j, qi),))
        op("dve", lambda e: e.tensor_copy(stage[:, 0:8], carry_a.ap), r=(carry_a.keys,), w=(stage.keys,), tiny=True)
        op("dve", lambda e: e.tensor_copy(stage[:, 8:16], carry_b.ap), r=(carry_b.keys,), w=(stage.keys,), tiny=True)
        dma("crow", ag_in_d, stage.ap, r=(stage.keys,), w=(("d", "agin"),))
        b.emit("pool", lambda e: e.collective_compute(
            "AllGather", OP.bypass, replica_groups=[list(range(8))],
            ins=[ag_in_d.opt()], outs=[ag_out_d.opt()]),
            r=(("d", "agin"),), w=(("d", "agout"),), coll=True)
        dma("crow", gsb.t[:, :, :], ag_out_d.rearrange("(r p) c -> p r c", p=128),
            r=(("d", "agout"),), w=(gsb.keys,))
        op("dve", lambda e: e.tensor_scalar(SAB2.ap, gsb.t[:, 0, :], selv[:, 0:1], None, OP.mult),
           r=(gsb.keys, selv.keys), w=(SAB2.keys,), tiny=True)
        for rr in range(1, 8):
            src_, dst_ = (SAB2, SAB1) if rr % 2 == 1 else (SAB1, SAB2)
            op("dve", lambda e, rr=rr, src_=src_, dst_=dst_: e.scalar_tensor_tensor(
                dst_.ap, gsb.t[:, rr, :], selv[:, rr:rr + 1], src_.ap, OP.mult, OP.add),
               r=(gsb.keys, selv.keys, src_.keys), w=(dst_.keys,), tiny=True)

    def emit_PE():
        names = ("kt", "vt", "rt", "ap", "bp", "dpw", "eku", "tq")
        PEB = [{nm: b.sb(f"pe_{nm}{p}", [128, QW], F32, off=R2 + i * 4096 + p * 32768)
                for i, nm in enumerate(names)} for p in (0, 1)]
        iqs = [b.sb(f"pe_iq{p}", [128, QW], F32, off=R3 + p * 4096) for p in (0, 1)]
        for qi in range(TOK // QW):
            q0 = qi * QW
            kkey = pc_cov(q0, "k")
            vkey = pc_cov(q0, "v")
            rkey = pc_cov(q0, "rs", rs=True)
            iq = iqs[qi % 2]
            op("pool", lambda e, q0=q0, iq=iq: e.iota(iq.ap, [[1, QW]], base=q0, channel_multiplier=0,
                                                      allow_small_or_imprecise_dtypes=True), w=(iq.keys,))
            for j in range(DT):
                P = PEB[(qi * DT + j) % 2]
                kt_, vt_, rt_ = P["kt"], P["vt"], P["rt"]
                ap_, bp_, dpw, eku, tq = P["ap"], P["bp"], P["dpw"], P["eku"], P["tq"]
                akey = (("d", "as", j, qi), ("d", "as", j, max(0, qi - 1)))
                bkey = (("d", "bs", j, qi), ("d", "bs", j, max(0, qi - 1)))
                dma("csc", kt_.ap, k_d[128 * j:128 * (j + 1), q0:q0 + QW], r=kkey, w=(kt_.keys,))
                dma("csc", vt_.ap, v_d[128 * j:128 * (j + 1), q0:q0 + QW], r=vkey, w=(vt_.keys,))
                dma("csc", rt_.ap, rs_d[128 * j:128 * (j + 1), q0:q0 + QW], r=rkey, w=(rt_.keys,))
                if q0 == 0:
                    op("dve", lambda e, ap_=ap_: e.memset(ap_[:, 0:1], 0.0), w=(ap_.keys,), tiny=True)
                    op("dve", lambda e, bp_=bp_: e.memset(bp_[:, 0:1], 0.0), w=(bp_.keys,), tiny=True)
                    dma("csc", ap_[:, 1:QW], as_d[128 * j:128 * (j + 1), 0:QW - 1], r=akey, w=(ap_.keys,))
                    dma("csc", bp_[:, 1:QW], bs_d[128 * j:128 * (j + 1), 0:QW - 1], r=bkey, w=(bp_.keys,))
                else:
                    dma("csc", ap_.ap, as_d[128 * j:128 * (j + 1), q0 - 1:q0 + QW - 1], r=akey, w=(ap_.keys,))
                    dma("csc", bp_.ap, bs_d[128 * j:128 * (j + 1), q0 - 1:q0 + QW - 1], r=bkey, w=(bp_.keys,))
                op("dve", lambda e, j=j, tq=tq, iq=iq: e.tensor_scalar(tq.ap, iq.ap, netd[:, j:j + 1], None, OP.mult),
                   r=(iq.keys, netd.keys), w=(tq.keys,))
                op("act", lambda e, dpw=dpw, tq=tq: e.activation(dpw.ap, tq.ap, AF.Exp), r=(tq.keys,), w=(dpw.keys,))
                op("act", lambda e, j=j, eku=eku, kt_=kt_: e.activation(eku.ap, kt_.ap, AF.Exp, bias=tfir[:, j:j + 1]),
                   r=(kt_.keys, tfir.keys), w=(eku.keys,))
                op("pool", lambda e, tq=tq, eku=eku, vt_=vt_: e.tensor_tensor(tq.ap, eku.ap, vt_.ap, OP.mult),
                   r=(eku.keys, vt_.keys), w=(tq.keys,))
                op("dve", lambda e, j=j, kt_=kt_, dpw=dpw, ap_=ap_: e.scalar_tensor_tensor(kt_.ap, dpw.ap, SAB1[:, j:j + 1], ap_.ap, OP.mult, OP.add),
                   r=(dpw.keys, SAB1.keys, ap_.keys), w=(kt_.keys,))
                op("dve", lambda e, j=j, vt_=vt_, dpw=dpw, bp_=bp_: e.scalar_tensor_tensor(vt_.ap, dpw.ap, SAB1[:, 8 + j:8 + j + 1], bp_.ap, OP.mult, OP.add),
                   r=(dpw.keys, SAB1.keys, bp_.keys), w=(vt_.keys,))
                op("pool", lambda e, ap_=ap_, kt_=kt_, tq=tq: e.tensor_tensor(ap_.ap, kt_.ap, tq.ap, OP.add),
                   r=(kt_.keys, tq.keys), w=(ap_.keys,))
                op("pool", lambda e, bp_=bp_, vt_=vt_, eku=eku: e.tensor_tensor(bp_.ap, vt_.ap, eku.ap, OP.add),
                   r=(vt_.keys, eku.keys), w=(bp_.keys,))
                op("dve", lambda e, dpw=dpw, bp_=bp_: e.tensor_scalar(dpw.ap, bp_.ap, 1e-6, None, OP.max),
                   r=(bp_.keys,), w=(dpw.keys,))
                op("dve", lambda e, tq=tq, dpw=dpw: e.reciprocal(tq.ap, dpw.ap), r=(dpw.keys,), w=(tq.keys,))
                op("pool", lambda e, kt_=kt_, ap_=ap_, tq=tq: e.tensor_tensor(kt_.ap, ap_.ap, tq.ap, OP.mult),
                   r=(ap_.keys, tq.keys), w=(kt_.keys,))
                op("dve", lambda e, j=j, q0=q0, kt_=kt_, rt_=rt_: e.tensor_tensor(rwb.t[:, j, q0:q0 + QW], kt_.ap, rt_.ap, OP.mult),
                   r=(kt_.keys, rt_.keys), w=(rwb.keys,))


    def emit_PF():
        A = RegionAlloc(b, R3, WO_OFF - R3, "pf")
        xbs = [A.t(f"xb{i}", [128, DT, TC], F32) for i in range(2)]
        ob = A.t("ob", [128, DT, TC], F32)
        xr1keys = tuple(("d", "xr1", t0) for (t0, _w) in grid_h)
        for ci, (t0, w) in enumerate(grid_t):
            xb = xbs[ci % 2]
            dma("cxb", xb[:, 0:4, 0:w], xr1vw[:, 0:4, t0 + 1:t0 + 1 + w],
                r=xr1keys, w=(xb.keys,))
            dma("cxb", xb[:, 4:8, 0:w], xr1vw[:, 4:8, t0 + 1:t0 + 1 + w],
                r=xr1keys, w=(xb.keys,))
            for m in range(DT):
                bank = m % 4
                mm(bank, PS[bank][:, 0:w],
                   [(wo_sb.t[:, kt, 128 * m:128 * (m + 1)], rwb.t[:, kt, t0:t0 + w]) for kt in range(DT)],
                   (wo_sb.keys, rwb.keys))
                op("dve", lambda e, bank=bank, m=m, w=w, xb=xb: e.tensor_tensor(ob.t[:, m, 0:w], PS[bank][:, 0:w], xb.t[:, m, 0:w], OP.add),
                   r=(pkk(bank), xb.keys), w=(ob.keys,))
            dma("cin", blk(xr2_d, ci), ob.t[:, :, :].rearrange("p j t -> p (j t)"),
                r=(ob.keys,), w=(("d", "xr2", t0),))
        wload3(w1_sb[1], w1_T, DT, 2 * DH, "w1_1")

    allph = ["A2", "A1", "B", "C", "D", "E", "F", "A2b", "A1b", "Bb"]
    ph = allph if phases is None else phases
    if "A2" in ph:
        emit_PA2(0)
    if "A1" in ph:
        emit_PA1(0)
    if "B" in ph:
        emit_q0_load(0)
        emit_PB_call(0)
    if "C" in ph:
        emit_PC()
    if "D" in ph:
        emit_PD()
    if "E" in ph:
        emit_PE()
    if "F" in ph:
        emit_PF()
    if "A2b" in ph:
        emit_PA2(1)
    if "A1b" in ph:
        emit_PA1(1)
    if "Bb" in ph:
        emit_q0_load(1)
        emit_PB_call(1)

    dbg_src = {"xln1": (xln_d, "xln", grid_h), "wse": (wse_d, "wse", grid_h),
               "hsh": (hsh_d, "hsh", grid_h), "pB": (pB_d, "pB", grid_h),
               "as": (as_d, "as", None), "bs": (bs_d, "bs", None),
               "dcb": (dcb_d, "dcb", None), "ptd": (pt_d, "ptd", None),
               "pA": (pA_d, "pA", grid_h), "xr1": (xr1_d, "xr1", grid_h),
               "hrt": (hrt_d, "hrt", grid_h),
               "k": (k_d, "k", pc_grid), "v": (v_d, "v", pc_grid), "rs": (rs_d, "rs", pc_grid),
               "xr2": (xr2_d, "xr2", grid_t)}
    if "rwb" in debug:
        dd = nc.dram_tensor("dbg_rwb", [128, DT * TOK], BF16, kind="ExternalOutput").ap()
        dma("cout", dd, rwb.t[:, :, :].rearrange("p j t -> p (j t)"), r=(rwb.keys,), w=())
    for nm in debug:
        if nm not in dbg_src:
            continue
        src, tag, grid = dbg_src[nm]
        shp = list(src.tensor.shape) if hasattr(src, "tensor") else None
        dt = BF16 if nm in ("xln1", "hrt", "hsh") else F32
        dd = nc.dram_tensor("dbg_" + nm, [src.shape[0], src.shape[1]], dt, kind="ExternalOutput").ap()
        if nm in ("dcb", "ptd"):
            keys = tuple(("d", tag, j) for j in range(DT))
        elif nm in ("as", "bs"):
            keys = tuple(("d", tag, j, qi) for j in range(DT) for qi in range(TOK // QW))
        elif nm == "rs":
            keys = tuple(("d", "rs", t0, h) for (t0, _w) in grid for h in (0, 1))
        else:
            keys = tuple(("d", tag, t0) for (t0, _w) in grid)
        dma("cout", dd, src, r=keys, w=())

    b.build_blocks(serial=serial)
    nc._b = b
    return nc



def _prep(inputs):
    f = np.float32
    xs = np.asarray(inputs["L0_attn_x"], f)
    w_in = np.asarray(inputs["L0_ffn_w_in"], f)
    w_out = np.asarray(inputs["L0_ffn_w_out"], f)
    w1 = np.asarray(inputs["L0_ffn_w1"], f)
    w2 = np.asarray(inputs["L0_ffn_w2"], f)
    wg = np.asarray(inputs["L0_ffn_wg"], f)
    we = np.asarray(inputs["L0_ffn_we"], f)
    gb = np.asarray(inputs["L0_ffn_gb"], f)
    eb = np.asarray(inputs["L0_ffn_eb"], f)
    wk = np.asarray(inputs["L0_attn_Wk"], f)
    wv = np.asarray(inputs["L0_attn_Wv"], f)
    wr = np.asarray(inputs["L0_attn_Wr"], f)
    wo = np.asarray(inputs["L0_attn_Wo"], f)
    td = np.asarray(inputs["L0_attn_time_decay"], f)
    tf = np.asarray(inputs["L0_attn_time_first"], f)
    tmk = np.asarray(inputs["L0_attn_time_mix_k"], f)
    tmv = np.asarray(inputs["L0_attn_time_mix_v"], f)
    tmr = np.asarray(inputs["L0_attn_time_mix_r"], f)

    def colmaj(vec):
        return np.ascontiguousarray(vec.reshape(DT, 128).T)

    tmix_a = np.concatenate([colmaj(tmk), colmaj(tmv), colmaj(tmr)], axis=1)
    tpar_base = np.concatenate([tmix_a, colmaj(td), colmaj(tf)], axis=1)
    shared = {
        "w_in_T": np.ascontiguousarray(w_in.T),
        "w1_T": np.ascontiguousarray(w1.T),
        "w_out_T": np.ascontiguousarray(w_out.T),
        "w2_T": np.ascontiguousarray(w2.transpose(0, 2, 1).reshape(NE * DH, D)),
        "wk_T": np.ascontiguousarray(wk.T),
        "wv_T": np.ascontiguousarray(wv.T),
        "wr_T": np.ascontiguousarray(wr.T),
        "wo_T": np.ascontiguousarray(wo.T),
        "wge_T": np.ascontiguousarray(np.concatenate([wg, we], 0).T),
        "gbias": np.ascontiguousarray(np.concatenate([gb, eb])[:, None]),
    }
    grid = []
    t = 0
    while t < HTOK:
        grid.append((t, min(TC, HTOK - t)))
        t += TC
    nch = len(grid)
    in_maps = []
    for c in range(8):
        bb, h = divmod(c, 2)
        seg = xs[bb, h * TOK:(h + 1) * TOK]
        halo = xs[bb, h * TOK - 1][None] if h == 1 else np.zeros((1, D), f)
        shard = np.concatenate([halo, seg], 0).T
        sh3 = shard.reshape(DT, 128, HTOK).transpose(1, 0, 2)
        xblk = np.zeros((nch, 128, DT, TC), f)
        for ci, (t0, w) in enumerate(grid):
            xblk[ci, :, :, 0:w] = sh3[:, :, t0:t0 + w]
        sel = np.zeros((128, DT), f)
        if h == 1:
            sel[:, c - 1] = 1.0
        m = dict(shared)
        m["x"] = np.ascontiguousarray(xblk.reshape(nch * 128, DT * TC))
        m["tparams"] = np.ascontiguousarray(np.concatenate([tpar_base, sel], axis=1))
        in_maps.append(m)
    return in_maps


_NC_CACHE = {}


def kernel(**inputs):
    in_maps = _prep(inputs)
    key = "main"
    if key not in _NC_CACHE:
        _NC_CACHE[key] = build_program()
    nc = _NC_CACHE[key]
    res = run_bass_kernel_spmd(nc, in_maps, core_ids=list(range(8)))
    out = np.empty((4, 8192, D), np.float32)
    for c in range(8):
        bb, h = divmod(c, 2)
        out[bb, h * TOK:(h + 1) * TOK] = np.asarray(res.results[c]["out"]).T
    return out
